# version 10
# baseline (speedup 1.0000x reference)
import contextlib
import numpy as np
import concourse.bass as bass
import concourse.mybir as mybir
from concourse.bass_utils import run_bass_kernel_spmd

F32 = mybir.dt.float32
BF16 = mybir.dt.bfloat16
AF = mybir.ActivationFunctionType
ALU = mybir.AluOpType
AX = mybir.AxisListType

ENGINES = ("tensor", "vector", "scalar", "gpsimd", "sync")


def apname(a):
    if isinstance(a, str):
        return a
    t = getattr(a, "tensor", None)
    if t is not None:
        return t.name
    return a.name


class Prog:

    SEM_CAP = 24000

    def __init__(self, nc, n_dma_sems=24):
        self.nc = nc
        self.eng = {e: getattr(nc, e) for e in ENGINES}
        self.stack = contextlib.ExitStack()
        self.psem = {e: [self.stack.enter_context(nc.semaphore(f"prog_{e}0"))]
                     for e in ENGINES}
        self.gen = {e: 0 for e in ENGINES}
        self.base = {e: 0 for e in ENGINES}
        self.pcnt = {e: 0 for e in ENGINES}
        self.dsems = [self.stack.enter_context(nc.semaphore(f"dma{i}"))
                      for i in range(n_dma_sems)]
        self.dcnt = [0] * n_dma_sems
        self.drr = 0
        self.res = {}
        self.waited = {e: {} for e in ENGINES}

    def _sem(self, kind, key):
        if kind == "eng":
            e, gen = key
            return self.psem[e][gen]
        return self.dsems[key]

    def _etok(self, e):
        if self.pcnt[e] - self.base[e] > self.SEM_CAP:
            self.gen[e] += 1
            self.base[e] = self.pcnt[e] - 1
            self.psem[e].append(self.stack.enter_context(
                self.nc.semaphore(f"prog_{e}{self.gen[e]}")))
        return ("eng", (e, self.gen[e]), self.pcnt[e] - self.base[e])

    def _edges(self, e, reads, writes, extra=()):
        need = {}

        def add(tok):
            if tok is None:
                return
            kind, key, val = tok
            if kind == "eng" and key[0] == e == "tensor":
                return
            k = (kind, key)
            if val > need.get(k, 0):
                need[k] = val

        for r in reads:
            rec = self.res.get(apname(r))
            if rec:
                add(rec["w"])
        for w in writes:
            rec = self.res.get(apname(w))
            if rec:
                add(rec["w"])
                for t in rec["r"]:
                    add(t)
        for t in extra:
            add(t)
        out = []
        for (kind, key), val in need.items():
            if val > self.waited[e].get((kind, key), 0):
                out.append((kind, key, val))
        return out

    def _post(self, e, tok, reads, writes, edges):
        w = self.waited[e]
        for (kind, key, val) in edges:
            if val > w.get((kind, key), 0):
                w[(kind, key)] = val
        for x in writes:
            self.res[apname(x)] = {"w": tok, "r": []}
        for r in reads:
            rec = self.res.setdefault(apname(r), {"w": None, "r": []})
            rec["r"].append(tok)

    def op(self, e, method, *args, reads=(), writes=(), **kw):
        edges = self._edges(e, reads, writes)
        for (kind, key, val) in edges[:-1]:
            self.eng[e].wait_ge(self._sem(kind, key), val)
        inst = getattr(self.eng[e], method)(*args, **kw)
        if edges:
            kind, key, val = edges[-1]
            inst._wait_ge(self._sem(kind, key), val)
        self.pcnt[e] += 1
        tok = self._etok(e)
        inst.then_inc(self._sem(tok[0], tok[1]), 1)
        self._post(e, tok, reads, writes, edges)
        return inst

    def dma(self, e, out, in_, reads=(), writes=(), **kw):
        i = self.drr
        self.drr = (self.drr + 1) % len(self.dsems)
        extra = []
        if self.dcnt[i] > 0:
            extra.append(("dma", i, self.dcnt[i]))
        reads = list(reads)
        writes = list(writes)
        if not isinstance(in_, str) and getattr(in_, "tensor", None) is not None:
            reads.append(in_)
        if not isinstance(out, str) and getattr(out, "tensor", None) is not None:
            writes.append(out)
        edges = self._edges(e, reads, writes, extra)
        for (kind, key, val) in edges[:-1]:
            self.eng[e].wait_ge(self._sem(kind, key), val)
        inst = self.eng[e].dma_start(out=out, in_=in_, **kw)
        if edges:
            kind, key, val = edges[-1]
            inst._wait_ge(self._sem(kind, key), val)
        self.dcnt[i] += 16
        inst.then_inc(self.dsems[i], 16)
        w = self.waited[e]
        for (kind, key, val) in edges:
            if val > w.get((kind, key), 0):
                w[(kind, key)] = val
        tok = ("dma", i, self.dcnt[i])
        for x in writes:
            self.res[apname(x)] = {"w": tok, "r": []}
        for r in reads:
            rec = self.res.setdefault(apname(r), {"w": None, "r": []})
            rec["r"].append(tok)
        return inst

    def barrier(self):
        for e in ENGINES:
            w = self.waited[e]
            for f in ENGINES:
                if f == e:
                    continue
                key = (f, self.gen[f])
                val = self.pcnt[f] - self.base[f]
                if val > w.get(("eng", key), 0):
                    self.eng[e].wait_ge(self.psem[f][self.gen[f]], val)
                    w[("eng", key)] = val
            for i, c in enumerate(self.dcnt):
                if c > w.get(("dma", i), 0):
                    self.eng[e].wait_ge(self.dsems[i], c)
                    w[("dma", i)] = c

    def finish(self):
        self.barrier()
        self.stack.close()


def ap3(t, off, dims, nparts=128, part0=0):
    fs = int(t.shape[-1])
    return bass.AP(t, part0 * fs + off, [[fs, nparts]] + dims)


class Cfg:
    def __init__(self, T=2048, NSEQ=2, HID=4096, E=8):
        self.D = 1024
        self.NH = 16
        self.DH = 64
        self.F = 64
        self.T = T
        self.NSEQ = NSEQ
        self.HID = HID
        self.E = E
        self.KC = self.D // 128
        self.NT = T // 128
        self.NQC = T // 512
        self.NHC = HID // 128
        self.EPS = 1e-5
        assert T % 512 == 0


def make_ln_tmp(nc, st, name):
    t = {}
    for n, sh in (("st6", 12), ("mv", 2), ("lnv", 1), ("rstd", 1),
                  ("eps", 1), ("sc1", 1), ("sc2", 1), ("sc3", 1), ("sc4", 1)):
        t[n] = st.enter_context(nc.sbuf_tensor(f"{name}_{n}", [128, sh], F32))
    t["init"] = False
    return t


def ln_plain(p, cfg, x_ap, out_ap, t, x_res=None, out_res=None):
    v = "vector"
    D = cfg.D
    if x_res is None:
        xr = [x_ap]
    elif isinstance(x_res, (list, tuple)):
        xr = list(x_res)
    else:
        xr = [x_res]
    if not t["init"]:
        p.op(v, "memset", t["eps"][:], cfg.EPS, writes=[t["eps"]])
        t["init"] = True
    half = 512
    xa = x_ap
    p.op(v, "bn_stats", t["st6"][:, 0:6],
         bass.AP(xa.tensor, xa.offset, [xa.ap[0], [1, half]]),
         reads=xr, writes=[t["st6"]])
    p.op(v, "bn_stats", t["st6"][:, 6:12],
         bass.AP(xa.tensor, xa.offset + half, [xa.ap[0], [1, half]]),
         reads=xr, writes=[t["st6"]])
    p.op(v, "bn_aggr", t["mv"][:], t["st6"][:],
         reads=[t["st6"]], writes=[t["mv"]])
    p.op("scalar", "activation", t["lnv"][:], t["mv"][:, 1:2], AF.Ln,
         bias=t["eps"][:, 0:1], scale=1.0,
         reads=[t["mv"], t["eps"]], writes=[t["lnv"]])
    p.op("scalar", "activation", t["rstd"][:], t["lnv"][:], AF.Exp,
         bias=0.0, scale=-0.5, reads=[t["lnv"]], writes=[t["rstd"]])
    p.op(v, "tensor_scalar", out_ap, x_ap, t["mv"][:, 0:1], t["rstd"][:, 0:1],
         ALU.subtract, ALU.mult,
         reads=xr + [t["mv"], t["rstd"]],
         writes=[out_res if out_res else out_ap])


def load_w_dd(p, cfg, wbuf, stag, hbm_ap, cols):
    src = hbm_ap.rearrange("(k p) c -> k p c", p=128)
    wn = wbuf.name
    for kc in range(cfg.KC):
        s = stag[kc % 2]
        p.dma("sync", s[:, 0:cols], src[kc], writes=[s])
        p.op("vector", "tensor_copy", wbuf[:, kc * cols:(kc + 1) * cols],
             s[:, 0:cols], reads=[s], writes=[f"{wn}:{kc}"])
    return [f"{wn}:{kc}" for kc in range(cfg.KC)]


def build_program(cfg: Cfg):
    nc = bass.Bass("TRN2", target_bir_lowering=False)
    D, NH, DH, Fd = cfg.D, cfg.NH, cfg.DH, cfg.F
    T, NSEQ, KC, NT, NQC = cfg.T, cfg.NSEQ, cfg.KC, cfg.NT, cfg.NQC
    NHC, E, HID = cfg.NHC, cfg.E, cfg.HID

    def dt(n, s, d=F32, k="ExternalInput"):
        return nc.dram_tensor(n, s, d, kind=k).ap()

    x0 = dt("x0", [NSEQ, T, D])
    Wq = dt("Wq", [D, D]); Wk = dt("Wk", [D, D]); Wv = dt("Wv", [D, D])
    Wo = dt("Wo", [D, D])
    Wf = dt("Wf", [DH, Fd])
    bf_col = dt("bf_col", [Fd, 1])
    bf_bc = dt("bf_bc", [128, Fd])
    bo_bc = dt("bo_bc", [128, D])
    w_qkv = dt("w_qkv", [D, 3 * D])
    bqkv_cols = dt("bqkv_cols", [128, 16])
    bqkv_v_bc = dt("bqkv_v_bc", [128, D])
    w_o2 = dt("w_o2", [D, D])
    bo2_bc = dt("bo2_bc", [128, D])
    ln1w_bc = dt("ln1w_bc", [128, D]); ln1b_bc = dt("ln1b_bc", [128, D])
    ln2w_bc = dt("ln2w_bc", [128, D]); ln2b_bc = dt("ln2b_bc", [128, D])
    w_gate = dt("w_gate", [D, E])
    w1 = dt("w1", [E, NHC, D, 128])
    b1_cols = dt("b1_cols", [128, E * NHC])
    w2 = dt("w2", [E, HID, D])
    b2pad = dt("b2pad", [E, 128, D])
    row0 = dt("row0", [128, 128], BF16)
    eye = dt("eye", [128, 128], BF16)
    y = dt("y", [NSEQ, T, D], F32, "ExternalOutput")

    def sc(n, s, d):
        return nc.dram_tensor(n, s, d, kind="Internal").ap()

    x1_s = sc("x1_s", [NSEQ, NT, 128, D], F32)
    fin_s = sc("fin_s", [NSEQ, NT, 128, D], F32)
    h_s = sc("h_s", [NSEQ, NT, 128, D], F32)
    t1f_s = sc("t1f_s", [NSEQ, KC, 128, T], BF16)
    t2f_s = sc("t2f_s", [NSEQ, KC, 128, T], BF16)

    p = Prog(nc)
    phase1(p, cfg, nc, locals())
    p.barrier()
    phase2(p, cfg, nc, locals())
    p.barrier()
    phase3(p, cfg, nc, locals())
    p.finish()
    return nc


def phase1(p, cfg, nc, g):
    D, NH, DH, Fd = cfg.D, cfg.NH, cfg.DH, cfg.F
    T, NSEQ, KC, NT, NQC = cfg.T, cfg.NSEQ, cfg.KC, cfg.NT, cfg.NQC
    x0, Wq, Wk, Wv, Wo = g["x0"], g["Wq"], g["Wk"], g["Wv"], g["Wo"]
    Wf, bf_col, bf_bc, bo_bc = g["Wf"], g["bf_col"], g["bf_bc"], g["bo_bc"]
    ln1w_bc, ln1b_bc = g["ln1w_bc"], g["ln1b_bc"]
    eye, x1_s, fin_s, t1f_s = g["eye"], g["x1_s"], g["fin_s"], g["t1f_s"]

    with contextlib.ExitStack() as S:
        def sb(n, s, d=F32):
            return S.enter_context(nc.sbuf_tensor(n, s, d))

        eye_sb = sb("eye_sb", [128, 128], BF16)
        p.dma("sync", eye_sb[:], eye)
        bfb = sb("bfb", [128, Fd]); p.dma("sync", bfb[:], bf_bc)
        bfc = sb("bfc", [Fd, 1]); p.dma("sync", bfc[:], bf_col)
        bob = sb("bob", [128, D]); p.dma("sync", bob[:], bo_bc)
        l1w = sb("l1w", [128, D]); p.dma("sync", l1w[:], ln1w_bc)
        l1b = sb("l1b", [128, D]); p.dma("sync", l1b[:], ln1b_bc)
        wf_st = sb("wf_st", [DH, Fd]); p.dma("sync", wf_st[:], Wf)
        wf_sb = sb("wf_sb", [128, Fd], BF16)
        p.op("vector", "tensor_copy", wf_sb[0:64, :], wf_st[:],
             reads=[wf_st], writes=["wf_sb:0"])
        p.op("vector", "tensor_copy", wf_sb[64:128, :], wf_st[:],
             reads=[wf_st], writes=["wf_sb:1"])
        ones64 = sb("ones64", [1, Fd], F32)
        p.op("vector", "memset", ones64[:], 1.0, writes=[ones64])

        a_fm = sb("a_fm", [128, KC * T], BF16)
        qf_fm = sb("qf_fm", [128, (NH // 2) * 512], BF16)
        v_tm = sb("v_tm", [128, NT * NH * 65], BF16)
        kv_sb = sb("kv_sb", [128, NH * 65], BF16)
        outn = sb("outn", [128, (NH // 2) * 512], BF16)
        wbuf = sb("wbufA", [128, KC * D], BF16)
        wbufB = sb("wbufB", [128, KC * D], BF16)
        wstag = [sb("wstag0", [128, D])]
        wstag.append(wstag[0])
        x0t = [sb(f"x0t{i}", [128, D]) for i in range(2)]
        a_tm = [sb(f"a_tm{i}", [128, D], BF16) for i in range(2)]
        lnt = make_ln_tmp(nc, S, "p1ln")
        lnt2 = make_ln_tmp(nc, S, "p1lnB")
        p1sc = sb("p1sc", [128, 1024])
        nr_row = p1sc[0:1, 0:512]
        nr_rec = p1sc[0:1, 512:1024]
        rb_sb = p1sc[64:128, 0:512]
        SQ = "p1sc"
        ksb = [sb(f"ksb{i}", [128, T], BF16) for i in range(2)]
        kf_f = [sb(f"kf_f{i}", [128, Fd]) for i in range(2)]
        kf_t = [sb(f"kf_t{i}", [128, Fd], BF16) for i in range(2)]

        ao_t = [sb(f"ao_t{i}", [128, D]) for i in range(2)]
        lnao = sb("lnao", [128, D])
        aotmp = sb("aotmp", [128, 512])
        x1_t = [sb(f"x1_t{i}", [128, D]) for i in range(2)]
        fin_t = [sb(f"fin_t{i}", [128, D]) for i in range(2)]
        xh_t = sb("xh_t", [128, D])
        t1_tm = [sb(f"t1_tm{i}", [128, D], BF16) for i in range(2)]
        t1f_buf = sb("t1f_buf", [128, KC * 512], BF16)

        prj = [S.enter_context(nc.psum_tensor(f"prj{i}", [128, 512], F32))
               for i in range(4)]
        kv_ps = [S.enter_context(nc.psum_tensor(f"kv_ps{i}", [Fd, 65], F32))
                 for i in range(2)]
        kv_pn = ["kv_ps0", "kv_ps1"]
        rbp = S.enter_context(nc.psum_tensor("rbp", [Fd, 512], F32))
        pTT = S.enter_context(nc.psum_tensor("pTT", [128, 256], BF16))
        pT = [pTT[:, 0:128], pTT[:, 128:256]]
        pTn = ["pTT", "pTT"]
        qf_ps = [prj[2][0:64, :], prj[3][0:64, :]]
        qf_pn = ["prj2", "prj3"]
        kf_ps = [prj[2][:, 0:64], prj[3][:, 0:64]]
        kf_pn = ["prj2", "prj3"]
        out_ps = [prj[2][0:65, :], prj[3][0:65, :]]
        out_pn = ["prj2", "prj3"]
        prjn = ["prj0", "prj1", "prj2", "prj3"]

        p.op("vector", "memset",
             ap3(v_tm, 64, [[NH * 65, NT], [65, NH], [1, 1]]),
             1.0, writes=["v_tm:ones"])

        afm_r = [f"a_fm:{kc}_{tt}" for kc in range(KC) for tt in range(NT)]
        wbuf_r = [f"wbufA:{kc}" for kc in range(KC)]
        wbufB_r = [f"wbufB:{kc}" for kc in range(KC)]

        for s in range(NSEQ):
            for tt in range(NT):
                xt = x0t[tt % 2]
                p.dma("sync", xt[:], x0[s, tt * 128:(tt + 1) * 128, :])
                at = a_tm[tt % 2]
                ln_plain(p, cfg, xt[:], at[:], lnt)
                for kc in range(KC):
                    i = kc % 2
                    p.op("tensor", "transpose", pT[i],
                         at[:, kc * 128:(kc + 1) * 128], eye_sb[:],
                         reads=[at, eye_sb], writes=[pTn[i]])
                    p.op("vector", "tensor_copy",
                         a_fm[:, kc * T + tt * 128: kc * T + tt * 128 + 128],
                         pT[i], reads=[pTn[i]], writes=[f"a_fm:{kc}_{tt}"])

            load_w_dd(p, cfg, wbuf, wstag, Wv, D)
            load_w_dd(p, cfg, wbufB, wstag, Wk, D)
            for lt in range(NT):
                alt_r = [f"a_fm:{kc}_{lt}" for kc in range(KC)]
                for half in range(2):
                    ps = prj[half]
                    for kc in range(KC):
                        p.op("tensor", "matmul", ps[:],
                             a_fm[:, kc * T + lt * 128: kc * T + lt * 128 + 128],
                             wbuf[:, kc * D + half * 512: kc * D + half * 512 + 512],
                             start=(kc == 0), stop=(kc == KC - 1),
                             reads=alt_r + wbuf_r, writes=[prjn[half]])
                    p.op("vector", "tensor_copy",
                         ap3(v_tm, lt * NH * 65 + half * 8 * 65,
                             [[65, 8], [1, 64]]),
                         ap3(ps, 0, [[64, 8], [1, 64]]),
                         reads=[prjn[half]], writes=[f"v_tm:{lt}_{half}"])
            v_r = [f"v_tm:{lt}_{h}" for lt in range(NT) for h in range(2)]

            load_w_dd(p, cfg, wbuf, wstag, Wq, D)

            def c_proj_units(dc):
                us = []
                ks = ksb[dc % 2]
                for qc in range(NQC):
                    aqc_r = [f"a_fm:{kc}_{qc * 4 + j}" for kc in range(KC)
                             for j in range(4)]
                    for kc in range(KC):
                        def um(dc=dc, qc=qc, kc=kc, aqc_r=aqc_r):
                            p.op("tensor", "matmul", prj[qc % 2][:],
                                 wbufB[:, kc * D + dc * 128:
                                       kc * D + dc * 128 + 128],
                                 a_fm[:, kc * T + qc * 512:
                                      kc * T + qc * 512 + 512],
                                 start=(kc == 0), stop=(kc == KC - 1),
                                 reads=aqc_r + wbufB_r, writes=[prjn[qc % 2]])
                        us.append(um)
                    def ue(dc=dc, qc=qc, ks=ks):
                        p.op("scalar", "copy",
                             ks[:, qc * 512:(qc + 1) * 512], prj[qc % 2][:],
                             reads=[prjn[qc % 2]],
                             writes=[f"{ks.name}:{qc}"])
                    us.append(ue)
                return us

            def c_kfkv_units(dc):
                us = []
                ks = ksb[dc % 2]
                ks_r = [f"{ks.name}:{qc}" for qc in range(NQC)]
                for hh in range(2):
                    h16 = dc * 2 + hh
                    for lt in range(NT):
                        def uk(hh=hh, h16=h16, lt=lt, ks=ks, ks_r=ks_r):
                            i = lt % 2
                            p.op("tensor", "matmul", kf_ps[i],
                                 ks[hh * 64: hh * 64 + 64,
                                    lt * 128: lt * 128 + 128],
                                 wf_sb[hh * 64: hh * 64 + 64, :],
                                 start=True, stop=True,
                                 reads=ks_r + ["wf_sb:0", "wf_sb:1"],
                                 writes=[kf_pn[i]])
                            p.op("vector", "tensor_add", kf_f[i][:], kf_ps[i],
                                 bfb[:], reads=[kf_pn[i], bfb],
                                 writes=[kf_f[i]])
                            p.op("vector", "tensor_relu", kf_t[i][:],
                                 kf_f[i][:], reads=[kf_f[i]],
                                 writes=[kf_t[i]])
                            p.op("tensor", "matmul",
                                 kv_ps[h16 % 2][:], kf_t[i][:],
                                 ap3(v_tm, lt * NH * 65 + h16 * 65, [[1, 65]]),
                                 start=(lt == 0), stop=(lt == NT - 1),
                                 reads=[kf_t[i], "v_tm:ones"] + v_r,
                                 writes=[kv_pn[h16 % 2]])
                        us.append(uk)
                    def uev(h16=h16):
                        for hlf, wres in ((0, "kv_sb:0"), (64, "kv_sb:1")):
                            p.op("scalar", "copy",
                                 kv_sb[hlf:hlf + 64, h16 * 65: h16 * 65 + 65],
                                 kv_ps[h16 % 2][:],
                                 reads=[kv_pn[h16 % 2]],
                                 writes=[f"{wres}_{h16}"])
                    us.append(uev)
                return us

            for dc in range(KC):
                for u in c_proj_units(dc):
                    u()
                for u in c_kfkv_units(dc):
                    u()

            load_w_dd(p, cfg, wbufB, wstag, Wo, D)
            qf_r = [f"qf_fm:{h}" for h in range(NH)]
            for qc in range(NQC):
                for dc in range(KC):
                    ks = ksb[dc % 2]
                    ps = prj[dc % 2]
                    for kc in range(KC):
                        p.op("tensor", "matmul", ps[:],
                             wbuf[:, kc * D + dc * 128: kc * D + dc * 128 + 128],
                             a_fm[:, kc * T + qc * 512: kc * T + qc * 512 + 512],
                             start=(kc == 0), stop=(kc == KC - 1),
                             reads=afm_r + wbuf_r, writes=[prjn[dc % 2]])
                    p.op("scalar", "copy", ks[:, 0:512], ps[:],
                         reads=[prjn[dc % 2]], writes=[f"{ks.name}:0"])
                    for hh in range(2):
                        h16 = dc * 2 + hh
                        i = dc % 2
                        p.op("tensor", "matmul", qf_ps[i],
                             wf_sb[hh * 64: hh * 64 + 64, :],
                             ks[hh * 64: hh * 64 + 64, 0:512],
                             start=True, stop=True,
                             reads=[f"{ks.name}:0", "wf_sb:0", "wf_sb:1"],
                             writes=[qf_pn[i]])
                        p.op("scalar", "activation",
                             qf_fm[(h16 % 2) * 64:(h16 % 2) * 64 + 64,
                                   (h16 // 2) * 512:(h16 // 2) * 512 + 512],
                             qf_ps[i], AF.Relu, bias=bfc[:, 0:1], scale=1.0,
                             reads=[qf_pn[i], bfc], writes=[f"qf_fm:{h16}"])
                on_r = [f"outn:{h}" for h in range(NH)]
                for h16 in range(NH):
                    i = h16 % 2
                    p.op("tensor", "matmul", out_ps[i],
                         kv_sb[(h16 % 2) * 64:(h16 % 2) * 64 + 64,
                               h16 * 65: h16 * 65 + 65],
                         qf_fm[(h16 % 2) * 64:(h16 % 2) * 64 + 64,
                               (h16 // 2) * 512:(h16 // 2) * 512 + 512],
                         start=True, stop=True,
                         reads=[f"kv_sb:0_{h16}", f"kv_sb:1_{h16}"] + qf_r,
                         writes=[out_pn[i]])
                    p.op("vector", "tensor_scalar_max", nr_row,
                         out_ps[i][64:65, :], 1e-6,
                         reads=[out_pn[i], SQ], writes=["nr_row", SQ])
                    p.op("vector", "reciprocal", nr_rec, nr_row,
                         reads=["nr_row"], writes=["nr_rec", SQ])
                    p.op("tensor", "matmul", rbp[:], ones64[:], nr_rec,
                         start=True, stop=True, reads=[ones64, "nr_rec", SQ],
                         writes=["rbp"])
                    p.op("scalar", "copy", rb_sb, rbp[:],
                         reads=["rbp", SQ], writes=["rb_sb", SQ])
                    p.op("vector", "tensor_mul",
                         outn[(h16 % 2) * 64:(h16 % 2) * 64 + 64,
                              (h16 // 2) * 512:(h16 // 2) * 512 + 512],
                         out_ps[i][0:64, :], rb_sb,
                         reads=[out_pn[i], "rb_sb", SQ], writes=[f"outn:{h16}"])
                wbuf_r2 = wbufB_r

                def chain(t4):
                    tt = qc * 4 + t4
                    ao = ao_t[t4 % 2]
                    t = [lnt, lnt2][t4 % 2]
                    lna = [lnao, xh_t][t4 % 2]
                    t1t = t1_tm[t4 % 2]
                    ln_plain(p, cfg, ao[:], lna[:], t,
                             x_res=[f"{ao.name}:0", f"{ao.name}:1"])
                    xt = x0t[tt % 2]
                    p.dma("sync", xt[:], x0[s, tt * 128:(tt + 1) * 128, :])
                    x1t = x1_t[tt % 2]; fint = fin_t[tt % 2]
                    p.op("vector", "tensor_add", x1t[:], lna[:], xt[:],
                         reads=[lna, xt], writes=[x1t])
                    p.dma("scalar", x1_s[s, tt], x1t[:])
                    v = "vector"
                    p.op(v, "bn_stats", t["st6"][:, 0:6], x1t[:, 0:512],
                         reads=[x1t], writes=[t["st6"]])
                    p.op(v, "bn_stats", t["st6"][:, 6:12], x1t[:, 512:1024],
                         reads=[x1t], writes=[t["st6"]])
                    p.op(v, "bn_aggr", t["mv"][:], t["st6"][:],
                         reads=[t["st6"]], writes=[t["mv"]])
                    p.op("scalar", "activation", t["lnv"][:], t["mv"][:, 1:2],
                         AF.Ln, bias=t["eps"][:, 0:1], scale=1.0,
                         reads=[t["mv"], t["eps"]], writes=[t["lnv"]])
                    p.op("scalar", "activation", t["rstd"][:], t["lnv"][:],
                         AF.Exp, bias=0.0, scale=-0.5,
                         reads=[t["lnv"]], writes=[t["rstd"]])
                    p.op(v, "tensor_scalar", fint[:], x1t[:], t["mv"][:, 0:1],
                         t["rstd"][:, 0:1], ALU.subtract, ALU.mult,
                         reads=[x1t, t["mv"], t["rstd"]], writes=[fint])
                    p.dma("scalar", fin_s[s, tt], fint[:])
                    p.op("scalar", "activation", t["sc1"][:], t["lnv"][:],
                         AF.Exp, bias=0.0, scale=-1.0,
                         reads=[t["lnv"]], writes=[t["sc1"]])
                    p.op(v, "tensor_mul", t["sc2"][:], t["mv"][:, 1:2],
                         t["sc1"][:], reads=[t["mv"], t["sc1"]],
                         writes=[t["sc2"]])
                    p.op("scalar", "activation", t["sc3"][:], t["sc2"][:],
                         AF.Ln, bias=t["eps"][:, 0:1], scale=1.0,
                         reads=[t["sc2"], t["eps"]], writes=[t["sc3"]])
                    p.op("scalar", "activation", t["sc4"][:], t["sc3"][:],
                         AF.Exp, bias=0.0, scale=-0.5,
                         reads=[t["sc3"]], writes=[t["sc4"]])
                    p.op(v, "tensor_mul", t["sc1"][:], t["rstd"][:],
                         t["sc4"][:], reads=[t["rstd"], t["sc4"]],
                         writes=[t["sc1"]])
                    p.op(v, "tensor_scalar", lna[:], x1t[:], t["mv"][:, 0:1],
                         t["sc1"][:, 0:1], ALU.subtract, ALU.mult,
                         reads=[x1t, t["mv"], t["sc1"]], writes=[lna])
                    p.op(v, "tensor_mul", lna[:], lna[:], l1w[:],
                         reads=[lna, l1w], writes=[lna])
                    p.op(v, "tensor_add", t1t[:], lna[:], l1b[:],
                         reads=[lna, l1b], writes=[t1t])
                    for kc in range(KC):
                        i = kc % 2
                        p.op("tensor", "transpose", pT[i],
                             t1t[:, kc * 128:(kc + 1) * 128], eye_sb[:],
                             reads=[t1t, eye_sb], writes=[pTn[i]])
                        p.op("vector", "tensor_copy",
                             t1f_buf[:, kc * 512 + t4 * 128:
                                     kc * 512 + t4 * 128 + 128],
                             pT[i], reads=[pTn[i]],
                             writes=[f"t1f_buf:{kc}_{t4}"])
                def attn_out(t4):
                    ao = ao_t[t4 % 2]
                    for half in range(2):
                        for h16 in range(NH):
                            pi = half + 2 * (h16 % 2)
                            p.op("tensor", "matmul", prj[pi][:],
                                 outn[(h16 % 2) * 64:(h16 % 2) * 64 + 64,
                                      (h16 // 2) * 512 + t4 * 128:
                                      (h16 // 2) * 512 + t4 * 128 + 128],
                                 wbufB[(h16 % 2) * 64:(h16 % 2) * 64 + 64,
                                       (h16 // 2) * D + half * 512:
                                       (h16 // 2) * D + half * 512 + 512],
                                 start=(h16 <= 1), stop=(h16 >= NH - 2),
                                 reads=on_r + wbuf_r2, writes=[prjn[pi]])
                        p.op("scalar", "copy", aotmp[:], prj[2 + half][:],
                             reads=[prjn[2 + half]], writes=[aotmp])
                        p.op("vector", "tensor_add", aotmp[:], aotmp[:],
                             bob[:, half * 512: half * 512 + 512],
                             reads=[aotmp, bob], writes=[aotmp])
                        p.op("vector", "tensor_add",
                             ao[:, half * 512: half * 512 + 512],
                             prj[half][:], aotmp[:],
                             reads=[prjn[half], aotmp],
                             writes=[f"{ao.name}:{half}"])

                attn_out(0)
                for t4 in range(4):
                    if t4 + 1 < 4:
                        attn_out(t4 + 1)
                    chain(t4)
                p.dma("scalar",
                      t1f_s[s, :, :, qc * 512:(qc + 1) * 512]
                      .rearrange("k p t -> p k t"),
                      ap3(t1f_buf, 0, [[512, KC], [1, 512]]),
                      reads=[f"t1f_buf:{kc}_{t}" for kc in range(KC)
                             for t in range(4)],
                      writes=[f"t1f_s:{s}_{qc}"])


def phase2(p, cfg, nc, g):
    D, NH, DH = cfg.D, cfg.NH, cfg.DH
    T, NSEQ, KC, NT, NQC = cfg.T, cfg.NSEQ, cfg.KC, cfg.NT, cfg.NQC
    w_qkv, bqkv_cols, bqkv_v_bc = g["w_qkv"], g["bqkv_cols"], g["bqkv_v_bc"]
    w_o2, bo2_bc = g["w_o2"], g["bo2_bc"]
    ln2w_bc, ln2b_bc = g["ln2w_bc"], g["ln2b_bc"]
    eye, fin_s, h_s, t1f_s, t2f_s = (g["eye"], g["fin_s"], g["h_s"],
                                     g["t1f_s"], g["t2f_s"])

    with contextlib.ExitStack() as S:
        def sb(n, s, d=F32):
            return S.enter_context(nc.sbuf_tensor(n, s, d))

        eye_sb = sb("eye_sb2", [128, 128], BF16)
        p.dma("sync", eye_sb[:], eye)
        bqc = sb("bqc", [128, 16]); p.dma("sync", bqc[:], bqkv_cols)
        bvb = sb("bvb", [128, D]); p.dma("sync", bvb[:], bqkv_v_bc)
        bo2b = sb("bo2b", [128, D]); p.dma("sync", bo2b[:], bo2_bc)
        l2w = sb("l2w", [128, D]); p.dma("sync", l2w[:], ln2w_bc)
        l2b = sb("l2b", [128, D]); p.dma("sync", l2b[:], ln2b_bc)
        ones1 = sb("ones1", [1, 128], F32)
        p.op("vector", "memset", ones1[:], 1.0, writes=[ones1])

        t1_fm = sb("t1_fm", [128, KC * T], BF16)
        q_fm = sb("q_fm", [128, KC * T], BF16)
        k_fm = sb("k_fm", [128, KC * T], BF16)
        v_tm = sb("v_tm2", [128, NT * NH * 65], BF16)
        ovt = t1_fm
        wv2_bf = sb("wv2_bf", [128, KC * D], BF16)
        wqk = [sb(f"wqk{i}", [128, KC * 128], BF16) for i in range(2)]
        wqst = [sb("wqst0", [128, KC * 128])]
        wqst.append(wqst[0])
        t2f_tt = sb("t2f_tt", [128, KC * 128], BF16)
        wstag = [sb("w2stag0", [128, D])]
        wstag.append(wstag[0])
        p_bf = [wqk[0][:, 0:512], wqk[1][:, 0:512]]
        p_bn = ["wqk0", "wqk1"]
        lnt = make_ln_tmp(nc, S, "p2ln")
        lnt2x = make_ln_tmp(nc, S, "p2lnB")
        p2sc = sb("p2sc", [128, 512])
        vtmp = p2sc[:, 0:512]
        VT = "p2sc"
        mkc = sb("mkc", [128, 1]); macc = sb("macc", [128, 1])
        macc_b = sb("macc_b", [128, 1], BF16)
        mrow = sb("mrow", [1, 128]); mg = sb("mg", [1, 1])
        bias_all = sb("bias_all", [128, NH])
        rr = sb("rr", [1, 512]); rb2 = sb("rb2", [DH, 512])
        ao2 = sb("ao2", [128, D]); h_t = [sb(f"h_t{i}", [128, D]) for i in range(2)]
        fin_tl = sb("fin_tl", [128, D])
        xh2 = sb("xh2", [128, D])
        t2_tm = sb("t2_tm", [128, D], BF16)

        sc_ps = [S.enter_context(nc.psum_tensor(f"sc_ps{i}", [128, 512], F32))
                 for i in range(3)]
        scn = [f"sc_ps{i}" for i in range(3)]
        ctx_ps = S.enter_context(nc.psum_tensor("ctx_ps", [65, 512], F32))
        prj2 = [S.enter_context(nc.psum_tensor(f"prj2_{i}", [128, 512], F32))
                for i in range(2)]
        prj2n = ["prj2_0", "prj2_1"]
        smallT = S.enter_context(nc.psum_tensor("smallT", [128, 256], BF16))
        pT = [smallT[:, 0:128], smallT[:, 128:256]]
        pTn = ["smallT", "smallT"]
        mr_ps = smallT[0:1, 0:128]
        rb2_ps = prj2[0][0:64, :]
        bc_ps = prj2[1][:, 0:1]

        p.op("vector", "memset",
             ap3(v_tm, 64, [[NH * 65, NT], [65, NH], [1, 1]]),
             1.0, writes=["v_tm2:ones"])

        t1_r = [f"t1_fm:{kc}" for kc in range(KC)]
        wv2_r = [f"wv2_bf:{kc}" for kc in range(KC)]
        q_r = [f"q_fm:{dc}_{qc}" for dc in range(KC) for qc in range(NQC)]
        k_r = [f"k_fm:{dc}_{qc}" for dc in range(KC) for qc in range(NQC)]
        v_r = [f"v_tm2:{lt}_{h}" for lt in range(NT) for h in range(2)]

        for s in range(NSEQ):
            for kc in range(KC):
                p.dma("sync", t1_fm[:, kc * T:(kc + 1) * T], t1f_s[s, kc],
                      writes=[f"t1_fm:{kc}"])
            for kc in range(KC):
                st_ = wstag[kc % 2]
                p.dma("sync", st_[:],
                      w_qkv.rearrange("(k p) c -> k p c", p=128)[kc, :,
                                                                2 * D:3 * D])
                p.op("vector", "tensor_copy",
                     wv2_bf[:, kc * D: kc * D + D],
                     st_[:], reads=[st_], writes=[f"wv2_bf:{kc}"])

            for qk in range(2):
                dst, dname = (q_fm, "q_fm") if qk == 0 else (k_fm, "k_fm")
                for dc in range(KC):
                    wq = wqk[dc % 2]; wqs = wqst[dc % 2]
                    p.dma("sync",
                          ap3(wqs, 0, [[128, KC], [1, 128]]),
                          w_qkv[:, qk * D + dc * 128: qk * D + dc * 128 + 128]
                          .rearrange("(k p) c -> p k c", p=128),
                          writes=[wqs])
                    p.op("vector", "tensor_copy", wq[:], wqs[:],
                         reads=[wqs], writes=[wq])
                    for qc in range(NQC):
                        ps = prj2[qc % 2]
                        for kc in range(KC):
                            p.op("tensor", "matmul", ps[:],
                                 wq[:, kc * 128: kc * 128 + 128],
                                 t1_fm[:, kc * T + qc * 512:
                                       kc * T + qc * 512 + 512],
                                 start=(kc == 0), stop=(kc == KC - 1),
                                 reads=t1_r + [wq], writes=[prj2n[qc % 2]])
                        p.op("scalar", "activation",
                             dst[:, dc * T + qc * 512: dc * T + qc * 512 + 512],
                             ps[:], AF.Identity,
                             bias=bqc[:, qk * 8 + dc: qk * 8 + dc + 1],
                             scale=1.0,
                             reads=[prj2n[qc % 2], bqc],
                             writes=[f"{dname}:{dc}_{qc}"])

            for lt in range(NT):
                for half in range(2):
                    ps = prj2[half]
                    for kc in range(KC):
                        p.op("tensor", "matmul", ps[:],
                             t1_fm[:, kc * T + lt * 128: kc * T + lt * 128 + 128],
                             wv2_bf[:, kc * D + half * 512:
                                    kc * D + half * 512 + 512],
                             start=(kc == 0), stop=(kc == KC - 1),
                             reads=t1_r + wv2_r, writes=[prj2n[half]])
                    p.op("vector", "tensor_add", vtmp, ps[:],
                         bvb[:, half * 512: half * 512 + 512],
                         reads=[prj2n[half], bvb, VT], writes=[VT])
                    p.op("vector", "tensor_copy",
                         ap3(v_tm, lt * NH * 65 + half * 8 * 65,
                             [[65, 8], [1, 64]]),
                         ap3(p2sc, 0, [[64, 8], [1, 64]]),
                         reads=[VT], writes=[f"v_tm2:{lt}_{half}"])

            load_w_dd(p, cfg, wv2_bf, wstag, w_o2, D)

            sct = [0]

            def A_unit(h16, kc, qc, first):
                hp = (h16 % 2) * 64
                hb = (h16 // 2) * T
                i = sct[0] % 3
                sct[0] += 1
                p.op("tensor", "matmul", sc_ps[i][:],
                     k_fm[hp:hp + 64, hb + kc * 128: hb + kc * 128 + 128],
                     q_fm[hp:hp + 64, hb + qc * 512: hb + qc * 512 + 512],
                     start=True, stop=True,
                     reads=q_r + k_r, writes=[scn[i]])
                p.op("vector", "reduce_max", mkc[:], sc_ps[i][:],
                     axis=AX.X, reads=[scn[i]], writes=[mkc])
                if first:
                    p.op("vector", "tensor_copy", macc[:], mkc[:],
                         reads=[mkc], writes=[macc])
                else:
                    p.op("vector", "tensor_max", macc[:], macc[:],
                         mkc[:], reads=[macc, mkc], writes=[macc])

            def A_final(h16):
                p.op("vector", "tensor_copy", macc_b[:], macc[:],
                     reads=[macc], writes=[macc_b])
                p.op("tensor", "transpose", mr_ps, macc_b[:], eye_sb[:],
                     reads=[macc_b, eye_sb], writes=["smallT"])
                p.op("vector", "tensor_copy", mrow[:], mr_ps,
                     reads=["smallT"], writes=[mrow])
                p.op("vector", "reduce_max", mg[:], mrow[:], axis=AX.X,
                     reads=[mrow], writes=[mg])
                p.op("tensor", "matmul", bc_ps, ones1[:], mg[:],
                     start=True, stop=True, reads=[mg, ones1],
                     writes=[prj2n[1]])
                p.op("scalar", "mul",
                     bias_all[:, h16: h16 + 1], bc_ps, -0.125,
                     reads=[prj2n[1]], writes=[f"bias_all:{h16}"])

            def B_score_exp(h16, kc, qc, pb):
                hp = (h16 % 2) * 64
                hb = (h16 // 2) * T
                i = sct[0] % 3
                sct[0] += 1
                p.op("tensor", "matmul", sc_ps[i][:],
                     k_fm[hp:hp + 64, hb + kc * 128: hb + kc * 128 + 128],
                     q_fm[hp:hp + 64, hb + qc * 512: hb + qc * 512 + 512],
                     start=True, stop=True,
                     reads=q_r + k_r, writes=[scn[i]])
                p.op("scalar", "activation", p_bf[pb], sc_ps[i][:],
                     AF.Exp, bias=bias_all[:, h16: h16 + 1], scale=0.125,
                     reads=[scn[i], f"bias_all:{h16}"],
                     writes=[p_bn[pb]])

            def B_ctx(h16, kc, qc, pb):
                p.op("tensor", "matmul", ctx_ps[:],
                     ap3(v_tm, kc * NH * 65 + h16 * 65, [[1, 65]]),
                     p_bf[pb],
                     start=(kc == 0), stop=(kc == NT - 1),
                     reads=[p_bn[pb], "v_tm2:ones"] + v_r,
                     writes=["ctx_ps"])

            def B_final(h16, qc):
                hp = (h16 % 2) * 64
                p.op("vector", "reciprocal", rr[:], ctx_ps[64:65, :],
                     reads=["ctx_ps"], writes=[rr])
                p.op("tensor", "matmul", rb2_ps, ones1[:, 0:64], rr[:],
                     start=True, stop=True, reads=[rr, ones1],
                     writes=[prj2n[0]])
                p.op("scalar", "copy", rb2[:], rb2_ps,
                     reads=[prj2n[0]], writes=[rb2])
                p.op("vector", "tensor_mul",
                     ovt[hp:hp + 64,
                         (h16 // 2) * T + qc * 512:
                         (h16 // 2) * T + qc * 512 + 512],
                     ctx_ps[0:64, :], rb2[:],
                     reads=["ctx_ps", rb2],
                     writes=[f"ctxq:{h16}_{qc}", f"t1_fm:{h16 // 2}"])

            NU = NT * NQC
            for u in range(NU):
                A_unit(0, u // NQC, u % NQC, u == 0)
            A_final(0)
            for h16 in range(NH):
                pend = None
                for u in range(NU):
                    if h16 + 1 < NH:
                        A_unit(h16 + 1, u // NQC, u % NQC, u == 0)
                    qc, kc = u // NT, u % NT
                    B_score_exp(h16, kc, qc, u % 2)
                    if pend is not None:
                        B_ctx(*pend)
                    pend = (h16, kc, qc, u % 2)
                    if kc == NT - 1:
                        B_ctx(*pend)
                        pend = None
                        B_final(h16, qc)
                if h16 + 1 < NH:
                    A_final(h16 + 1)

            for qc in range(NQC):
                ctx_r = [f"ctxq:{h}_{qc}" for h in range(NH)]
                for t4 in range(4):
                    tt = qc * 4 + t4
                    for half in range(2):
                        for h16 in range(NH):
                            pj = sc_ps[half] if h16 % 2 == 0 else prj2[half]
                            pjn = scn[half] if h16 % 2 == 0 else prj2n[half]
                            p.op("tensor", "matmul", pj[:],
                                 ovt[(h16 % 2) * 64:(h16 % 2) * 64 + 64,
                                     (h16 // 2) * T + qc * 512 + t4 * 128:
                                     (h16 // 2) * T + qc * 512 + t4 * 128
                                     + 128],
                                 wv2_bf[(h16 % 2) * 64:(h16 % 2) * 64 + 64,
                                        (h16 // 2) * D + half * 512:
                                        (h16 // 2) * D + half * 512 + 512],
                                 start=(h16 <= 1), stop=(h16 >= NH - 2),
                                 reads=ctx_r + wv2_r +
                                 [f"t1_fm:{h16 // 2}"],
                                 writes=[pjn])
                    p.dma("sync", fin_tl[:], fin_s[s, tt])
                    for half in range(2):
                        p.op("scalar", "copy", vtmp, prj2[half][:],
                             reads=[prj2n[half], VT], writes=[VT])
                        p.op("vector", "tensor_add", vtmp, vtmp,
                             bo2b[:, half * 512: half * 512 + 512],
                             reads=[VT, bo2b], writes=[VT])
                        p.op("vector", "tensor_add",
                             ao2[:, half * 512: half * 512 + 512],
                             sc_ps[half][:], vtmp,
                             reads=[scn[half], VT], writes=[f"ao2:{half}"])
                    ht = h_t[tt % 2]
                    p.op("vector", "tensor_add", ht[:], ao2[:], fin_tl[:],
                         reads=["ao2:0", "ao2:1", fin_tl], writes=[ht])
                    p.dma("scalar", h_s[s, tt], ht[:])
                    ln_plain(p, cfg, ht[:], xh2[:], [lnt, lnt2x][tt % 2])
                    p.op("vector", "tensor_mul", ao2[:], xh2[:], l2w[:],
                         reads=[xh2, l2w], writes=["ao2:0", "ao2:1"])
                    p.op("vector", "tensor_add", t2_tm[:], ao2[:], l2b[:],
                         reads=["ao2:0", "ao2:1", l2b], writes=[t2_tm])
                    for kc in range(KC):
                        i = kc % 2
                        p.op("tensor", "transpose", pT[i],
                             t2_tm[:, kc * 128:(kc + 1) * 128], eye_sb[:],
                             reads=[t2_tm, eye_sb], writes=[pTn[i]])
                        p.op("vector", "tensor_copy",
                             t2f_tt[:, kc * 128:(kc + 1) * 128],
                             pT[i], reads=[pTn[i]],
                             writes=[f"t2f_tt:{kc}"])
                    p.dma("scalar",
                          t2f_s[s, :, :, tt * 128:(tt + 1) * 128]
                          .rearrange("k p t -> p k t"),
                          ap3(t2f_tt, 0, [[128, KC], [1, 128]]),
                          reads=[f"t2f_tt:{kc}" for kc in range(KC)],
                          writes=[f"t2f_s:{s}_{tt}"])


def phase3(p, cfg, nc, g):
    D, E, HID, NHC = cfg.D, cfg.E, cfg.HID, cfg.NHC
    T, NSEQ, KC, NT, NQC = cfg.T, cfg.NSEQ, cfg.KC, cfg.NT, cfg.NQC
    w_gate, w1, b1_cols, w2 = (g["w_gate"], g["w1"], g["b1_cols"], g["w2"])
    b2pad, row0 = g["b2pad"], g["row0"]
    x1_s, h_s, t2f_s, y = g["x1_s"], g["h_s"], g["t2f_s"], g["y"]

    GH = min(4, NHC)
    NG = NHC // GH

    with contextlib.ExitStack() as S:
        def sb(n, s, d=F32):
            return S.enter_context(nc.sbuf_tensor(n, s, d))

        t2_fm = sb("t2_fm", [128, KC * T], BF16)
        acc = sb("acc", [128, NT * D])
        h1_sb = sb("h1_sb", [128, GH * T], BF16)
        w1blk = [sb(f"w1blk{i}", [128, KC * 128], BF16) for i in range(2)]
        w1st = [sb(f"w1st{i}", [128, KC * 128]) for i in range(2)]
        w2g = [sb(f"w2g{i}", [128, GH * D], BF16) for i in range(2)]
        w2st = [sb(f"w2st{i}", [128, D]) for i in range(2)]
        wg_st = sb("wg_st", [128, E * KC]); wg_bf = sb("wg_bf", [128, E * KC], BF16)
        b1c = sb("b1c", [128, E * NHC])
        b2bf = [sb(f"b2bf{i}", [128, D], BF16) for i in range(2)]
        b2stg = sb("b2stg", [128, D])
        row0_sb = sb("row0_sb", [128, 128], BF16)
        we_sb = sb("we_sb", [128, NT * 8])
        gl_all = sb("gl_all", [128, NT * 8])
        gwide = sb("gwide", [128, NT * 8]); gwide2 = sb("gwide2", [128, NT * 8])
        gmaskw = sb("gmaskw", [128, NT * 8])
        g16a = sb("g16a", [128, NT]); g16b = sb("g16b", [128, NT])
        g16c = sb("g16c", [128, NT])
        h_tl = [sb(f"h_tl{i}", [128, D]) for i in range(2)]
        x1_tl = [sb(f"x1_tl{i}", [128, D]) for i in range(2)]
        of_t = [sb(f"of_t{i}", [128, D]) for i in range(2)]
        lnof = [sb(f"lnof{i}", [128, D]) for i in range(2)]
        lnt = make_ln_tmp(nc, S, "p3ln")
        lnt3b = make_ln_tmp(nc, S, "p3lnB")

        hps = [S.enter_context(nc.psum_tensor(f"hps{i}", [128, 512], F32))
               for i in range(8)]
        hpn = [f"hps{i}" for i in range(8)]
        gl_ps = hps[0][:, 0:8]

        p.dma("sync", row0_sb[:], row0)
        p.dma("sync", ap3(wg_st, 0, [[E, KC], [1, E]]),
              w_gate.rearrange("(k p) e -> p k e", p=128), writes=[wg_st])
        p.op("vector", "tensor_copy", wg_bf[:], wg_st[:],
             reads=[wg_st], writes=[wg_bf])
        p.dma("sync", b1c[:], b1_cols)

        t2_r = [f"t2_fm:{kc}" for kc in range(KC)]
        for s in range(NSEQ):
            for kc in range(KC):
                p.dma("sync", t2_fm[:, kc * T:(kc + 1) * T], t2f_s[s, kc],
                      writes=[f"t2_fm:{kc}"])

            W8 = NT * 8
            seg = lambda t_: ap3(t_, 0, [[8, NT], [1, 8]])
            bc8 = lambda t_: ap3(t_, 0, [[16 if False else NT, 128]]
                                 ) if False else bass.AP(
                                     t_, 0, [[NT, 128], [1, NT], [0, 8]])
            for tt in range(NT):
                for kc in range(KC):
                    p.op("tensor", "matmul", gl_ps,
                         t2_fm[:, kc * T + tt * 128: kc * T + tt * 128 + 128],
                         wg_bf[:, kc * E: kc * E + E],
                         start=(kc == 0), stop=(kc == KC - 1),
                         reads=t2_r + [wg_bf], writes=[hpn[0]])
                p.op("scalar", "copy", gl_all[:, tt * 8: tt * 8 + 8], gl_ps,
                     reads=[hpn[0]], writes=[f"gl_all:{tt}"])
            gar = [f"gl_all:{tt}" for tt in range(NT)]
            p.op("vector", "reduce_max", g16a[:], seg(gl_all), axis=AX.X,
                 reads=gar, writes=[g16a])
            p.op("vector", "tensor_sub", gwide[:], gl_all[:], bc8(g16a),
                 reads=gar + [g16a], writes=[gwide])
            p.op("scalar", "activation", gwide[:], gwide[:], AF.Exp,
                 bias=0.0, scale=1.0, reads=[gwide], writes=[gwide])
            p.op("vector", "reduce_sum", g16b[:], seg(gwide), axis=AX.X,
                 reads=[gwide], writes=[g16b])
            p.op("vector", "reciprocal", g16c[:], g16b[:],
                 reads=[g16b], writes=[g16c])
            p.op("vector", "tensor_mul", gwide[:], gwide[:], bc8(g16c),
                 reads=[gwide, g16c], writes=[gwide])
            p.op("vector", "reduce_max", g16a[:], seg(gwide), axis=AX.X,
                 reads=[gwide], writes=[g16a])
            p.op("vector", "tensor_scalar_add", gwide2[:], gwide[:], 2.0,
                 reads=[gwide], writes=[gwide2])
            p.op("vector", "tensor_tensor", gmaskw[:], gwide[:], bc8(g16a),
                 ALU.is_lt, reads=[gwide, g16a], writes=[gmaskw])
            p.op("vector", "tensor_mul", gmaskw[:], gwide2[:], gmaskw[:],
                 reads=[gwide2, gmaskw], writes=[gmaskw])
            p.op("vector", "reduce_max", g16b[:], seg(gmaskw), axis=AX.X,
                 reads=[gmaskw], writes=[g16b])
            p.op("vector", "tensor_tensor", gmaskw[:], gwide2[:], bc8(g16b),
                 ALU.is_ge, reads=[gwide2, g16b], writes=[gmaskw])
            p.op("vector", "tensor_mul", gwide[:], gwide[:], gmaskw[:],
                 reads=[gwide, gmaskw], writes=[gwide])
            p.op("vector", "reduce_sum", g16b[:], seg(gwide), axis=AX.X,
                 reads=[gwide], writes=[g16b])
            p.op("vector", "reciprocal", g16c[:], g16b[:],
                 reads=[g16b], writes=[g16c])
            p.op("vector", "tensor_mul", we_sb[:], gwide[:], bc8(g16c),
                 reads=[gwide, g16c],
                 writes=[f"we_sb:{tt}" for tt in range(NT)])

            for tt in range(NT):
                p.op("gpsimd", "memset", acc[:, tt * D:(tt + 1) * D], 0.0,
                     writes=[f"acc:{tt}"])

            fin_done = [0]

            def finalize(tt):
                fin_done[0] += 1
                ht = h_tl[tt % 2]; x1t = x1_tl[tt % 2]
                p.dma("sync", ht[:], h_s[s, tt])
                p.dma("sync", x1t[:], x1_s[s, tt])
                t = [lnt, lnt3b][tt % 2]
                p.op("vector", "tensor_add", of_t[tt % 2][:],
                     acc[:, tt * D:(tt + 1) * D], ht[:],
                     reads=[f"acc:{tt}", ht], writes=[of_t[tt % 2]])
                ln_plain(p, cfg, of_t[tt % 2][:], lnof[tt % 2][:], t)
                p.op("vector", "tensor_add", lnof[tt % 2][:],
                     lnof[tt % 2][:], x1t[:],
                     reads=[lnof[tt % 2], x1t], writes=[lnof[tt % 2]])
                p.dma("scalar", y[s, tt * 128:(tt + 1) * 128, :],
                      lnof[tt % 2][:], writes=[f"y:{s}_{tt}"])

            for e in range(E):
                b2b = b2bf[e % 2]
                p.dma("sync", b2stg[:], b2pad[e])
                p.op("vector", "tensor_copy", b2b[:], b2stg[:],
                     reads=[b2stg], writes=[b2b])
                for gi in range(NG):
                    h1_r = []
                    for hg in range(GH):
                        hc = gi * GH + hg
                        wb = w1blk[hc % 2]; wst = w1st[hc % 2]
                        p.dma("sync",
                              ap3(wst, 0, [[128, KC], [1, 128]]),
                              w1[e, hc].rearrange("(k p) c -> p k c", p=128),
                              writes=[wst])
                        p.op("vector", "tensor_copy", wb[:], wst[:],
                             reads=[wst], writes=[wb])
                        for kc in range(KC):
                            for sub in range(NQC):
                                j = (hg % 2) * 4 + (sub % 4)
                                p.op("tensor", "matmul", hps[j][:],
                                     wb[:, kc * 128: kc * 128 + 128],
                                     t2_fm[:, kc * T + sub * 512:
                                           kc * T + sub * 512 + 512],
                                     start=(kc == 0), stop=(kc == KC - 1),
                                     reads=t2_r + [wb],
                                     writes=[hpn[j]])
                        for sub in range(NQC):
                            j = (hg % 2) * 4 + (sub % 4)
                            p.op("scalar", "activation",
                                 h1_sb[:, hg * T + sub * 512:
                                       hg * T + sub * 512 + 512],
                                 hps[j][:], AF.Gelu_apprx_tanh,
                                 bias=b1c[:, e * NHC + hc: e * NHC + hc + 1],
                                 scale=1.0,
                                 reads=[hpn[j], b1c],
                                 writes=[f"h1_sb:{hg}_{sub}"])
                            h1_r.append(f"h1_sb:{hg}_{sub}")
                    wg2 = w2g[gi % 2]
                    w2_r = []
                    for hg in range(GH):
                        hc = gi * GH + hg
                        wst = w2st[hg % 2]
                        p.dma("sync", wst[:], w2[e, hc * 128:(hc + 1) * 128, :])
                        p.op("vector", "tensor_copy",
                             wg2[:, hg * D: hg * D + D], wst[:],
                             reads=[wst], writes=[f"{wg2.name}:{hg}"])
                        w2_r.append(f"{wg2.name}:{hg}")
                    for tt in range(NT):
                        for half in range(2):
                            j = 4 + (tt % 2) * 2 + half
                            ps = hps[j]
                            if gi == 0:
                                p.op("tensor", "matmul", ps[:], row0_sb[:],
                                     b2b[:, half * 512: half * 512 + 512],
                                     start=True, stop=False,
                                     reads=[row0_sb, b2b],
                                     writes=[hpn[j]])
                            for hg in range(GH):
                                p.op("tensor", "matmul", ps[:],
                                     h1_sb[:, hg * T + tt * 128:
                                           hg * T + tt * 128 + 128],
                                     wg2[:, hg * D + half * 512:
                                         hg * D + half * 512 + 512],
                                     start=(hg == 0 and gi != 0),
                                     stop=(hg == GH - 1),
                                     reads=h1_r + w2_r, writes=[hpn[j]])
                            p.op("vector", "scalar_tensor_tensor",
                                 acc[:, tt * D + half * 512:
                                     tt * D + half * 512 + 512],
                                 ps[:],
                                 we_sb[:, tt * 8 + e: tt * 8 + e + 1],
                                 acc[:, tt * D + half * 512:
                                     tt * D + half * 512 + 512],
                                 ALU.mult, ALU.add,
                                 reads=[hpn[j], f"we_sb:{tt}", f"acc:{tt}"],
                                 writes=[f"acc:{tt}"])
                        if e == E - 1 and gi == NG - 1:
                            finalize(tt)

            assert fin_done[0] == NT
            fin_done[0] = 0


def _b2pad(b2, E, D):
    z = np.zeros((E, 128, D), np.float32)
    z[:, 0, :] = np.asarray(b2, np.float32)
    return z


def _row0():
    import ml_dtypes
    z = np.zeros((128, 128), np.float32)
    z[0] = 1.0
    return z.astype(ml_dtypes.bfloat16)


def make_in_map(cfg, x0_core, W):
    import ml_dtypes
    D, E, NHC = cfg.D, cfg.E, cfg.NHC

    def f32(v):
        return np.ascontiguousarray(np.asarray(v, np.float32))

    def bcast(v):
        v = np.asarray(v, np.float32)
        return np.ascontiguousarray(np.broadcast_to(v[None, :], (128, v.shape[0])))

    def cols(v, n):
        return np.ascontiguousarray(np.asarray(v, np.float32).reshape(n, 128).T)

    m = {
        "x0": f32(x0_core),
        "Wq": f32(W["L0_attn_Wq"]), "Wk": f32(W["L0_attn_Wk"]),
        "Wv": f32(W["L0_attn_Wv"]), "Wo": f32(W["L0_attn_Wo"]),
        "Wf": f32(W["L0_attn_Wf"]),
        "bf_col": f32(np.asarray(W["L0_attn_bf"]).reshape(-1, 1)),
        "bf_bc": bcast(np.asarray(W["L0_attn_bf"])),
        "bo_bc": bcast(np.asarray(W["L0_attn_bo"])),
        "w_qkv": f32(W["L0_ffn_w_qkv"]),
        "bqkv_cols": np.concatenate(
            [cols(np.asarray(W["L0_ffn_b_qkv"])[0:D], 8),
             cols(np.asarray(W["L0_ffn_b_qkv"])[D:2 * D], 8)], axis=1),
        "bqkv_v_bc": bcast(np.asarray(W["L0_ffn_b_qkv"])[2 * D:3 * D]),
        "w_o2": f32(W["L0_ffn_w_o"]),
        "bo2_bc": bcast(np.asarray(W["L0_ffn_b_o"])),
        "ln1w_bc": bcast(np.asarray(W["L0_ffn_ln1_w"])),
        "ln1b_bc": bcast(np.asarray(W["L0_ffn_ln1_b"])),
        "ln2w_bc": bcast(np.asarray(W["L0_ffn_ln2_w"])),
        "ln2b_bc": bcast(np.asarray(W["L0_ffn_ln2_b"])),
        "w_gate": f32(W["L0_ffn_w_gate"]),
        "w1": np.ascontiguousarray(
            np.asarray(W["L0_ffn_w1"], np.float32)
            .reshape(E, D, NHC, 128).transpose(0, 2, 1, 3)),
        "b1_cols": np.ascontiguousarray(
            np.asarray(W["L0_ffn_b1"], np.float32).reshape(E, NHC, 128)
            .transpose(2, 0, 1).reshape(128, E * NHC)),
        "w2": f32(W["L0_ffn_w2"]),
        "b2pad": _b2pad(W["L0_ffn_b2"], E, D),
        "row0": _row0(),
        "eye": np.eye(128).astype(ml_dtypes.bfloat16),
    }
    return {k: np.ascontiguousarray(v) for k, v in m.items()}


def kernel(**inputs):
    cfg = Cfg()
    x = np.asarray(inputs["L0_attn_x"], np.float32)
    nc = build_program(cfg)
    in_maps = [make_in_map(cfg, x[2 * c: 2 * c + 2], inputs) for c in range(8)]
    res = run_bass_kernel_spmd(nc, in_maps, core_ids=list(range(8)))
    out = np.concatenate([r["y"] for r in res.results], axis=0)
    return np.ascontiguousarray(out.astype(np.float32))


# version 11
# speedup vs baseline: 1.0158x; 1.0158x over previous
import contextlib
import numpy as np
import concourse.bass as bass
import concourse.mybir as mybir
from concourse.bass_utils import run_bass_kernel_spmd

F32 = mybir.dt.float32
BF16 = mybir.dt.bfloat16
AF = mybir.ActivationFunctionType
ALU = mybir.AluOpType
AX = mybir.AxisListType

ENGINES = ("tensor", "vector", "scalar", "gpsimd", "sync")


def apname(a):
    if isinstance(a, str):
        return a
    t = getattr(a, "tensor", None)
    if t is not None:
        return t.name
    return a.name


class Prog:

    SEM_CAP = 24000

    def __init__(self, nc, n_dma_sems=24):
        self.nc = nc
        self.eng = {e: getattr(nc, e) for e in ENGINES}
        self.stack = contextlib.ExitStack()
        self.psem = {e: [self.stack.enter_context(nc.semaphore(f"prog_{e}0"))]
                     for e in ENGINES}
        self.gen = {e: 0 for e in ENGINES}
        self.base = {e: 0 for e in ENGINES}
        self.pcnt = {e: 0 for e in ENGINES}
        self.dsems = [self.stack.enter_context(nc.semaphore(f"dma{i}"))
                      for i in range(n_dma_sems)]
        self.dcnt = [0] * n_dma_sems
        self.drr = 0
        self.res = {}
        self.waited = {e: {} for e in ENGINES}

    def _sem(self, kind, key):
        if kind == "eng":
            e, gen = key
            return self.psem[e][gen]
        return self.dsems[key]

    def _etok(self, e):
        if self.pcnt[e] - self.base[e] > self.SEM_CAP:
            self.gen[e] += 1
            self.base[e] = self.pcnt[e] - 1
            self.psem[e].append(self.stack.enter_context(
                self.nc.semaphore(f"prog_{e}{self.gen[e]}")))
        return ("eng", (e, self.gen[e]), self.pcnt[e] - self.base[e])

    def _edges(self, e, reads, writes, extra=()):
        need = {}

        def add(tok):
            if tok is None:
                return
            kind, key, val = tok
            if kind == "eng" and key[0] == e == "tensor":
                return
            k = (kind, key)
            if val > need.get(k, 0):
                need[k] = val

        for r in reads:
            rec = self.res.get(apname(r))
            if rec:
                add(rec["w"])
        for w in writes:
            rec = self.res.get(apname(w))
            if rec:
                add(rec["w"])
                for t in rec["r"]:
                    add(t)
        for t in extra:
            add(t)
        out = []
        for (kind, key), val in need.items():
            if val > self.waited[e].get((kind, key), 0):
                out.append((kind, key, val))
        return out

    def _post(self, e, tok, reads, writes, edges):
        w = self.waited[e]
        for (kind, key, val) in edges:
            if val > w.get((kind, key), 0):
                w[(kind, key)] = val
        for x in writes:
            self.res[apname(x)] = {"w": tok, "r": []}
        for r in reads:
            rec = self.res.setdefault(apname(r), {"w": None, "r": []})
            rec["r"].append(tok)

    def op(self, e, method, *args, reads=(), writes=(), **kw):
        edges = self._edges(e, reads, writes)
        for (kind, key, val) in edges[:-1]:
            self.eng[e].wait_ge(self._sem(kind, key), val)
        inst = getattr(self.eng[e], method)(*args, **kw)
        if edges:
            kind, key, val = edges[-1]
            inst._wait_ge(self._sem(kind, key), val)
        self.pcnt[e] += 1
        tok = self._etok(e)
        inst.then_inc(self._sem(tok[0], tok[1]), 1)
        self._post(e, tok, reads, writes, edges)
        return inst

    def dma(self, e, out, in_, reads=(), writes=(), **kw):
        i = self.drr
        self.drr = (self.drr + 1) % len(self.dsems)
        extra = []
        if self.dcnt[i] > 0:
            extra.append(("dma", i, self.dcnt[i]))
        reads = list(reads)
        writes = list(writes)
        if not isinstance(in_, str) and getattr(in_, "tensor", None) is not None:
            reads.append(in_)
        if not isinstance(out, str) and getattr(out, "tensor", None) is not None:
            writes.append(out)
        edges = self._edges(e, reads, writes, extra)
        for (kind, key, val) in edges[:-1]:
            self.eng[e].wait_ge(self._sem(kind, key), val)
        inst = self.eng[e].dma_start(out=out, in_=in_, **kw)
        if edges:
            kind, key, val = edges[-1]
            inst._wait_ge(self._sem(kind, key), val)
        self.dcnt[i] += 16
        inst.then_inc(self.dsems[i], 16)
        w = self.waited[e]
        for (kind, key, val) in edges:
            if val > w.get((kind, key), 0):
                w[(kind, key)] = val
        tok = ("dma", i, self.dcnt[i])
        for x in writes:
            self.res[apname(x)] = {"w": tok, "r": []}
        for r in reads:
            rec = self.res.setdefault(apname(r), {"w": None, "r": []})
            rec["r"].append(tok)
        return inst

    def barrier(self):
        for e in ENGINES:
            w = self.waited[e]
            for f in ENGINES:
                if f == e:
                    continue
                key = (f, self.gen[f])
                val = self.pcnt[f] - self.base[f]
                if val > w.get(("eng", key), 0):
                    self.eng[e].wait_ge(self.psem[f][self.gen[f]], val)
                    w[("eng", key)] = val
            for i, c in enumerate(self.dcnt):
                if c > w.get(("dma", i), 0):
                    self.eng[e].wait_ge(self.dsems[i], c)
                    w[("dma", i)] = c

    def finish(self):
        self.barrier()
        self.stack.close()


def ap3(t, off, dims, nparts=128, part0=0):
    fs = int(t.shape[-1])
    return bass.AP(t, part0 * fs + off, [[fs, nparts]] + dims)


class Cfg:
    def __init__(self, T=2048, NSEQ=2, HID=4096, E=8):
        self.D = 1024
        self.NH = 16
        self.DH = 64
        self.F = 64
        self.T = T
        self.NSEQ = NSEQ
        self.HID = HID
        self.E = E
        self.KC = self.D // 128
        self.NT = T // 128
        self.NQC = T // 512
        self.NHC = HID // 128
        self.EPS = 1e-5
        assert T % 512 == 0


def make_ln_tmp(nc, st, name):
    t = {}
    for n, sh in (("st6", 12), ("mv", 2), ("lnv", 1), ("rstd", 1),
                  ("eps", 1), ("sc1", 1), ("sc2", 1), ("sc3", 1), ("sc4", 1)):
        t[n] = st.enter_context(nc.sbuf_tensor(f"{name}_{n}", [128, sh], F32))
    t["init"] = False
    return t


def ln_plain(p, cfg, x_ap, out_ap, t, x_res=None, out_res=None):
    v = "vector"
    D = cfg.D
    if x_res is None:
        xr = [x_ap]
    elif isinstance(x_res, (list, tuple)):
        xr = list(x_res)
    else:
        xr = [x_res]
    if not t["init"]:
        p.op(v, "memset", t["eps"][:], cfg.EPS, writes=[t["eps"]])
        t["init"] = True
    half = 512
    xa = x_ap
    p.op(v, "bn_stats", t["st6"][:, 0:6],
         bass.AP(xa.tensor, xa.offset, [xa.ap[0], [1, half]]),
         reads=xr, writes=[t["st6"]])
    p.op(v, "bn_stats", t["st6"][:, 6:12],
         bass.AP(xa.tensor, xa.offset + half, [xa.ap[0], [1, half]]),
         reads=xr, writes=[t["st6"]])
    p.op(v, "bn_aggr", t["mv"][:], t["st6"][:],
         reads=[t["st6"]], writes=[t["mv"]])
    p.op("scalar", "activation", t["lnv"][:], t["mv"][:, 1:2], AF.Ln,
         bias=t["eps"][:, 0:1], scale=1.0,
         reads=[t["mv"], t["eps"]], writes=[t["lnv"]])
    p.op("scalar", "activation", t["rstd"][:], t["lnv"][:], AF.Exp,
         bias=0.0, scale=-0.5, reads=[t["lnv"]], writes=[t["rstd"]])
    p.op(v, "tensor_scalar", out_ap, x_ap, t["mv"][:, 0:1], t["rstd"][:, 0:1],
         ALU.subtract, ALU.mult,
         reads=xr + [t["mv"], t["rstd"]],
         writes=[out_res if out_res else out_ap])


def load_w_dd(p, cfg, wbuf, stag, hbm_ap, cols):
    src = hbm_ap.rearrange("(k p) c -> k p c", p=128)
    wn = wbuf.name
    for kc in range(cfg.KC):
        s = stag[kc % 2]
        p.dma("sync", s[:, 0:cols], src[kc], writes=[s])
        p.op("vector", "tensor_copy", wbuf[:, kc * cols:(kc + 1) * cols],
             s[:, 0:cols], reads=[s], writes=[f"{wn}:{kc}"])
    return [f"{wn}:{kc}" for kc in range(cfg.KC)]


def build_program(cfg: Cfg):
    nc = bass.Bass("TRN2", target_bir_lowering=False)
    D, NH, DH, Fd = cfg.D, cfg.NH, cfg.DH, cfg.F
    T, NSEQ, KC, NT, NQC = cfg.T, cfg.NSEQ, cfg.KC, cfg.NT, cfg.NQC
    NHC, E, HID = cfg.NHC, cfg.E, cfg.HID

    def dt(n, s, d=F32, k="ExternalInput"):
        return nc.dram_tensor(n, s, d, kind=k).ap()

    x0 = dt("x0", [NSEQ, T, D])
    Wq = dt("Wq", [D, D]); Wk = dt("Wk", [D, D]); Wv = dt("Wv", [D, D])
    Wo = dt("Wo", [D, D])
    Wf = dt("Wf", [DH, Fd])
    bf_col = dt("bf_col", [Fd, 1])
    bf_bc = dt("bf_bc", [128, Fd])
    bo_bc = dt("bo_bc", [128, D])
    w_qkv = dt("w_qkv", [D, 3 * D])
    bqkv_cols = dt("bqkv_cols", [128, 16])
    bqkv_v_bc = dt("bqkv_v_bc", [128, D])
    w_o2 = dt("w_o2", [D, D])
    bo2_bc = dt("bo2_bc", [128, D])
    ln1w_bc = dt("ln1w_bc", [128, D]); ln1b_bc = dt("ln1b_bc", [128, D])
    ln2w_bc = dt("ln2w_bc", [128, D]); ln2b_bc = dt("ln2b_bc", [128, D])
    w_gate = dt("w_gate", [D, E])
    w1 = dt("w1", [E, NHC, D, 128])
    b1_cols = dt("b1_cols", [128, E * NHC])
    w2 = dt("w2", [E, HID, D])
    b2pad = dt("b2pad", [E, 128, D])
    row0 = dt("row0", [128, 128], BF16)
    eye = dt("eye", [128, 128], BF16)
    y = dt("y", [NSEQ, T, D], F32, "ExternalOutput")

    def sc(n, s, d):
        return nc.dram_tensor(n, s, d, kind="Internal").ap()

    x1_s = sc("x1_s", [NSEQ, NT, 128, D], F32)
    fin_s = sc("fin_s", [NSEQ, NT, 128, D], F32)
    h_s = sc("h_s", [NSEQ, NT, 128, D], F32)
    t1f_s = sc("t1f_s", [NSEQ, KC, 128, T], BF16)
    t2f_s = sc("t2f_s", [NSEQ, KC, 128, T], BF16)

    p = Prog(nc)
    phase1(p, cfg, nc, locals())
    p.barrier()
    phase2(p, cfg, nc, locals())
    p.barrier()
    phase3(p, cfg, nc, locals())
    p.finish()
    return nc


def phase1(p, cfg, nc, g):
    D, NH, DH, Fd = cfg.D, cfg.NH, cfg.DH, cfg.F
    T, NSEQ, KC, NT, NQC = cfg.T, cfg.NSEQ, cfg.KC, cfg.NT, cfg.NQC
    x0, Wq, Wk, Wv, Wo = g["x0"], g["Wq"], g["Wk"], g["Wv"], g["Wo"]
    Wf, bf_col, bf_bc, bo_bc = g["Wf"], g["bf_col"], g["bf_bc"], g["bo_bc"]
    ln1w_bc, ln1b_bc = g["ln1w_bc"], g["ln1b_bc"]
    eye, x1_s, fin_s, t1f_s = g["eye"], g["x1_s"], g["fin_s"], g["t1f_s"]

    with contextlib.ExitStack() as S:
        def sb(n, s, d=F32):
            return S.enter_context(nc.sbuf_tensor(n, s, d))

        eye_sb = sb("eye_sb", [128, 128], BF16)
        p.dma("sync", eye_sb[:], eye)
        bfb = sb("bfb", [128, Fd]); p.dma("sync", bfb[:], bf_bc)
        bfc = sb("bfc", [Fd, 1]); p.dma("sync", bfc[:], bf_col)
        bob = sb("bob", [128, D]); p.dma("sync", bob[:], bo_bc)
        l1w = sb("l1w", [128, D]); p.dma("sync", l1w[:], ln1w_bc)
        l1b = sb("l1b", [128, D]); p.dma("sync", l1b[:], ln1b_bc)
        wf_st = sb("wf_st", [DH, Fd]); p.dma("sync", wf_st[:], Wf)
        wf_sb = sb("wf_sb", [128, Fd], BF16)
        p.op("vector", "tensor_copy", wf_sb[0:64, :], wf_st[:],
             reads=[wf_st], writes=["wf_sb:0"])
        p.op("vector", "tensor_copy", wf_sb[64:128, :], wf_st[:],
             reads=[wf_st], writes=["wf_sb:1"])
        ones64 = sb("ones64", [1, Fd], F32)
        p.op("vector", "memset", ones64[:], 1.0, writes=[ones64])

        a_fm = sb("a_fm", [128, KC * T], BF16)
        qf_fm = sb("qf_fm", [128, (NH // 2) * 512], BF16)
        v_tm = sb("v_tm", [128, NT * NH * 65], BF16)
        kv_sb = sb("kv_sb", [128, NH * 65], BF16)
        outn = sb("outn", [128, (NH // 2) * 512], BF16)
        wbuf = sb("wbufA", [128, KC * D], BF16)
        wbufB = sb("wbufB", [128, KC * D], BF16)
        wstag = [sb("wstag0", [128, D])]
        wstag.append(wstag[0])
        x0t = [sb(f"x0t{i}", [128, D]) for i in range(2)]
        a_tm = [sb(f"a_tm{i}", [128, D], BF16) for i in range(2)]
        lnt = make_ln_tmp(nc, S, "p1ln")
        lnt2 = make_ln_tmp(nc, S, "p1lnB")
        nrA = [sb(f"nrA{i}", [128, 1024]) for i in range(2)]
        ksb = [sb(f"ksb{i}", [128, T], BF16) for i in range(2)]
        kf_f = [sb(f"kf_f{i}", [128, Fd]) for i in range(2)]
        kf_t = [sb(f"kf_t{i}", [128, Fd], BF16) for i in range(2)]

        ao_t = [sb(f"ao_t{i}", [128, D]) for i in range(2)]
        lnao = sb("lnao", [128, D])
        aotmp = sb("aotmp", [128, 512])
        x1_t = [sb(f"x1_t{i}", [128, D]) for i in range(2)]
        fin_t = [sb(f"fin_t{i}", [128, D]) for i in range(2)]
        xh_t = sb("xh_t", [128, D])
        t1_tm = [sb(f"t1_tm{i}", [128, D], BF16) for i in range(2)]
        t1f_buf = [sb(f"t1f_buf{i}", [128, KC * 128], BF16) for i in range(2)]

        prj = [S.enter_context(nc.psum_tensor(f"prj{i}", [128, 512], F32))
               for i in range(4)]
        kv_ps = [S.enter_context(nc.psum_tensor(f"kv_ps{i}", [Fd, 65], F32))
                 for i in range(2)]
        kv_pn = ["kv_ps0", "kv_ps1"]
        rbp = S.enter_context(nc.psum_tensor("rbp", [Fd, 512], F32))
        pTT = S.enter_context(nc.psum_tensor("pTT", [128, 256], BF16))
        pT = [pTT[:, 0:128], pTT[:, 128:256]]
        pTn = ["pTT", "pTT"]
        qf_ps = [prj[2][0:64, :], prj[3][0:64, :]]
        qf_pn = ["prj2", "prj3"]
        kf_ps = [prj[2][:, 0:64], prj[3][:, 0:64]]
        kf_pn = ["prj2", "prj3"]
        out_ps = [prj[2][0:65, :], prj[3][0:65, :]]
        out_pn = ["prj2", "prj3"]
        prjn = ["prj0", "prj1", "prj2", "prj3"]

        p.op("vector", "memset",
             ap3(v_tm, 64, [[NH * 65, NT], [65, NH], [1, 1]]),
             1.0, writes=["v_tm:ones"])

        afm_r = [f"a_fm:{kc}_{tt}" for kc in range(KC) for tt in range(NT)]
        wbuf_r = [f"wbufA:{kc}" for kc in range(KC)]
        wbufB_r = [f"wbufB:{kc}" for kc in range(KC)]

        for s in range(NSEQ):
            for tt in range(NT):
                xt = x0t[tt % 2]
                p.dma("sync", xt[:], x0[s, tt * 128:(tt + 1) * 128, :])
                at = a_tm[tt % 2]
                ln_plain(p, cfg, xt[:], at[:], lnt)
                for kc in range(KC):
                    i = kc % 2
                    p.op("tensor", "transpose", pT[i],
                         at[:, kc * 128:(kc + 1) * 128], eye_sb[:],
                         reads=[at, eye_sb], writes=[pTn[i]])
                    p.op("vector", "tensor_copy",
                         a_fm[:, kc * T + tt * 128: kc * T + tt * 128 + 128],
                         pT[i], reads=[pTn[i]], writes=[f"a_fm:{kc}_{tt}"])

            load_w_dd(p, cfg, wbuf, wstag, Wv, D)
            load_w_dd(p, cfg, wbufB, wstag, Wk, D)
            for lt in range(NT):
                alt_r = [f"a_fm:{kc}_{lt}" for kc in range(KC)]
                for half in range(2):
                    ps = prj[half]
                    for kc in range(KC):
                        p.op("tensor", "matmul", ps[:],
                             a_fm[:, kc * T + lt * 128: kc * T + lt * 128 + 128],
                             wbuf[:, kc * D + half * 512: kc * D + half * 512 + 512],
                             start=(kc == 0), stop=(kc == KC - 1),
                             reads=alt_r + wbuf_r, writes=[prjn[half]])
                    p.op("vector", "tensor_copy",
                         ap3(v_tm, lt * NH * 65 + half * 8 * 65,
                             [[65, 8], [1, 64]]),
                         ap3(ps, 0, [[64, 8], [1, 64]]),
                         reads=[prjn[half]], writes=[f"v_tm:{lt}_{half}"])
            v_r = [f"v_tm:{lt}_{h}" for lt in range(NT) for h in range(2)]

            load_w_dd(p, cfg, wbuf, wstag, Wq, D)

            def c_proj_units(dc):
                us = []
                ks = ksb[dc % 2]
                for qc in range(NQC):
                    aqc_r = [f"a_fm:{kc}_{qc * 4 + j}" for kc in range(KC)
                             for j in range(4)]
                    for kc in range(KC):
                        def um(dc=dc, qc=qc, kc=kc, aqc_r=aqc_r):
                            p.op("tensor", "matmul", prj[qc % 2][:],
                                 wbufB[:, kc * D + dc * 128:
                                       kc * D + dc * 128 + 128],
                                 a_fm[:, kc * T + qc * 512:
                                      kc * T + qc * 512 + 512],
                                 start=(kc == 0), stop=(kc == KC - 1),
                                 reads=aqc_r + wbufB_r, writes=[prjn[qc % 2]])
                        us.append(um)
                    def ue(dc=dc, qc=qc, ks=ks):
                        p.op("scalar", "copy",
                             ks[:, qc * 512:(qc + 1) * 512], prj[qc % 2][:],
                             reads=[prjn[qc % 2]],
                             writes=[f"{ks.name}:{qc}"])
                    us.append(ue)
                return us

            def c_kfkv_units(dc):
                us = []
                ks = ksb[dc % 2]
                ks_r = [f"{ks.name}:{qc}" for qc in range(NQC)]
                for hh in range(2):
                    h16 = dc * 2 + hh
                    for lt in range(NT):
                        def uk(hh=hh, h16=h16, lt=lt, ks=ks, ks_r=ks_r):
                            i = lt % 2
                            p.op("tensor", "matmul", kf_ps[i],
                                 ks[hh * 64: hh * 64 + 64,
                                    lt * 128: lt * 128 + 128],
                                 wf_sb[hh * 64: hh * 64 + 64, :],
                                 start=True, stop=True,
                                 reads=ks_r + ["wf_sb:0", "wf_sb:1"],
                                 writes=[kf_pn[i]])
                            p.op("vector", "tensor_add", kf_f[i][:], kf_ps[i],
                                 bfb[:], reads=[kf_pn[i], bfb],
                                 writes=[kf_f[i]])
                            p.op("vector", "tensor_relu", kf_t[i][:],
                                 kf_f[i][:], reads=[kf_f[i]],
                                 writes=[kf_t[i]])
                            p.op("tensor", "matmul",
                                 kv_ps[h16 % 2][:], kf_t[i][:],
                                 ap3(v_tm, lt * NH * 65 + h16 * 65, [[1, 65]]),
                                 start=(lt == 0), stop=(lt == NT - 1),
                                 reads=[kf_t[i], "v_tm:ones"] + v_r,
                                 writes=[kv_pn[h16 % 2]])
                        us.append(uk)
                    def uev(h16=h16):
                        for hlf, wres in ((0, "kv_sb:0"), (64, "kv_sb:1")):
                            p.op("scalar", "copy",
                                 kv_sb[hlf:hlf + 64, h16 * 65: h16 * 65 + 65],
                                 kv_ps[h16 % 2][:],
                                 reads=[kv_pn[h16 % 2]],
                                 writes=[f"{wres}_{h16}"])
                    us.append(uev)
                return us

            for u in c_proj_units(0):
                u()
            for dc in range(KC):
                kfu = c_kfkv_units(dc)
                nxt = c_proj_units(dc + 1) if dc + 1 < KC else []
                a = b = 0
                while a < len(kfu) or b < len(nxt):
                    if a < len(kfu):
                        kfu[a]()
                        a += 1
                    if b < len(nxt):
                        nxt[b]()
                        b += 1

            load_w_dd(p, cfg, wbufB, wstag, Wo, D)
            qf_r = [f"qf_fm:{h}" for h in range(NH)]
            for qc in range(NQC):
                for dc in range(KC):
                    ks = ksb[dc % 2]
                    ps = prj[dc % 2]
                    for kc in range(KC):
                        p.op("tensor", "matmul", ps[:],
                             wbuf[:, kc * D + dc * 128: kc * D + dc * 128 + 128],
                             a_fm[:, kc * T + qc * 512: kc * T + qc * 512 + 512],
                             start=(kc == 0), stop=(kc == KC - 1),
                             reads=afm_r + wbuf_r, writes=[prjn[dc % 2]])
                    p.op("scalar", "copy", ks[:, 0:512], ps[:],
                         reads=[prjn[dc % 2]], writes=[f"{ks.name}:0"])
                    for hh in range(2):
                        h16 = dc * 2 + hh
                        i = dc % 2
                        p.op("tensor", "matmul", qf_ps[i],
                             wf_sb[hh * 64: hh * 64 + 64, :],
                             ks[hh * 64: hh * 64 + 64, 0:512],
                             start=True, stop=True,
                             reads=[f"{ks.name}:0", "wf_sb:0", "wf_sb:1"],
                             writes=[qf_pn[i]])
                        p.op("scalar", "activation",
                             qf_fm[(h16 % 2) * 64:(h16 % 2) * 64 + 64,
                                   (h16 // 2) * 512:(h16 // 2) * 512 + 512],
                             qf_ps[i], AF.Relu, bias=bfc[:, 0:1], scale=1.0,
                             reads=[qf_pn[i], bfc], writes=[f"qf_fm:{h16}"])
                on_r = [f"outn:{h}" for h in range(NH)]
                for h16 in range(NH):
                    i = h16 % 2
                    p.op("tensor", "matmul", out_ps[i],
                         kv_sb[(h16 % 2) * 64:(h16 % 2) * 64 + 64,
                               h16 * 65: h16 * 65 + 65],
                         qf_fm[(h16 % 2) * 64:(h16 % 2) * 64 + 64,
                               (h16 // 2) * 512:(h16 // 2) * 512 + 512],
                         start=True, stop=True,
                         reads=[f"kv_sb:0_{h16}", f"kv_sb:1_{h16}"] + qf_r,
                         writes=[out_pn[i]])
                    nr = nrA[i]
                    p.op("vector", "tensor_scalar_max", nr[0:1, 0:512],
                         out_ps[i][64:65, :], 1e-6,
                         reads=[out_pn[i]],
                         writes=[f"{nr.name}:a", f"{nr.name}:c"])
                    p.op("scalar", "activation", nr[0:1, 512:1024],
                         nr[0:1, 0:512], AF.Ln, bias=0.0, scale=1.0,
                         reads=[f"{nr.name}:a"], writes=[f"{nr.name}:b"])
                    p.op("scalar", "activation", nr[0:1, 0:512],
                         nr[0:1, 512:1024], AF.Exp, bias=0.0, scale=-1.0,
                         reads=[f"{nr.name}:b"],
                         writes=[f"{nr.name}:c", f"{nr.name}:a"])
                    p.op("tensor", "matmul", rbp[:], ones64[:],
                         nr[0:1, 0:512],
                         start=True, stop=True, reads=[ones64, f"{nr.name}:c"],
                         writes=["rbp"])
                    p.op("scalar", "copy", nr[64:128, 0:512], rbp[:],
                         reads=["rbp"], writes=[f"{nr.name}:rb"])
                    p.op("vector", "tensor_mul",
                         outn[(h16 % 2) * 64:(h16 % 2) * 64 + 64,
                              (h16 // 2) * 512:(h16 // 2) * 512 + 512],
                         out_ps[i][0:64, :], nr[64:128, 0:512],
                         reads=[out_pn[i], f"{nr.name}:rb"],
                         writes=[f"outn:{h16}"])
                wbuf_r2 = wbufB_r

                def chain(t4):
                    tt = qc * 4 + t4
                    ao = ao_t[t4 % 2]
                    t = [lnt, lnt2][t4 % 2]
                    lna = [lnao, xh_t][t4 % 2]
                    t1t = t1_tm[t4 % 2]
                    ln_plain(p, cfg, ao[:], lna[:], t,
                             x_res=[f"{ao.name}:0", f"{ao.name}:1"])
                    xt = x0t[tt % 2]
                    p.dma("sync", xt[:], x0[s, tt * 128:(tt + 1) * 128, :])
                    x1t = x1_t[tt % 2]; fint = fin_t[tt % 2]
                    p.op("vector", "tensor_add", x1t[:], lna[:], xt[:],
                         reads=[lna, xt], writes=[x1t])
                    p.dma("scalar", x1_s[s, tt], x1t[:])
                    v = "vector"
                    p.op(v, "bn_stats", t["st6"][:, 0:6], x1t[:, 0:512],
                         reads=[x1t], writes=[t["st6"]])
                    p.op(v, "bn_stats", t["st6"][:, 6:12], x1t[:, 512:1024],
                         reads=[x1t], writes=[t["st6"]])
                    p.op(v, "bn_aggr", t["mv"][:], t["st6"][:],
                         reads=[t["st6"]], writes=[t["mv"]])
                    p.op("scalar", "activation", t["lnv"][:], t["mv"][:, 1:2],
                         AF.Ln, bias=t["eps"][:, 0:1], scale=1.0,
                         reads=[t["mv"], t["eps"]], writes=[t["lnv"]])
                    p.op("scalar", "activation", t["rstd"][:], t["lnv"][:],
                         AF.Exp, bias=0.0, scale=-0.5,
                         reads=[t["lnv"]], writes=[t["rstd"]])
                    p.op(v, "tensor_scalar", fint[:], x1t[:], t["mv"][:, 0:1],
                         t["rstd"][:, 0:1], ALU.subtract, ALU.mult,
                         reads=[x1t, t["mv"], t["rstd"]], writes=[fint])
                    p.dma("scalar", fin_s[s, tt], fint[:])
                    p.op("scalar", "activation", t["sc1"][:], t["lnv"][:],
                         AF.Exp, bias=0.0, scale=-1.0,
                         reads=[t["lnv"]], writes=[t["sc1"]])
                    p.op(v, "tensor_mul", t["sc2"][:], t["mv"][:, 1:2],
                         t["sc1"][:], reads=[t["mv"], t["sc1"]],
                         writes=[t["sc2"]])
                    p.op("scalar", "activation", t["sc3"][:], t["sc2"][:],
                         AF.Ln, bias=t["eps"][:, 0:1], scale=1.0,
                         reads=[t["sc2"], t["eps"]], writes=[t["sc3"]])
                    p.op("scalar", "activation", t["sc4"][:], t["sc3"][:],
                         AF.Exp, bias=0.0, scale=-0.5,
                         reads=[t["sc3"]], writes=[t["sc4"]])
                    p.op(v, "tensor_mul", t["sc1"][:], t["rstd"][:],
                         t["sc4"][:], reads=[t["rstd"], t["sc4"]],
                         writes=[t["sc1"]])
                    p.op(v, "tensor_scalar", lna[:], x1t[:], t["mv"][:, 0:1],
                         t["sc1"][:, 0:1], ALU.subtract, ALU.mult,
                         reads=[x1t, t["mv"], t["sc1"]], writes=[lna])
                    p.op(v, "tensor_mul", lna[:], lna[:], l1w[:],
                         reads=[lna, l1w], writes=[lna])
                    p.op(v, "tensor_add", t1t[:], lna[:], l1b[:],
                         reads=[lna, l1b], writes=[t1t])
                    tb = t1f_buf[t4 % 2]
                    for kc in range(KC):
                        i = kc % 2
                        p.op("tensor", "transpose", pT[i],
                             t1t[:, kc * 128:(kc + 1) * 128], eye_sb[:],
                             reads=[t1t, eye_sb], writes=[pTn[i]])
                        p.op("vector", "tensor_copy",
                             tb[:, kc * 128:(kc + 1) * 128],
                             pT[i], reads=[pTn[i]],
                             writes=[f"{tb.name}:{kc}"])
                    p.dma("scalar",
                          t1f_s[s, :, :, tt * 128:(tt + 1) * 128]
                          .rearrange("k p t -> p k t"),
                          ap3(tb, 0, [[128, KC], [1, 128]]),
                          reads=[f"{tb.name}:{kc}" for kc in range(KC)],
                          writes=[f"t1f_s:{s}_{tt}"])
                def attn_out(t4):
                    ao = ao_t[t4 % 2]
                    for half in range(2):
                        for h16 in range(NH):
                            pi = half + 2 * (h16 % 2)
                            p.op("tensor", "matmul", prj[pi][:],
                                 outn[(h16 % 2) * 64:(h16 % 2) * 64 + 64,
                                      (h16 // 2) * 512 + t4 * 128:
                                      (h16 // 2) * 512 + t4 * 128 + 128],
                                 wbufB[(h16 % 2) * 64:(h16 % 2) * 64 + 64,
                                       (h16 // 2) * D + half * 512:
                                       (h16 // 2) * D + half * 512 + 512],
                                 start=(h16 <= 1), stop=(h16 >= NH - 2),
                                 reads=on_r + wbuf_r2, writes=[prjn[pi]])
                        p.op("scalar", "copy", aotmp[:], prj[2 + half][:],
                             reads=[prjn[2 + half]], writes=[aotmp])
                        p.op("vector", "tensor_add", aotmp[:], aotmp[:],
                             bob[:, half * 512: half * 512 + 512],
                             reads=[aotmp, bob], writes=[aotmp])
                        p.op("vector", "tensor_add",
                             ao[:, half * 512: half * 512 + 512],
                             prj[half][:], aotmp[:],
                             reads=[prjn[half], aotmp],
                             writes=[f"{ao.name}:{half}"])

                attn_out(0)
                for t4 in range(4):
                    if t4 + 1 < 4:
                        attn_out(t4 + 1)
                    chain(t4)


def phase2(p, cfg, nc, g):
    D, NH, DH = cfg.D, cfg.NH, cfg.DH
    T, NSEQ, KC, NT, NQC = cfg.T, cfg.NSEQ, cfg.KC, cfg.NT, cfg.NQC
    w_qkv, bqkv_cols, bqkv_v_bc = g["w_qkv"], g["bqkv_cols"], g["bqkv_v_bc"]
    w_o2, bo2_bc = g["w_o2"], g["bo2_bc"]
    ln2w_bc, ln2b_bc = g["ln2w_bc"], g["ln2b_bc"]
    eye, fin_s, h_s, t1f_s, t2f_s = (g["eye"], g["fin_s"], g["h_s"],
                                     g["t1f_s"], g["t2f_s"])

    with contextlib.ExitStack() as S:
        def sb(n, s, d=F32):
            return S.enter_context(nc.sbuf_tensor(n, s, d))

        eye_sb = sb("eye_sb2", [128, 128], BF16)
        p.dma("sync", eye_sb[:], eye)
        bqc = sb("bqc", [128, 16]); p.dma("sync", bqc[:], bqkv_cols)
        bvb = sb("bvb", [128, D]); p.dma("sync", bvb[:], bqkv_v_bc)
        bo2b = sb("bo2b", [128, D]); p.dma("sync", bo2b[:], bo2_bc)
        l2w = sb("l2w", [128, D]); p.dma("sync", l2w[:], ln2w_bc)
        l2b = sb("l2b", [128, D]); p.dma("sync", l2b[:], ln2b_bc)
        ones1 = sb("ones1", [1, 128], F32)
        p.op("vector", "memset", ones1[:], 1.0, writes=[ones1])

        t1_fm = sb("t1_fm", [128, KC * T], BF16)
        q_fm = sb("q_fm", [128, KC * T], BF16)
        k_fm = sb("k_fm", [128, KC * T], BF16)
        v_tm = sb("v_tm2", [128, NT * NH * 65], BF16)
        ovt = t1_fm
        wv2_bf = sb("wv2_bf", [128, KC * D], BF16)
        wqk = [sb(f"wqk{i}", [128, KC * 128], BF16) for i in range(2)]
        wqst = [sb("wqst0", [128, KC * 128])]
        wqst.append(wqst[0])
        t2f_tt = sb("t2f_tt", [128, KC * 128], BF16)
        wstag = [sb("w2stag0", [128, D])]
        wstag.append(wstag[0])
        p_bf = [wqk[0][:, 0:512], wqk[1][:, 0:512]]
        p_bn = ["wqk0", "wqk1"]
        lnt = make_ln_tmp(nc, S, "p2ln")
        lnt2x = make_ln_tmp(nc, S, "p2lnB")
        p2sc = sb("p2sc", [128, 512])
        vtmp = p2sc[:, 0:512]
        VT = "p2sc"
        bias_all = sb("bias_all", [128, KC])
        ones128b = sb("ones128b", [128, 1], BF16)
        p.op("vector", "memset", ones128b[:], 1.0, writes=[ones128b])
        nmaxq = sb("nmaxq", [1, 32]); nmaxk = sb("nmaxk", [1, 32])
        nsq = sb("nsq", [1, 16]); nsk = sb("nsk", [1, 16])
        brow = sb("brow", [1, 16])
        rrb = [sb(f"rrb{i}", [128, 512]) for i in range(2)]
        ao2 = sb("ao2", [128, D]); h_t = [sb(f"h_t{i}", [128, D]) for i in range(2)]
        fin_tl = sb("fin_tl", [128, D])
        xh2 = sb("xh2", [128, D])
        t2_tm = sb("t2_tm", [128, D], BF16)

        sc_ps = [S.enter_context(nc.psum_tensor(f"sc_ps{i}", [128, 512], F32))
                 for i in range(3)]
        scn = [f"sc_ps{i}" for i in range(3)]
        ctx_ps = [S.enter_context(nc.psum_tensor(f"ctx_ps{i}", [65, 512], F32))
                  for i in range(2)]
        ctx_pn = ["ctx_ps0", "ctx_ps1"]
        prj2 = [S.enter_context(nc.psum_tensor(f"prj2_{i}", [128, 512], F32))
                for i in range(2)]
        prj2n = ["prj2_0", "prj2_1"]
        smallT = S.enter_context(nc.psum_tensor("smallT", [128, 256], BF16))
        pT = [smallT[:, 0:128], smallT[:, 128:256]]
        pTn = ["smallT", "smallT"]
        rb2_ps = prj2[0][0:64, :]

        p.op("vector", "memset",
             ap3(v_tm, 64, [[NH * 65, NT], [65, NH], [1, 1]]),
             1.0, writes=["v_tm2:ones"])

        t1_r = [f"t1_fm:{kc}" for kc in range(KC)]
        wv2_r = [f"wv2_bf:{kc}" for kc in range(KC)]
        q_r = [f"q_fm:{dc}_{qc}" for dc in range(KC) for qc in range(NQC)]
        k_r = [f"k_fm:{dc}_{qc}" for dc in range(KC) for qc in range(NQC)]
        v_r = [f"v_tm2:{lt}_{h}" for lt in range(NT) for h in range(2)]

        for s in range(NSEQ):
            for kc in range(KC):
                p.dma("sync", t1_fm[:, kc * T:(kc + 1) * T], t1f_s[s, kc],
                      writes=[f"t1_fm:{kc}"])
            for kc in range(KC):
                st_ = wstag[kc % 2]
                p.dma("sync", st_[:],
                      w_qkv.rearrange("(k p) c -> k p c", p=128)[kc, :,
                                                                2 * D:3 * D])
                p.op("vector", "tensor_copy",
                     wv2_bf[:, kc * D: kc * D + D],
                     st_[:], reads=[st_], writes=[f"wv2_bf:{kc}"])

            for qk in range(2):
                dst, dname = (q_fm, "q_fm") if qk == 0 else (k_fm, "k_fm")
                for dc in range(KC):
                    wq = wqk[dc % 2]; wqs = wqst[dc % 2]
                    p.dma("sync",
                          ap3(wqs, 0, [[128, KC], [1, 128]]),
                          w_qkv[:, qk * D + dc * 128: qk * D + dc * 128 + 128]
                          .rearrange("(k p) c -> p k c", p=128),
                          writes=[wqs])
                    p.op("vector", "tensor_copy", wq[:], wqs[:],
                         reads=[wqs], writes=[wq])
                    for qc in range(NQC):
                        ps = prj2[qc % 2]
                        for kc in range(KC):
                            p.op("tensor", "matmul", ps[:],
                                 wq[:, kc * 128: kc * 128 + 128],
                                 t1_fm[:, kc * T + qc * 512:
                                       kc * T + qc * 512 + 512],
                                 start=(kc == 0), stop=(kc == KC - 1),
                                 reads=t1_r + [wq], writes=[prj2n[qc % 2]])
                        p.op("scalar", "activation",
                             dst[:, dc * T + qc * 512: dc * T + qc * 512 + 512],
                             ps[:], AF.Identity,
                             bias=bqc[:, qk * 8 + dc: qk * 8 + dc + 1],
                             scale=1.0,
                             reads=[prj2n[qc % 2], bqc],
                             writes=[f"{dname}:{dc}_{qc}"])

            for lt in range(NT):
                for half in range(2):
                    ps = prj2[half]
                    for kc in range(KC):
                        p.op("tensor", "matmul", ps[:],
                             t1_fm[:, kc * T + lt * 128: kc * T + lt * 128 + 128],
                             wv2_bf[:, kc * D + half * 512:
                                    kc * D + half * 512 + 512],
                             start=(kc == 0), stop=(kc == KC - 1),
                             reads=t1_r + wv2_r, writes=[prj2n[half]])
                    p.op("vector", "tensor_add", vtmp, ps[:],
                         bvb[:, half * 512: half * 512 + 512],
                         reads=[prj2n[half], bvb, VT], writes=[VT])
                    p.op("vector", "tensor_copy",
                         ap3(v_tm, lt * NH * 65 + half * 8 * 65,
                             [[65, 8], [1, 64]]),
                         ap3(p2sc, 0, [[64, 8], [1, 64]]),
                         reads=[VT], writes=[f"v_tm2:{lt}_{half}"])

            load_w_dd(p, cfg, wv2_bf, wstag, w_o2, D)

            sct = [0]
            for src, s_r, nmx, nsq_ in ((q_fm, q_r, nmaxq, nsq),
                                        (k_fm, k_r, nmaxk, nsk)):
                for cb in range(KC):
                    p.op("vector", "tensor_mul", ovt[:, cb * T:(cb + 1) * T],
                         src[:, cb * T:(cb + 1) * T],
                         src[:, cb * T:(cb + 1) * T],
                         reads=s_r, writes=[f"t1_fm:{cb}"])
                    for ch in range(NQC):
                        i = sct[0] % 3
                        sct[0] += 1
                        p.op("tensor", "matmul", sc_ps[i][0:1, :],
                             ones128b[:],
                             ovt[:, cb * T + ch * 512: cb * T + ch * 512 + 512],
                             start=True, stop=True,
                             reads=[ones128b, f"t1_fm:{cb}"], writes=[scn[i]])
                        p.op("vector", "reduce_max",
                             nmx[0:1, cb * 4 + ch: cb * 4 + ch + 1],
                             sc_ps[i][0:1, :], axis=AX.X,
                             reads=[scn[i]], writes=[nmx])
                p.op("vector", "reduce_max", nsq_[0:1, 0:8],
                     ap3(nmx, 0, [[4, 8], [1, 4]], nparts=1), axis=AX.X,
                     reads=[nmx], writes=[f"{nsq_.name}:m"])
                p.op("scalar", "activation", nsq_[0:1, 8:16], nsq_[0:1, 0:8],
                     AF.Sqrt, bias=0.0, scale=1.0,
                     reads=[f"{nsq_.name}:m"], writes=[nsq_])
            p.op("vector", "tensor_mul", brow[0:1, 0:8], nsq[0:1, 8:16],
                 nsk[0:1, 8:16], reads=[nsq, nsk], writes=[f"{brow.name}:p"])
            p.op("vector", "tensor_scalar", brow[0:1, 8:16], brow[0:1, 0:8],
                 -0.125, None, ALU.mult,
                 reads=[f"{brow.name}:p"], writes=[brow])
            p.op("tensor", "matmul", prj2[1][:, 0:8], ones1[:],
                 brow[0:1, 8:16], start=True, stop=True,
                 reads=[ones1, brow], writes=[prj2n[1]])
            p.op("scalar", "copy", bias_all[:], prj2[1][:, 0:8],
                 reads=[prj2n[1]],
                 writes=[f"bias_all:{cb}" for cb in range(KC)])

            def B_score_exp(h16, kc, qc, pb):
                hp = (h16 % 2) * 64
                hb = (h16 // 2) * T
                i = sct[0] % 3
                sct[0] += 1
                p.op("tensor", "matmul", sc_ps[i][:],
                     k_fm[hp:hp + 64, hb + kc * 128: hb + kc * 128 + 128],
                     q_fm[hp:hp + 64, hb + qc * 512: hb + qc * 512 + 512],
                     start=True, stop=True,
                     reads=q_r + k_r, writes=[scn[i]])
                p.op("scalar", "activation", p_bf[pb], sc_ps[i][:],
                     AF.Exp, bias=bias_all[:, h16 // 2: h16 // 2 + 1],
                     scale=0.125,
                     reads=[scn[i], f"bias_all:{h16 // 2}"],
                     writes=[p_bn[pb]])

            def B_ctx(h16, kc, qc, pb):
                p.op("tensor", "matmul", ctx_ps[qc % 2][:],
                     ap3(v_tm, kc * NH * 65 + h16 * 65, [[1, 65]]),
                     p_bf[pb],
                     start=(kc == 0), stop=(kc == NT - 1),
                     reads=[p_bn[pb], "v_tm2:ones"] + v_r,
                     writes=[ctx_pn[qc % 2]])

            def B_final(h16, qc):
                hp = (h16 % 2) * 64
                cp, cpn = ctx_ps[qc % 2], ctx_pn[qc % 2]
                rbt = rrb[qc % 2]
                p.op("vector", "reciprocal", rbt[0:1, :], cp[64:65, :],
                     reads=[cpn], writes=[f"{rbt.name}:rr"])
                p.op("tensor", "matmul", rb2_ps, ones1[:, 0:64],
                     rbt[0:1, :],
                     start=True, stop=True,
                     reads=[f"{rbt.name}:rr", ones1],
                     writes=[prj2n[0]])
                p.op("scalar", "copy", rbt[64:128, :], rb2_ps,
                     reads=[prj2n[0]], writes=[f"{rbt.name}:rb"])
                p.op("vector", "tensor_mul",
                     ovt[hp:hp + 64,
                         (h16 // 2) * T + qc * 512:
                         (h16 // 2) * T + qc * 512 + 512],
                     cp[0:64, :], rbt[64:128, :],
                     reads=[cpn, f"{rbt.name}:rb"],
                     writes=[f"ctxq:{h16}_{qc}", f"t1_fm:{h16 // 2}"])

            NU = NT * NQC
            for h16 in range(NH):
                pend = None
                for u in range(NU):
                    qc, kc = u // NT, u % NT
                    B_score_exp(h16, kc, qc, u % 2)
                    if pend is not None:
                        B_ctx(*pend)
                    pend = (h16, kc, qc, u % 2)
                    if kc == NT - 1:
                        B_ctx(*pend)
                        pend = None
                        B_final(h16, qc)

            for qc in range(NQC):
                ctx_r = [f"ctxq:{h}_{qc}" for h in range(NH)]
                for t4 in range(4):
                    tt = qc * 4 + t4
                    for half in range(2):
                        for h16 in range(NH):
                            pj = sc_ps[half] if h16 % 2 == 0 else prj2[half]
                            pjn = scn[half] if h16 % 2 == 0 else prj2n[half]
                            p.op("tensor", "matmul", pj[:],
                                 ovt[(h16 % 2) * 64:(h16 % 2) * 64 + 64,
                                     (h16 // 2) * T + qc * 512 + t4 * 128:
                                     (h16 // 2) * T + qc * 512 + t4 * 128
                                     + 128],
                                 wv2_bf[(h16 % 2) * 64:(h16 % 2) * 64 + 64,
                                        (h16 // 2) * D + half * 512:
                                        (h16 // 2) * D + half * 512 + 512],
                                 start=(h16 <= 1), stop=(h16 >= NH - 2),
                                 reads=ctx_r + wv2_r +
                                 [f"t1_fm:{h16 // 2}"],
                                 writes=[pjn])
                    p.dma("sync", fin_tl[:], fin_s[s, tt])
                    for half in range(2):
                        p.op("scalar", "copy", vtmp, prj2[half][:],
                             reads=[prj2n[half], VT], writes=[VT])
                        p.op("vector", "tensor_add", vtmp, vtmp,
                             bo2b[:, half * 512: half * 512 + 512],
                             reads=[VT, bo2b], writes=[VT])
                        p.op("vector", "tensor_add",
                             ao2[:, half * 512: half * 512 + 512],
                             sc_ps[half][:], vtmp,
                             reads=[scn[half], VT], writes=[f"ao2:{half}"])
                    ht = h_t[tt % 2]
                    p.op("vector", "tensor_add", ht[:], ao2[:], fin_tl[:],
                         reads=["ao2:0", "ao2:1", fin_tl], writes=[ht])
                    p.dma("scalar", h_s[s, tt], ht[:])
                    ln_plain(p, cfg, ht[:], xh2[:], [lnt, lnt2x][tt % 2])
                    p.op("vector", "tensor_mul", ao2[:], xh2[:], l2w[:],
                         reads=[xh2, l2w], writes=["ao2:0", "ao2:1"])
                    p.op("vector", "tensor_add", t2_tm[:], ao2[:], l2b[:],
                         reads=["ao2:0", "ao2:1", l2b], writes=[t2_tm])
                    for kc in range(KC):
                        i = kc % 2
                        p.op("tensor", "transpose", pT[i],
                             t2_tm[:, kc * 128:(kc + 1) * 128], eye_sb[:],
                             reads=[t2_tm, eye_sb], writes=[pTn[i]])
                        p.op("vector", "tensor_copy",
                             t2f_tt[:, kc * 128:(kc + 1) * 128],
                             pT[i], reads=[pTn[i]],
                             writes=[f"t2f_tt:{kc}"])
                    p.dma("scalar",
                          t2f_s[s, :, :, tt * 128:(tt + 1) * 128]
                          .rearrange("k p t -> p k t"),
                          ap3(t2f_tt, 0, [[128, KC], [1, 128]]),
                          reads=[f"t2f_tt:{kc}" for kc in range(KC)],
                          writes=[f"t2f_s:{s}_{tt}"])


def phase3(p, cfg, nc, g):
    D, E, HID, NHC = cfg.D, cfg.E, cfg.HID, cfg.NHC
    T, NSEQ, KC, NT, NQC = cfg.T, cfg.NSEQ, cfg.KC, cfg.NT, cfg.NQC
    w_gate, w1, b1_cols, w2 = (g["w_gate"], g["w1"], g["b1_cols"], g["w2"])
    b2pad, row0 = g["b2pad"], g["row0"]
    x1_s, h_s, t2f_s, y = g["x1_s"], g["h_s"], g["t2f_s"], g["y"]

    GH = min(4, NHC)
    NG = NHC // GH

    with contextlib.ExitStack() as S:
        def sb(n, s, d=F32):
            return S.enter_context(nc.sbuf_tensor(n, s, d))

        t2_fm = sb("t2_fm", [128, KC * T], BF16)
        acc = sb("acc", [128, NT * D])
        h1_sb = sb("h1_sb", [128, GH * T], BF16)
        w1blk = [sb(f"w1blk{i}", [128, KC * 128], BF16) for i in range(2)]
        w1st = [sb(f"w1st{i}", [128, KC * 128]) for i in range(2)]
        w2g = [sb(f"w2g{i}", [128, GH * D], BF16) for i in range(2)]
        w2st = [sb(f"w2st{i}", [128, D]) for i in range(2)]
        wg_st = sb("wg_st", [128, E * KC]); wg_bf = sb("wg_bf", [128, E * KC], BF16)
        b1c = sb("b1c", [128, E * NHC])
        b2bf = [sb(f"b2bf{i}", [128, D], BF16) for i in range(2)]
        b2stg = sb("b2stg", [128, D])
        row0_sb = sb("row0_sb", [128, 128], BF16)
        we_sb = sb("we_sb", [128, NT * 8])
        gl_all = sb("gl_all", [128, NT * 8])
        gwide = sb("gwide", [128, NT * 8]); gwide2 = sb("gwide2", [128, NT * 8])
        gmaskw = sb("gmaskw", [128, NT * 8])
        g16a = sb("g16a", [128, NT]); g16b = sb("g16b", [128, NT])
        g16c = sb("g16c", [128, NT])
        h_tl = [sb(f"h_tl{i}", [128, D]) for i in range(2)]
        x1_tl = [sb(f"x1_tl{i}", [128, D]) for i in range(2)]
        of_t = [sb(f"of_t{i}", [128, D]) for i in range(2)]
        lnof = [sb(f"lnof{i}", [128, D]) for i in range(2)]
        lnt = make_ln_tmp(nc, S, "p3ln")
        lnt3b = make_ln_tmp(nc, S, "p3lnB")

        hps = [S.enter_context(nc.psum_tensor(f"hps{i}", [128, 512], F32))
               for i in range(8)]
        hpn = [f"hps{i}" for i in range(8)]
        gl_ps = hps[0][:, 0:8]

        p.dma("sync", row0_sb[:], row0)
        p.dma("sync", ap3(wg_st, 0, [[E, KC], [1, E]]),
              w_gate.rearrange("(k p) e -> p k e", p=128), writes=[wg_st])
        p.op("vector", "tensor_copy", wg_bf[:], wg_st[:],
             reads=[wg_st], writes=[wg_bf])
        p.dma("sync", b1c[:], b1_cols)

        t2_r = [f"t2_fm:{kc}" for kc in range(KC)]
        for s in range(NSEQ):
            for kc in range(KC):
                p.dma("sync", t2_fm[:, kc * T:(kc + 1) * T], t2f_s[s, kc],
                      writes=[f"t2_fm:{kc}"])

            W8 = NT * 8
            seg = lambda t_: ap3(t_, 0, [[8, NT], [1, 8]])
            bc8 = lambda t_: ap3(t_, 0, [[16 if False else NT, 128]]
                                 ) if False else bass.AP(
                                     t_, 0, [[NT, 128], [1, NT], [0, 8]])
            for tt in range(NT):
                for kc in range(KC):
                    p.op("tensor", "matmul", gl_ps,
                         t2_fm[:, kc * T + tt * 128: kc * T + tt * 128 + 128],
                         wg_bf[:, kc * E: kc * E + E],
                         start=(kc == 0), stop=(kc == KC - 1),
                         reads=t2_r + [wg_bf], writes=[hpn[0]])
                p.op("scalar", "copy", gl_all[:, tt * 8: tt * 8 + 8], gl_ps,
                     reads=[hpn[0]], writes=[f"gl_all:{tt}"])
            gar = [f"gl_all:{tt}" for tt in range(NT)]
            p.op("vector", "reduce_max", g16a[:], seg(gl_all), axis=AX.X,
                 reads=gar, writes=[g16a])
            p.op("vector", "tensor_sub", gwide[:], gl_all[:], bc8(g16a),
                 reads=gar + [g16a], writes=[gwide])
            p.op("scalar", "activation", gwide[:], gwide[:], AF.Exp,
                 bias=0.0, scale=1.0, reads=[gwide], writes=[gwide])
            p.op("vector", "reduce_sum", g16b[:], seg(gwide), axis=AX.X,
                 reads=[gwide], writes=[g16b])
            p.op("vector", "reciprocal", g16c[:], g16b[:],
                 reads=[g16b], writes=[g16c])
            p.op("vector", "tensor_mul", gwide[:], gwide[:], bc8(g16c),
                 reads=[gwide, g16c], writes=[gwide])
            p.op("vector", "reduce_max", g16a[:], seg(gwide), axis=AX.X,
                 reads=[gwide], writes=[g16a])
            p.op("vector", "tensor_scalar_add", gwide2[:], gwide[:], 2.0,
                 reads=[gwide], writes=[gwide2])
            p.op("vector", "tensor_tensor", gmaskw[:], gwide[:], bc8(g16a),
                 ALU.is_lt, reads=[gwide, g16a], writes=[gmaskw])
            p.op("vector", "tensor_mul", gmaskw[:], gwide2[:], gmaskw[:],
                 reads=[gwide2, gmaskw], writes=[gmaskw])
            p.op("vector", "reduce_max", g16b[:], seg(gmaskw), axis=AX.X,
                 reads=[gmaskw], writes=[g16b])
            p.op("vector", "tensor_tensor", gmaskw[:], gwide2[:], bc8(g16b),
                 ALU.is_ge, reads=[gwide2, g16b], writes=[gmaskw])
            p.op("vector", "tensor_mul", gwide[:], gwide[:], gmaskw[:],
                 reads=[gwide, gmaskw], writes=[gwide])
            p.op("vector", "reduce_sum", g16b[:], seg(gwide), axis=AX.X,
                 reads=[gwide], writes=[g16b])
            p.op("vector", "reciprocal", g16c[:], g16b[:],
                 reads=[g16b], writes=[g16c])
            p.op("vector", "tensor_mul", we_sb[:], gwide[:], bc8(g16c),
                 reads=[gwide, g16c],
                 writes=[f"we_sb:{tt}" for tt in range(NT)])

            for tt in range(NT):
                p.op("gpsimd", "memset", acc[:, tt * D:(tt + 1) * D], 0.0,
                     writes=[f"acc:{tt}"])

            fin_done = [0]

            def finalize(tt):
                fin_done[0] += 1
                ht = h_tl[tt % 2]; x1t = x1_tl[tt % 2]
                p.dma("sync", ht[:], h_s[s, tt])
                p.dma("sync", x1t[:], x1_s[s, tt])
                t = [lnt, lnt3b][tt % 2]
                p.op("vector", "tensor_add", of_t[tt % 2][:],
                     acc[:, tt * D:(tt + 1) * D], ht[:],
                     reads=[f"acc:{tt}", ht], writes=[of_t[tt % 2]])
                ln_plain(p, cfg, of_t[tt % 2][:], lnof[tt % 2][:], t)
                p.op("vector", "tensor_add", lnof[tt % 2][:],
                     lnof[tt % 2][:], x1t[:],
                     reads=[lnof[tt % 2], x1t], writes=[lnof[tt % 2]])
                p.dma("scalar", y[s, tt * 128:(tt + 1) * 128, :],
                      lnof[tt % 2][:], writes=[f"y:{s}_{tt}"])

            for e in range(E):
                b2b = b2bf[e % 2]
                p.dma("sync", b2stg[:], b2pad[e])
                p.op("vector", "tensor_copy", b2b[:], b2stg[:],
                     reads=[b2stg], writes=[b2b])
                for gi in range(NG):
                    h1_r = []
                    for hg in range(GH):
                        hc = gi * GH + hg
                        wb = w1blk[hc % 2]; wst = w1st[hc % 2]
                        p.dma("sync",
                              ap3(wst, 0, [[128, KC], [1, 128]]),
                              w1[e, hc].rearrange("(k p) c -> p k c", p=128),
                              writes=[wst])
                        p.op("vector", "tensor_copy", wb[:], wst[:],
                             reads=[wst], writes=[wb])
                        for kc in range(KC):
                            for sub in range(NQC):
                                j = (hg % 2) * 4 + (sub % 4)
                                p.op("tensor", "matmul", hps[j][:],
                                     wb[:, kc * 128: kc * 128 + 128],
                                     t2_fm[:, kc * T + sub * 512:
                                           kc * T + sub * 512 + 512],
                                     start=(kc == 0), stop=(kc == KC - 1),
                                     reads=t2_r + [wb],
                                     writes=[hpn[j]])
                        for sub in range(NQC):
                            j = (hg % 2) * 4 + (sub % 4)
                            p.op("scalar", "activation",
                                 h1_sb[:, hg * T + sub * 512:
                                       hg * T + sub * 512 + 512],
                                 hps[j][:], AF.Gelu_apprx_tanh,
                                 bias=b1c[:, e * NHC + hc: e * NHC + hc + 1],
                                 scale=1.0,
                                 reads=[hpn[j], b1c],
                                 writes=[f"h1_sb:{hg}_{sub}"])
                            h1_r.append(f"h1_sb:{hg}_{sub}")
                    wg2 = w2g[gi % 2]
                    w2_r = []
                    for hg in range(GH):
                        hc = gi * GH + hg
                        wst = w2st[hg % 2]
                        p.dma("sync", wst[:], w2[e, hc * 128:(hc + 1) * 128, :])
                        p.op("vector", "tensor_copy",
                             wg2[:, hg * D: hg * D + D], wst[:],
                             reads=[wst], writes=[f"{wg2.name}:{hg}"])
                        w2_r.append(f"{wg2.name}:{hg}")
                    for tt in range(NT):
                        for half in range(2):
                            j = 4 + (tt % 2) * 2 + half
                            ps = hps[j]
                            if gi == 0:
                                p.op("tensor", "matmul", ps[:], row0_sb[:],
                                     b2b[:, half * 512: half * 512 + 512],
                                     start=True, stop=False,
                                     reads=[row0_sb, b2b],
                                     writes=[hpn[j]])
                            for hg in range(GH):
                                p.op("tensor", "matmul", ps[:],
                                     h1_sb[:, hg * T + tt * 128:
                                           hg * T + tt * 128 + 128],
                                     wg2[:, hg * D + half * 512:
                                         hg * D + half * 512 + 512],
                                     start=(hg == 0 and gi != 0),
                                     stop=(hg == GH - 1),
                                     reads=h1_r + w2_r, writes=[hpn[j]])
                            p.op("vector", "scalar_tensor_tensor",
                                 acc[:, tt * D + half * 512:
                                     tt * D + half * 512 + 512],
                                 ps[:],
                                 we_sb[:, tt * 8 + e: tt * 8 + e + 1],
                                 acc[:, tt * D + half * 512:
                                     tt * D + half * 512 + 512],
                                 ALU.mult, ALU.add,
                                 reads=[hpn[j], f"we_sb:{tt}", f"acc:{tt}"],
                                 writes=[f"acc:{tt}"])
                        if e == E - 1 and gi == NG - 1:
                            finalize(tt)

            assert fin_done[0] == NT
            fin_done[0] = 0


def _b2pad(b2, E, D):
    z = np.zeros((E, 128, D), np.float32)
    z[:, 0, :] = np.asarray(b2, np.float32)
    return z


def _row0():
    import ml_dtypes
    z = np.zeros((128, 128), np.float32)
    z[0] = 1.0
    return z.astype(ml_dtypes.bfloat16)


def make_in_map(cfg, x0_core, W):
    import ml_dtypes
    D, E, NHC = cfg.D, cfg.E, cfg.NHC

    def f32(v):
        return np.ascontiguousarray(np.asarray(v, np.float32))

    def bcast(v):
        v = np.asarray(v, np.float32)
        return np.ascontiguousarray(np.broadcast_to(v[None, :], (128, v.shape[0])))

    def cols(v, n):
        return np.ascontiguousarray(np.asarray(v, np.float32).reshape(n, 128).T)

    m = {
        "x0": f32(x0_core),
        "Wq": f32(W["L0_attn_Wq"]), "Wk": f32(W["L0_attn_Wk"]),
        "Wv": f32(W["L0_attn_Wv"]), "Wo": f32(W["L0_attn_Wo"]),
        "Wf": f32(W["L0_attn_Wf"]),
        "bf_col": f32(np.asarray(W["L0_attn_bf"]).reshape(-1, 1)),
        "bf_bc": bcast(np.asarray(W["L0_attn_bf"])),
        "bo_bc": bcast(np.asarray(W["L0_attn_bo"])),
        "w_qkv": f32(W["L0_ffn_w_qkv"]),
        "bqkv_cols": np.concatenate(
            [cols(np.asarray(W["L0_ffn_b_qkv"])[0:D], 8),
             cols(np.asarray(W["L0_ffn_b_qkv"])[D:2 * D], 8)], axis=1),
        "bqkv_v_bc": bcast(np.asarray(W["L0_ffn_b_qkv"])[2 * D:3 * D]),
        "w_o2": f32(W["L0_ffn_w_o"]),
        "bo2_bc": bcast(np.asarray(W["L0_ffn_b_o"])),
        "ln1w_bc": bcast(np.asarray(W["L0_ffn_ln1_w"])),
        "ln1b_bc": bcast(np.asarray(W["L0_ffn_ln1_b"])),
        "ln2w_bc": bcast(np.asarray(W["L0_ffn_ln2_w"])),
        "ln2b_bc": bcast(np.asarray(W["L0_ffn_ln2_b"])),
        "w_gate": f32(W["L0_ffn_w_gate"]),
        "w1": np.ascontiguousarray(
            np.asarray(W["L0_ffn_w1"], np.float32)
            .reshape(E, D, NHC, 128).transpose(0, 2, 1, 3)),
        "b1_cols": np.ascontiguousarray(
            np.asarray(W["L0_ffn_b1"], np.float32).reshape(E, NHC, 128)
            .transpose(2, 0, 1).reshape(128, E * NHC)),
        "w2": f32(W["L0_ffn_w2"]),
        "b2pad": _b2pad(W["L0_ffn_b2"], E, D),
        "row0": _row0(),
        "eye": np.eye(128).astype(ml_dtypes.bfloat16),
    }
    return {k: np.ascontiguousarray(v) for k, v in m.items()}


def kernel(**inputs):
    cfg = Cfg()
    x = np.asarray(inputs["L0_attn_x"], np.float32)
    nc = build_program(cfg)
    in_maps = [make_in_map(cfg, x[2 * c: 2 * c + 2], inputs) for c in range(8)]
    res = run_bass_kernel_spmd(nc, in_maps, core_ids=list(range(8)))
    out = np.concatenate([r["y"] for r in res.results], axis=0)
    return np.ascontiguousarray(out.astype(np.float32))


# version 12
# speedup vs baseline: 1.0687x; 1.0520x over previous
import contextlib
import numpy as np
import concourse.bass as bass
import concourse.mybir as mybir
from concourse.bass_utils import run_bass_kernel_spmd

F32 = mybir.dt.float32
BF16 = mybir.dt.bfloat16
AF = mybir.ActivationFunctionType
ALU = mybir.AluOpType
AX = mybir.AxisListType

ENGINES = ("tensor", "vector", "scalar", "gpsimd", "sync")


def apname(a):
    if isinstance(a, str):
        return a
    t = getattr(a, "tensor", None)
    if t is not None:
        return t.name
    return a.name


class Prog:

    SEM_CAP = 24000

    def __init__(self, nc, n_dma_sems=24):
        self.nc = nc
        self.eng = {e: getattr(nc, e) for e in ENGINES}
        self.stack = contextlib.ExitStack()
        self.psem = {e: [self.stack.enter_context(nc.semaphore(f"prog_{e}0"))]
                     for e in ENGINES}
        self.gen = {e: 0 for e in ENGINES}
        self.base = {e: 0 for e in ENGINES}
        self.pcnt = {e: 0 for e in ENGINES}
        self.dsems = [self.stack.enter_context(nc.semaphore(f"dma{i}"))
                      for i in range(n_dma_sems)]
        self.dcnt = [0] * n_dma_sems
        self.drr = 0
        self.res = {}
        self.waited = {e: {} for e in ENGINES}

    def _sem(self, kind, key):
        if kind == "eng":
            e, gen = key
            return self.psem[e][gen]
        return self.dsems[key]

    def _etok(self, e):
        if self.pcnt[e] - self.base[e] > self.SEM_CAP:
            self.gen[e] += 1
            self.base[e] = self.pcnt[e] - 1
            self.psem[e].append(self.stack.enter_context(
                self.nc.semaphore(f"prog_{e}{self.gen[e]}")))
        return ("eng", (e, self.gen[e]), self.pcnt[e] - self.base[e])

    def _edges(self, e, reads, writes, extra=()):
        need = {}

        def add(tok):
            if tok is None:
                return
            kind, key, val = tok
            if kind == "eng" and key[0] == e == "tensor":
                return
            k = (kind, key)
            if val > need.get(k, 0):
                need[k] = val

        for r in reads:
            rec = self.res.get(apname(r))
            if rec:
                add(rec["w"])
        for w in writes:
            rec = self.res.get(apname(w))
            if rec:
                add(rec["w"])
                for t in rec["r"]:
                    add(t)
        for t in extra:
            add(t)
        out = []
        for (kind, key), val in need.items():
            if val > self.waited[e].get((kind, key), 0):
                out.append((kind, key, val))
        return out

    def _post(self, e, tok, reads, writes, edges):
        w = self.waited[e]
        for (kind, key, val) in edges:
            if val > w.get((kind, key), 0):
                w[(kind, key)] = val
        for x in writes:
            self.res[apname(x)] = {"w": tok, "r": []}
        for r in reads:
            rec = self.res.setdefault(apname(r), {"w": None, "r": []})
            rec["r"].append(tok)

    def op(self, e, method, *args, reads=(), writes=(), **kw):
        edges = self._edges(e, reads, writes)
        for (kind, key, val) in edges[:-1]:
            self.eng[e].wait_ge(self._sem(kind, key), val)
        inst = getattr(self.eng[e], method)(*args, **kw)
        if edges:
            kind, key, val = edges[-1]
            inst._wait_ge(self._sem(kind, key), val)
        self.pcnt[e] += 1
        tok = self._etok(e)
        inst.then_inc(self._sem(tok[0], tok[1]), 1)
        self._post(e, tok, reads, writes, edges)
        return inst

    def dma(self, e, out, in_, reads=(), writes=(), **kw):
        i = self.drr
        self.drr = (self.drr + 1) % len(self.dsems)
        extra = []
        if self.dcnt[i] > 0:
            extra.append(("dma", i, self.dcnt[i]))
        reads = list(reads)
        writes = list(writes)
        if not isinstance(in_, str) and getattr(in_, "tensor", None) is not None:
            reads.append(in_)
        if not isinstance(out, str) and getattr(out, "tensor", None) is not None:
            writes.append(out)
        edges = self._edges(e, reads, writes, extra)
        for (kind, key, val) in edges[:-1]:
            self.eng[e].wait_ge(self._sem(kind, key), val)
        inst = self.eng[e].dma_start(out=out, in_=in_, **kw)
        if edges:
            kind, key, val = edges[-1]
            inst._wait_ge(self._sem(kind, key), val)
        self.dcnt[i] += 16
        inst.then_inc(self.dsems[i], 16)
        w = self.waited[e]
        for (kind, key, val) in edges:
            if val > w.get((kind, key), 0):
                w[(kind, key)] = val
        tok = ("dma", i, self.dcnt[i])
        for x in writes:
            self.res[apname(x)] = {"w": tok, "r": []}
        for r in reads:
            rec = self.res.setdefault(apname(r), {"w": None, "r": []})
            rec["r"].append(tok)
        return inst

    def barrier(self):
        for e in ENGINES:
            w = self.waited[e]
            for f in ENGINES:
                if f == e:
                    continue
                key = (f, self.gen[f])
                val = self.pcnt[f] - self.base[f]
                if val > w.get(("eng", key), 0):
                    self.eng[e].wait_ge(self.psem[f][self.gen[f]], val)
                    w[("eng", key)] = val
            for i, c in enumerate(self.dcnt):
                if c > w.get(("dma", i), 0):
                    self.eng[e].wait_ge(self.dsems[i], c)
                    w[("dma", i)] = c

    def finish(self):
        self.barrier()
        self.stack.close()


def ap3(t, off, dims, nparts=128, part0=0):
    fs = int(t.shape[-1])
    return bass.AP(t, part0 * fs + off, [[fs, nparts]] + dims)


class Cfg:
    def __init__(self, T=2048, NSEQ=2, HID=4096, E=8):
        self.D = 1024
        self.NH = 16
        self.DH = 64
        self.F = 64
        self.T = T
        self.NSEQ = NSEQ
        self.HID = HID
        self.E = E
        self.KC = self.D // 128
        self.NT = T // 128
        self.NQC = T // 512
        self.NHC = HID // 128
        self.EPS = 1e-5
        assert T % 512 == 0


def make_ln_tmp(nc, st, name):
    t = {}
    for n, sh in (("st6", 12), ("mv", 2), ("lnv", 1), ("rstd", 1),
                  ("eps", 1), ("sc1", 1), ("sc2", 1), ("sc3", 1), ("sc4", 1)):
        t[n] = st.enter_context(nc.sbuf_tensor(f"{name}_{n}", [128, sh], F32))
    t["init"] = False
    return t


def ln_plain(p, cfg, x_ap, out_ap, t, x_res=None, out_res=None):
    v = "vector"
    D = cfg.D
    if x_res is None:
        xr = [x_ap]
    elif isinstance(x_res, (list, tuple)):
        xr = list(x_res)
    else:
        xr = [x_res]
    if not t["init"]:
        p.op(v, "memset", t["eps"][:], cfg.EPS, writes=[t["eps"]])
        t["init"] = True
    half = 512
    xa = x_ap
    p.op(v, "bn_stats", t["st6"][:, 0:6],
         bass.AP(xa.tensor, xa.offset, [xa.ap[0], [1, half]]),
         reads=xr, writes=[t["st6"]])
    p.op(v, "bn_stats", t["st6"][:, 6:12],
         bass.AP(xa.tensor, xa.offset + half, [xa.ap[0], [1, half]]),
         reads=xr, writes=[t["st6"]])
    p.op(v, "bn_aggr", t["mv"][:], t["st6"][:],
         reads=[t["st6"]], writes=[t["mv"]])
    p.op("scalar", "activation", t["lnv"][:], t["mv"][:, 1:2], AF.Ln,
         bias=t["eps"][:, 0:1], scale=1.0,
         reads=[t["mv"], t["eps"]], writes=[t["lnv"]])
    p.op("scalar", "activation", t["rstd"][:], t["lnv"][:], AF.Exp,
         bias=0.0, scale=-0.5, reads=[t["lnv"]], writes=[t["rstd"]])
    p.op(v, "tensor_scalar", out_ap, x_ap, t["mv"][:, 0:1], t["rstd"][:, 0:1],
         ALU.subtract, ALU.mult,
         reads=xr + [t["mv"], t["rstd"]],
         writes=[out_res if out_res else out_ap])


def load_w_dd(p, cfg, wbuf, stag, hbm_ap, cols):
    src = hbm_ap.rearrange("(k p) c -> k p c", p=128)
    wn = wbuf.name
    for kc in range(cfg.KC):
        s = stag[kc % 2]
        p.dma("sync", s[:, 0:cols], src[kc], writes=[s])
        p.op("vector", "tensor_copy", wbuf[:, kc * cols:(kc + 1) * cols],
             s[:, 0:cols], reads=[s], writes=[f"{wn}:{kc}"])
    return [f"{wn}:{kc}" for kc in range(cfg.KC)]


def build_program(cfg: Cfg):
    nc = bass.Bass("TRN2", target_bir_lowering=False)
    D, NH, DH, Fd = cfg.D, cfg.NH, cfg.DH, cfg.F
    T, NSEQ, KC, NT, NQC = cfg.T, cfg.NSEQ, cfg.KC, cfg.NT, cfg.NQC
    NHC, E, HID = cfg.NHC, cfg.E, cfg.HID

    def dt(n, s, d=F32, k="ExternalInput"):
        return nc.dram_tensor(n, s, d, kind=k).ap()

    x0 = dt("x0", [NSEQ, T, D])
    Wq = dt("Wq", [D, D]); Wk = dt("Wk", [D, D]); Wv = dt("Wv", [D, D])
    Wo = dt("Wo", [D, D])
    Wf = dt("Wf", [DH, Fd])
    bf_col = dt("bf_col", [Fd, 1])
    bf_bc = dt("bf_bc", [128, Fd])
    bo_bc = dt("bo_bc", [128, D])
    w_qkv = dt("w_qkv", [D, 3 * D])
    bqkv_cols = dt("bqkv_cols", [128, 16])
    bqkv_v_bc = dt("bqkv_v_bc", [128, D])
    w_o2 = dt("w_o2", [D, D])
    bo2_bc = dt("bo2_bc", [128, D])
    ln1w_bc = dt("ln1w_bc", [128, D]); ln1b_bc = dt("ln1b_bc", [128, D])
    ln2w_bc = dt("ln2w_bc", [128, D]); ln2b_bc = dt("ln2b_bc", [128, D])
    w_gate = dt("w_gate", [D, E])
    w1 = dt("w1", [E, NHC, D, 128])
    b1_cols = dt("b1_cols", [128, E * NHC])
    w2 = dt("w2", [E, HID, D])
    b2pad = dt("b2pad", [E, 128, D])
    row0 = dt("row0", [128, 128], BF16)
    eye = dt("eye", [128, 128], BF16)
    y = dt("y", [NSEQ, T, D], F32, "ExternalOutput")

    def sc(n, s, d):
        return nc.dram_tensor(n, s, d, kind="Internal").ap()

    x1_s = sc("x1_s", [NSEQ, NT, 128, D], F32)
    fin_s = sc("fin_s", [NSEQ, NT, 128, D], F32)
    h_s = sc("h_s", [NSEQ, NT, 128, D], F32)
    t1f_s = sc("t1f_s", [NSEQ, KC, 128, T], BF16)
    t2f_s = sc("t2f_s", [NSEQ, KC, 128, T], BF16)

    p = Prog(nc)
    phase1(p, cfg, nc, locals())
    p.barrier()
    phase2(p, cfg, nc, locals())
    p.barrier()
    phase3(p, cfg, nc, locals())
    p.finish()
    return nc


def phase1(p, cfg, nc, g):
    D, NH, DH, Fd = cfg.D, cfg.NH, cfg.DH, cfg.F
    T, NSEQ, KC, NT, NQC = cfg.T, cfg.NSEQ, cfg.KC, cfg.NT, cfg.NQC
    x0, Wq, Wk, Wv, Wo = g["x0"], g["Wq"], g["Wk"], g["Wv"], g["Wo"]
    Wf, bf_col, bf_bc, bo_bc = g["Wf"], g["bf_col"], g["bf_bc"], g["bo_bc"]
    ln1w_bc, ln1b_bc = g["ln1w_bc"], g["ln1b_bc"]
    eye, x1_s, fin_s, t1f_s = g["eye"], g["x1_s"], g["fin_s"], g["t1f_s"]

    with contextlib.ExitStack() as S:
        def sb(n, s, d=F32):
            return S.enter_context(nc.sbuf_tensor(n, s, d))

        eye_sb = sb("eye_sb", [128, 128], BF16)
        p.dma("sync", eye_sb[:], eye)
        bfb = sb("bfb", [128, Fd]); p.dma("sync", bfb[:], bf_bc)
        bfc = sb("bfc", [Fd, 1]); p.dma("sync", bfc[:], bf_col)
        bob = sb("bob", [128, D]); p.dma("sync", bob[:], bo_bc)
        l1w = sb("l1w", [128, D]); p.dma("sync", l1w[:], ln1w_bc)
        l1b = sb("l1b", [128, D]); p.dma("sync", l1b[:], ln1b_bc)
        wf_st = sb("wf_st", [DH, Fd]); p.dma("sync", wf_st[:], Wf)
        wf_sb = sb("wf_sb", [128, Fd], BF16)
        p.op("vector", "tensor_copy", wf_sb[0:64, :], wf_st[:],
             reads=[wf_st], writes=["wf_sb:0"])
        p.op("vector", "tensor_copy", wf_sb[64:128, :], wf_st[:],
             reads=[wf_st], writes=["wf_sb:1"])
        ones64 = sb("ones64", [1, Fd], F32)
        p.op("vector", "memset", ones64[:], 1.0, writes=[ones64])

        a_fm = sb("a_fm", [128, KC * T], BF16)
        qf_fm = sb("qf_fm", [128, (NH // 2) * 512], BF16)
        v_tm = sb("v_tm", [128, NT * NH * 65], BF16)
        kv_sb = sb("kv_sb", [128, NH * 65], BF16)
        outn = sb("outn", [128, (NH // 2) * 512], BF16)
        wbuf = sb("wbufA", [128, KC * D], BF16)
        wbufB = sb("wbufB", [128, KC * D], BF16)
        wstag = [sb("wstag0", [128, D])]
        wstag.append(wstag[0])
        x0t = [sb(f"x0t{i}", [128, D]) for i in range(2)]
        a_tm = [sb(f"a_tm{i}", [128, D], BF16) for i in range(2)]
        lnt = make_ln_tmp(nc, S, "p1ln")
        lnt2 = make_ln_tmp(nc, S, "p1lnB")
        nrA = [sb(f"nrA{i}", [128, 1024]) for i in range(2)]
        ksb = [sb(f"ksb{i}", [128, T], BF16) for i in range(2)]
        kf_f = [sb(f"kf_f{i}", [128, Fd]) for i in range(2)]
        kf_t = [sb(f"kf_t{i}", [128, Fd], BF16) for i in range(4)]

        ao_t = [sb(f"ao_t{i}", [128, D]) for i in range(2)]
        lnao = sb("lnao", [128, D])
        aotmp = sb("aotmp", [128, 512])
        x1_t = [sb(f"x1_t{i}", [128, D]) for i in range(2)]
        fin_t = [sb(f"fin_t{i}", [128, D]) for i in range(2)]
        xh_t = sb("xh_t", [128, D])
        t1_tm = [sb(f"t1_tm{i}", [128, D], BF16) for i in range(2)]
        t1f_buf = [sb(f"t1f_buf{i}", [128, KC * 128], BF16) for i in range(2)]

        prj = [S.enter_context(nc.psum_tensor(f"prj{i}", [128, 512], F32))
               for i in range(4)]
        kv_ps = [S.enter_context(nc.psum_tensor(f"kv_ps{i}", [Fd, 65], F32))
                 for i in range(2)]
        kv_pn = ["kv_ps0", "kv_ps1"]
        rbp = S.enter_context(nc.psum_tensor("rbp", [Fd, 512], F32))
        pTT = S.enter_context(nc.psum_tensor("pTT", [128, 256], BF16))
        pT = [pTT[:, 0:128], pTT[:, 128:256]]
        pTn = ["pTT", "pTT"]
        qf_ps = [prj[2][0:64, :], prj[3][0:64, :]]
        qf_pn = ["prj2", "prj3"]
        kf_ps = [prj[2][:, 0:64], prj[3][:, 0:64]]
        kf_pn = ["prj2", "prj3"]
        out_ps = [prj[2][0:65, :], prj[3][0:65, :]]
        out_pn = ["prj2", "prj3"]
        prjn = ["prj0", "prj1", "prj2", "prj3"]

        p.op("vector", "memset",
             ap3(v_tm, 64, [[NH * 65, NT], [65, NH], [1, 1]]),
             1.0, writes=["v_tm:ones"])

        afm_r = [f"a_fm:{kc}_{tt}" for kc in range(KC) for tt in range(NT)]
        wbuf_r = [f"wbufA:{kc}" for kc in range(KC)]
        wbufB_r = [f"wbufB:{kc}" for kc in range(KC)]

        for s in range(NSEQ):
            for tt in range(NT):
                xt = x0t[tt % 2]
                p.dma("sync", xt[:], x0[s, tt * 128:(tt + 1) * 128, :])
                at = a_tm[tt % 2]
                ln_plain(p, cfg, xt[:], at[:], lnt)
                for kc in range(KC):
                    i = kc % 2
                    p.op("tensor", "transpose", pT[i],
                         at[:, kc * 128:(kc + 1) * 128], eye_sb[:],
                         reads=[at, eye_sb], writes=[pTn[i]])
                    p.op("vector", "tensor_copy",
                         a_fm[:, kc * T + tt * 128: kc * T + tt * 128 + 128],
                         pT[i], reads=[pTn[i]], writes=[f"a_fm:{kc}_{tt}"])

            load_w_dd(p, cfg, wbuf, wstag, Wv, D)
            load_w_dd(p, cfg, wbufB, wstag, Wk, D)
            for lt in range(NT):
                alt_r = [f"a_fm:{kc}_{lt}" for kc in range(KC)]
                for half in range(2):
                    ps = prj[half]
                    for kc in range(KC):
                        p.op("tensor", "matmul", ps[:],
                             a_fm[:, kc * T + lt * 128: kc * T + lt * 128 + 128],
                             wbuf[:, kc * D + half * 512: kc * D + half * 512 + 512],
                             start=(kc == 0), stop=(kc == KC - 1),
                             reads=alt_r + wbuf_r, writes=[prjn[half]])
                    p.op("vector", "tensor_copy",
                         ap3(v_tm, lt * NH * 65 + half * 8 * 65,
                             [[65, 8], [1, 64]]),
                         ap3(ps, 0, [[64, 8], [1, 64]]),
                         reads=[prjn[half]], writes=[f"v_tm:{lt}_{half}"])
            v_r = [f"v_tm:{lt}_{h}" for lt in range(NT) for h in range(2)]

            load_w_dd(p, cfg, wbuf, wstag, Wq, D)

            def c_proj_units(dc):
                us = []
                ks = ksb[dc % 2]
                for qc in range(NQC):
                    aqc_r = [f"a_fm:{kc}_{qc * 4 + j}" for kc in range(KC)
                             for j in range(4)]
                    for kc in range(KC):
                        def um(dc=dc, qc=qc, kc=kc, aqc_r=aqc_r):
                            p.op("tensor", "matmul", prj[qc % 2][:],
                                 wbufB[:, kc * D + dc * 128:
                                       kc * D + dc * 128 + 128],
                                 a_fm[:, kc * T + qc * 512:
                                      kc * T + qc * 512 + 512],
                                 start=(kc == 0), stop=(kc == KC - 1),
                                 reads=aqc_r + wbufB_r, writes=[prjn[qc % 2]])
                        us.append(um)
                    def ue(dc=dc, qc=qc, ks=ks):
                        p.op("scalar", "copy",
                             ks[:, qc * 512:(qc + 1) * 512], prj[qc % 2][:],
                             reads=[prjn[qc % 2]],
                             writes=[f"{ks.name}:{qc}"])
                    us.append(ue)
                return us

            def c_kfkv_units(dc):
                us = []
                ks = ksb[dc % 2]
                ks_r = [f"{ks.name}:{qc}" for qc in range(NQC)]
                seqs = [(hh, lt) for hh in range(2) for lt in range(NT)]

                def kf_part(i, hh, lt):
                    j = i % 2
                    p.op("tensor", "matmul", kf_ps[j],
                         ks[hh * 64: hh * 64 + 64,
                            lt * 128: lt * 128 + 128],
                         wf_sb[hh * 64: hh * 64 + 64, :],
                         start=True, stop=True,
                         reads=ks_r + ["wf_sb:0", "wf_sb:1"],
                         writes=[kf_pn[j]])
                    p.op("vector", "tensor_add", kf_f[j][:], kf_ps[j],
                         bfb[:], reads=[kf_pn[j], bfb],
                         writes=[kf_f[j]])
                    p.op("vector", "tensor_relu", kf_t[i % 4][:],
                         kf_f[j][:], reads=[kf_f[j]],
                         writes=[kf_t[i % 4]])

                def kv_part(i, hh, lt):
                    h16 = dc * 2 + hh
                    p.op("tensor", "matmul",
                         kv_ps[h16 % 2][:], kf_t[i % 4][:],
                         ap3(v_tm, lt * NH * 65 + h16 * 65, [[1, 65]]),
                         start=(lt == 0), stop=(lt == NT - 1),
                         reads=[kf_t[i % 4], "v_tm:ones"] + v_r,
                         writes=[kv_pn[h16 % 2]])
                    if lt == NT - 1:
                        for hlf, wres in ((0, "kv_sb:0"), (64, "kv_sb:1")):
                            p.op("scalar", "copy",
                                 kv_sb[hlf:hlf + 64, h16 * 65: h16 * 65 + 65],
                                 kv_ps[h16 % 2][:],
                                 reads=[kv_pn[h16 % 2]],
                                 writes=[f"{wres}_{h16}"])

                for i, (hh, lt) in enumerate(seqs):
                    def u(i=i, hh=hh, lt=lt):
                        kf_part(i, hh, lt)
                        if i >= 2:
                            kv_part(i - 2, *seqs[i - 2])
                    us.append(u)
                def utail():
                    kv_part(len(seqs) - 2, *seqs[-2])
                    kv_part(len(seqs) - 1, *seqs[-1])
                us.append(utail)
                return us

            for u in c_proj_units(0):
                u()
            for dc in range(KC):
                kfu = c_kfkv_units(dc)
                nxt = c_proj_units(dc + 1) if dc + 1 < KC else []
                a = b = 0
                while a < len(kfu) or b < len(nxt):
                    if a < len(kfu):
                        kfu[a]()
                        a += 1
                    if b < len(nxt):
                        nxt[b]()
                        b += 1

            load_w_dd(p, cfg, wbufB, wstag, Wo, D)
            qf_r = [f"qf_fm:{h}" for h in range(NH)]
            for qc in range(NQC):
                for dc in range(KC):
                    ks = ksb[dc % 2]
                    ps = prj[dc % 2]
                    for kc in range(KC):
                        p.op("tensor", "matmul", ps[:],
                             wbuf[:, kc * D + dc * 128: kc * D + dc * 128 + 128],
                             a_fm[:, kc * T + qc * 512: kc * T + qc * 512 + 512],
                             start=(kc == 0), stop=(kc == KC - 1),
                             reads=afm_r + wbuf_r, writes=[prjn[dc % 2]])
                    p.op("scalar", "copy", ks[:, 0:512], ps[:],
                         reads=[prjn[dc % 2]], writes=[f"{ks.name}:0"])
                    for hh in range(2):
                        h16 = dc * 2 + hh
                        i = dc % 2
                        p.op("tensor", "matmul", qf_ps[i],
                             wf_sb[hh * 64: hh * 64 + 64, :],
                             ks[hh * 64: hh * 64 + 64, 0:512],
                             start=True, stop=True,
                             reads=[f"{ks.name}:0", "wf_sb:0", "wf_sb:1"],
                             writes=[qf_pn[i]])
                        p.op("scalar", "activation",
                             qf_fm[(h16 % 2) * 64:(h16 % 2) * 64 + 64,
                                   (h16 // 2) * 512:(h16 // 2) * 512 + 512],
                             qf_ps[i], AF.Relu, bias=bfc[:, 0:1], scale=1.0,
                             reads=[qf_pn[i], bfc], writes=[f"qf_fm:{h16}"])
                on_r = [f"outn:{h}" for h in range(NH)]
                for h16 in range(NH):
                    i = h16 % 2
                    p.op("tensor", "matmul", out_ps[i],
                         kv_sb[(h16 % 2) * 64:(h16 % 2) * 64 + 64,
                               h16 * 65: h16 * 65 + 65],
                         qf_fm[(h16 % 2) * 64:(h16 % 2) * 64 + 64,
                               (h16 // 2) * 512:(h16 // 2) * 512 + 512],
                         start=True, stop=True,
                         reads=[f"kv_sb:0_{h16}", f"kv_sb:1_{h16}"] + qf_r,
                         writes=[out_pn[i]])
                    nr = nrA[i]
                    p.op("vector", "tensor_scalar_max", nr[0:1, 0:512],
                         out_ps[i][64:65, :], 1e-6,
                         reads=[out_pn[i]],
                         writes=[f"{nr.name}:a", f"{nr.name}:c"])
                    p.op("scalar", "activation", nr[0:1, 512:1024],
                         nr[0:1, 0:512], AF.Ln, bias=0.0, scale=1.0,
                         reads=[f"{nr.name}:a"], writes=[f"{nr.name}:b"])
                    p.op("scalar", "activation", nr[0:1, 0:512],
                         nr[0:1, 512:1024], AF.Exp, bias=0.0, scale=-1.0,
                         reads=[f"{nr.name}:b"],
                         writes=[f"{nr.name}:c", f"{nr.name}:a"])
                    p.op("tensor", "matmul", rbp[:], ones64[:],
                         nr[0:1, 0:512],
                         start=True, stop=True, reads=[ones64, f"{nr.name}:c"],
                         writes=["rbp"])
                    p.op("scalar", "copy", nr[64:128, 0:512], rbp[:],
                         reads=["rbp"], writes=[f"{nr.name}:rb"])
                    p.op("vector", "tensor_mul",
                         outn[(h16 % 2) * 64:(h16 % 2) * 64 + 64,
                              (h16 // 2) * 512:(h16 // 2) * 512 + 512],
                         out_ps[i][0:64, :], nr[64:128, 0:512],
                         reads=[out_pn[i], f"{nr.name}:rb"],
                         writes=[f"outn:{h16}"])
                wbuf_r2 = wbufB_r

                def chain(t4):
                    tt = qc * 4 + t4
                    ao = ao_t[t4 % 2]
                    t = [lnt, lnt2][t4 % 2]
                    lna = [lnao, xh_t][t4 % 2]
                    t1t = t1_tm[t4 % 2]
                    ln_plain(p, cfg, ao[:], lna[:], t,
                             x_res=[f"{ao.name}:0", f"{ao.name}:1"])
                    xt = x0t[tt % 2]
                    p.dma("sync", xt[:], x0[s, tt * 128:(tt + 1) * 128, :])
                    x1t = x1_t[tt % 2]; fint = fin_t[tt % 2]
                    p.op("vector", "tensor_add", x1t[:], lna[:], xt[:],
                         reads=[lna, xt], writes=[x1t])
                    p.dma("scalar", x1_s[s, tt], x1t[:])
                    v = "vector"
                    p.op(v, "bn_stats", t["st6"][:, 0:6], x1t[:, 0:512],
                         reads=[x1t], writes=[t["st6"]])
                    p.op(v, "bn_stats", t["st6"][:, 6:12], x1t[:, 512:1024],
                         reads=[x1t], writes=[t["st6"]])
                    p.op(v, "bn_aggr", t["mv"][:], t["st6"][:],
                         reads=[t["st6"]], writes=[t["mv"]])
                    p.op("scalar", "activation", t["lnv"][:], t["mv"][:, 1:2],
                         AF.Ln, bias=t["eps"][:, 0:1], scale=1.0,
                         reads=[t["mv"], t["eps"]], writes=[t["lnv"]])
                    p.op("scalar", "activation", t["rstd"][:], t["lnv"][:],
                         AF.Exp, bias=0.0, scale=-0.5,
                         reads=[t["lnv"]], writes=[t["rstd"]])
                    p.op(v, "tensor_scalar", fint[:], x1t[:], t["mv"][:, 0:1],
                         t["rstd"][:, 0:1], ALU.subtract, ALU.mult,
                         reads=[x1t, t["mv"], t["rstd"]], writes=[fint])
                    p.dma("scalar", fin_s[s, tt], fint[:])
                    p.op("scalar", "activation", t["sc1"][:], t["lnv"][:],
                         AF.Exp, bias=0.0, scale=-1.0,
                         reads=[t["lnv"]], writes=[t["sc1"]])
                    p.op(v, "tensor_mul", t["sc2"][:], t["mv"][:, 1:2],
                         t["sc1"][:], reads=[t["mv"], t["sc1"]],
                         writes=[t["sc2"]])
                    p.op("scalar", "activation", t["sc3"][:], t["sc2"][:],
                         AF.Ln, bias=t["eps"][:, 0:1], scale=1.0,
                         reads=[t["sc2"], t["eps"]], writes=[t["sc3"]])
                    p.op("scalar", "activation", t["sc4"][:], t["sc3"][:],
                         AF.Exp, bias=0.0, scale=-0.5,
                         reads=[t["sc3"]], writes=[t["sc4"]])
                    p.op(v, "tensor_mul", t["sc1"][:], t["rstd"][:],
                         t["sc4"][:], reads=[t["rstd"], t["sc4"]],
                         writes=[t["sc1"]])
                    p.op(v, "tensor_scalar", lna[:], x1t[:], t["mv"][:, 0:1],
                         t["sc1"][:, 0:1], ALU.subtract, ALU.mult,
                         reads=[x1t, t["mv"], t["sc1"]], writes=[lna])
                    p.op(v, "tensor_mul", lna[:], lna[:], l1w[:],
                         reads=[lna, l1w], writes=[lna])
                    p.op(v, "tensor_add", t1t[:], lna[:], l1b[:],
                         reads=[lna, l1b], writes=[t1t])
                    tb = t1f_buf[t4 % 2]
                    for kc in range(KC):
                        i = kc % 2
                        p.op("tensor", "transpose", pT[i],
                             t1t[:, kc * 128:(kc + 1) * 128], eye_sb[:],
                             reads=[t1t, eye_sb], writes=[pTn[i]])
                        p.op("vector", "tensor_copy",
                             tb[:, kc * 128:(kc + 1) * 128],
                             pT[i], reads=[pTn[i]],
                             writes=[f"{tb.name}:{kc}"])
                    p.dma("scalar",
                          t1f_s[s, :, :, tt * 128:(tt + 1) * 128]
                          .rearrange("k p t -> p k t"),
                          ap3(tb, 0, [[128, KC], [1, 128]]),
                          reads=[f"{tb.name}:{kc}" for kc in range(KC)],
                          writes=[f"t1f_s:{s}_{tt}"])
                def attn_out(t4):
                    ao = ao_t[t4 % 2]
                    for half in range(2):
                        for h16 in range(NH):
                            pi = half + 2 * (h16 % 2)
                            p.op("tensor", "matmul", prj[pi][:],
                                 outn[(h16 % 2) * 64:(h16 % 2) * 64 + 64,
                                      (h16 // 2) * 512 + t4 * 128:
                                      (h16 // 2) * 512 + t4 * 128 + 128],
                                 wbufB[(h16 % 2) * 64:(h16 % 2) * 64 + 64,
                                       (h16 // 2) * D + half * 512:
                                       (h16 // 2) * D + half * 512 + 512],
                                 start=(h16 <= 1), stop=(h16 >= NH - 2),
                                 reads=on_r + wbuf_r2, writes=[prjn[pi]])
                        p.op("scalar", "copy", aotmp[:], prj[2 + half][:],
                             reads=[prjn[2 + half]], writes=[aotmp])
                        p.op("vector", "tensor_add", aotmp[:], aotmp[:],
                             bob[:, half * 512: half * 512 + 512],
                             reads=[aotmp, bob], writes=[aotmp])
                        p.op("vector", "tensor_add",
                             ao[:, half * 512: half * 512 + 512],
                             prj[half][:], aotmp[:],
                             reads=[prjn[half], aotmp],
                             writes=[f"{ao.name}:{half}"])

                attn_out(0)
                for t4 in range(4):
                    if t4 + 1 < 4:
                        attn_out(t4 + 1)
                    chain(t4)


def phase2(p, cfg, nc, g):
    D, NH, DH = cfg.D, cfg.NH, cfg.DH
    T, NSEQ, KC, NT, NQC = cfg.T, cfg.NSEQ, cfg.KC, cfg.NT, cfg.NQC
    w_qkv, bqkv_cols, bqkv_v_bc = g["w_qkv"], g["bqkv_cols"], g["bqkv_v_bc"]
    w_o2, bo2_bc = g["w_o2"], g["bo2_bc"]
    ln2w_bc, ln2b_bc = g["ln2w_bc"], g["ln2b_bc"]
    eye, fin_s, h_s, t1f_s, t2f_s = (g["eye"], g["fin_s"], g["h_s"],
                                     g["t1f_s"], g["t2f_s"])

    with contextlib.ExitStack() as S:
        def sb(n, s, d=F32):
            return S.enter_context(nc.sbuf_tensor(n, s, d))

        eye_sb = sb("eye_sb2", [128, 128], BF16)
        p.dma("sync", eye_sb[:], eye)
        bqc = sb("bqc", [128, 16]); p.dma("sync", bqc[:], bqkv_cols)
        bvb = sb("bvb", [128, D]); p.dma("sync", bvb[:], bqkv_v_bc)
        bo2b = sb("bo2b", [128, D]); p.dma("sync", bo2b[:], bo2_bc)
        l2w = sb("l2w", [128, D]); p.dma("sync", l2w[:], ln2w_bc)
        l2b = sb("l2b", [128, D]); p.dma("sync", l2b[:], ln2b_bc)
        ones1 = sb("ones1", [1, 128], F32)
        p.op("vector", "memset", ones1[:], 1.0, writes=[ones1])

        t1_fm = sb("t1_fm", [128, KC * T], BF16)
        q_fm = sb("q_fm", [128, KC * T], BF16)
        k_fm = sb("k_fm", [128, KC * T], BF16)
        v_tm = sb("v_tm2", [128, NT * NH * 65], BF16)
        ovt = t1_fm
        wv2_bf = sb("wv2_bf", [128, KC * D], BF16)
        wqk = [sb(f"wqk{i}", [128, KC * 128], BF16) for i in range(2)]
        wqst = [sb("wqst0", [128, KC * 128])]
        wqst.append(wqst[0])
        t2f_tt = sb("t2f_tt", [128, KC * 128], BF16)
        wstag = [sb("w2stag0", [128, D])]
        wstag.append(wstag[0])
        p_bf = [wqk[0][:, 0:512], wqk[0][:, 512:1024],
                wqk[1][:, 0:512], wqk[1][:, 512:1024]]
        p_bn = ["wqk0a", "wqk0b", "wqk1a", "wqk1b"]
        lnt = make_ln_tmp(nc, S, "p2ln")
        lnt2x = make_ln_tmp(nc, S, "p2lnB")
        p2sc = sb("p2sc", [128, 512])
        vtmp = p2sc[:, 0:512]
        VT = "p2sc"
        bias_all = sb("bias_all", [128, KC])
        ones128b = sb("ones128b", [128, 1], BF16)
        p.op("vector", "memset", ones128b[:], 1.0, writes=[ones128b])
        nmaxq = sb("nmaxq", [1, 32]); nmaxk = sb("nmaxk", [1, 32])
        nsq = sb("nsq", [1, 16]); nsk = sb("nsk", [1, 16])
        brow = sb("brow", [1, 16])
        rrb = [sb(f"rrb{i}", [128, 512]) for i in range(2)]
        ao2 = sb("ao2", [128, D]); h_t = [sb(f"h_t{i}", [128, D]) for i in range(2)]
        fin_tl = sb("fin_tl", [128, D])
        xh2 = sb("xh2", [128, D])
        t2_tm = sb("t2_tm", [128, D], BF16)

        sc_ps = [S.enter_context(nc.psum_tensor(f"sc_ps{i}", [128, 512], F32))
                 for i in range(3)]
        scn = [f"sc_ps{i}" for i in range(3)]
        ctx_ps = [S.enter_context(nc.psum_tensor(f"ctx_ps{i}", [65, 512], F32))
                  for i in range(2)]
        ctx_pn = ["ctx_ps0", "ctx_ps1"]
        prj2 = [S.enter_context(nc.psum_tensor(f"prj2_{i}", [128, 512], F32))
                for i in range(2)]
        prj2n = ["prj2_0", "prj2_1"]
        smallT = S.enter_context(nc.psum_tensor("smallT", [128, 256], BF16))
        pT = [smallT[:, 0:128], smallT[:, 128:256]]
        pTn = ["smallT", "smallT"]
        rb2_ps = prj2[0][0:64, :]

        p.op("vector", "memset",
             ap3(v_tm, 64, [[NH * 65, NT], [65, NH], [1, 1]]),
             1.0, writes=["v_tm2:ones"])

        t1_r = [f"t1_fm:{kc}" for kc in range(KC)]
        wv2_r = [f"wv2_bf:{kc}" for kc in range(KC)]
        q_r = [f"q_fm:{dc}_{qc}" for dc in range(KC) for qc in range(NQC)]
        k_r = [f"k_fm:{dc}_{qc}" for dc in range(KC) for qc in range(NQC)]
        v_r = [f"v_tm2:{lt}_{h}" for lt in range(NT) for h in range(2)]

        for s in range(NSEQ):
            for kc in range(KC):
                p.dma("sync", t1_fm[:, kc * T:(kc + 1) * T], t1f_s[s, kc],
                      writes=[f"t1_fm:{kc}"])
            for kc in range(KC):
                st_ = wstag[kc % 2]
                p.dma("sync", st_[:],
                      w_qkv.rearrange("(k p) c -> k p c", p=128)[kc, :,
                                                                2 * D:3 * D])
                p.op("vector", "tensor_copy",
                     wv2_bf[:, kc * D: kc * D + D],
                     st_[:], reads=[st_], writes=[f"wv2_bf:{kc}"])

            for qk in range(2):
                dst, dname = (q_fm, "q_fm") if qk == 0 else (k_fm, "k_fm")
                for dc in range(KC):
                    wq = wqk[dc % 2]; wqs = wqst[dc % 2]
                    p.dma("sync",
                          ap3(wqs, 0, [[128, KC], [1, 128]]),
                          w_qkv[:, qk * D + dc * 128: qk * D + dc * 128 + 128]
                          .rearrange("(k p) c -> p k c", p=128),
                          writes=[wqs])
                    p.op("vector", "tensor_copy", wq[:], wqs[:],
                         reads=[wqs], writes=[wq])
                    for qc in range(NQC):
                        ps = prj2[qc % 2]
                        for kc in range(KC):
                            p.op("tensor", "matmul", ps[:],
                                 wq[:, kc * 128: kc * 128 + 128],
                                 t1_fm[:, kc * T + qc * 512:
                                       kc * T + qc * 512 + 512],
                                 start=(kc == 0), stop=(kc == KC - 1),
                                 reads=t1_r + [wq], writes=[prj2n[qc % 2]])
                        p.op("scalar", "activation",
                             dst[:, dc * T + qc * 512: dc * T + qc * 512 + 512],
                             ps[:], AF.Identity,
                             bias=bqc[:, qk * 8 + dc: qk * 8 + dc + 1],
                             scale=1.0,
                             reads=[prj2n[qc % 2], bqc],
                             writes=[f"{dname}:{dc}_{qc}"])

            for lt in range(NT):
                for half in range(2):
                    ps = prj2[half]
                    for kc in range(KC):
                        p.op("tensor", "matmul", ps[:],
                             t1_fm[:, kc * T + lt * 128: kc * T + lt * 128 + 128],
                             wv2_bf[:, kc * D + half * 512:
                                    kc * D + half * 512 + 512],
                             start=(kc == 0), stop=(kc == KC - 1),
                             reads=t1_r + wv2_r, writes=[prj2n[half]])
                    p.op("vector", "tensor_add", vtmp, ps[:],
                         bvb[:, half * 512: half * 512 + 512],
                         reads=[prj2n[half], bvb, VT], writes=[VT])
                    p.op("vector", "tensor_copy",
                         ap3(v_tm, lt * NH * 65 + half * 8 * 65,
                             [[65, 8], [1, 64]]),
                         ap3(p2sc, 0, [[64, 8], [1, 64]]),
                         reads=[VT], writes=[f"v_tm2:{lt}_{half}"])

            load_w_dd(p, cfg, wv2_bf, wstag, w_o2, D)

            sct = [0]
            for src, s_r, nmx, nsq_ in ((q_fm, q_r, nmaxq, nsq),
                                        (k_fm, k_r, nmaxk, nsk)):
                for cb in range(KC):
                    p.op("vector", "tensor_mul", ovt[:, cb * T:(cb + 1) * T],
                         src[:, cb * T:(cb + 1) * T],
                         src[:, cb * T:(cb + 1) * T],
                         reads=s_r, writes=[f"t1_fm:{cb}"])
                    for ch in range(NQC):
                        i = sct[0] % 3
                        sct[0] += 1
                        p.op("tensor", "matmul", sc_ps[i][0:1, :],
                             ones128b[:],
                             ovt[:, cb * T + ch * 512: cb * T + ch * 512 + 512],
                             start=True, stop=True,
                             reads=[ones128b, f"t1_fm:{cb}"], writes=[scn[i]])
                        p.op("vector", "reduce_max",
                             nmx[0:1, cb * 4 + ch: cb * 4 + ch + 1],
                             sc_ps[i][0:1, :], axis=AX.X,
                             reads=[scn[i]], writes=[nmx])
                p.op("vector", "reduce_max", nsq_[0:1, 0:8],
                     ap3(nmx, 0, [[4, 8], [1, 4]], nparts=1), axis=AX.X,
                     reads=[nmx], writes=[f"{nsq_.name}:m"])
                p.op("scalar", "activation", nsq_[0:1, 8:16], nsq_[0:1, 0:8],
                     AF.Sqrt, bias=0.0, scale=1.0,
                     reads=[f"{nsq_.name}:m"], writes=[nsq_])
            p.op("vector", "tensor_mul", brow[0:1, 0:8], nsq[0:1, 8:16],
                 nsk[0:1, 8:16], reads=[nsq, nsk], writes=[f"{brow.name}:p"])
            p.op("vector", "tensor_scalar", brow[0:1, 8:16], brow[0:1, 0:8],
                 -0.125, None, ALU.mult,
                 reads=[f"{brow.name}:p"], writes=[brow])
            p.op("tensor", "matmul", prj2[1][:, 0:8], ones1[:],
                 brow[0:1, 8:16], start=True, stop=True,
                 reads=[ones1, brow], writes=[prj2n[1]])
            p.op("scalar", "copy", bias_all[:], prj2[1][:, 0:8],
                 reads=[prj2n[1]],
                 writes=[f"bias_all:{cb}" for cb in range(KC)])

            pb_init = [False] * 4

            def B_score_exp(h16, kc, qc, pb):
                hp = (h16 % 2) * 64
                hb = (h16 // 2) * T
                i = sct[0] % 3
                sct[0] += 1
                p.op("tensor", "matmul", sc_ps[i][:],
                     k_fm[hp:hp + 64, hb + kc * 128: hb + kc * 128 + 128],
                     q_fm[hp:hp + 64, hb + qc * 512: hb + qc * 512 + 512],
                     start=True, stop=True,
                     reads=q_r + k_r, writes=[scn[i]])
                wl = [p_bn[pb]]
                if not pb_init[pb]:
                    pb_init[pb] = True
                    wl.append(f"wqk{pb // 2}")
                p.op("scalar", "activation", p_bf[pb], sc_ps[i][:],
                     AF.Exp, bias=bias_all[:, h16 // 2: h16 // 2 + 1],
                     scale=0.125,
                     reads=[scn[i], f"bias_all:{h16 // 2}"],
                     writes=wl)

            def B_ctx(h16, kc, qc, pb):
                p.op("tensor", "matmul", ctx_ps[qc % 2][:],
                     ap3(v_tm, kc * NH * 65 + h16 * 65, [[1, 65]]),
                     p_bf[pb],
                     start=(kc == 0), stop=(kc == NT - 1),
                     reads=[p_bn[pb], "v_tm2:ones"] + v_r,
                     writes=[ctx_pn[qc % 2]])

            def B_final_a(h16, qc):
                cp, cpn = ctx_ps[qc % 2], ctx_pn[qc % 2]
                rbt = rrb[qc % 2]
                p.op("vector", "reciprocal", rbt[0:1, :], cp[64:65, :],
                     reads=[cpn], writes=[f"{rbt.name}:rr"])

            def B_final_b(h16, qc):
                hp = (h16 % 2) * 64
                cp, cpn = ctx_ps[qc % 2], ctx_pn[qc % 2]
                rbt = rrb[qc % 2]
                p.op("tensor", "matmul", rb2_ps, ones1[:, 0:64],
                     rbt[0:1, :],
                     start=True, stop=True,
                     reads=[f"{rbt.name}:rr", ones1],
                     writes=[prj2n[0]])
                p.op("scalar", "copy", rbt[64:128, :], rb2_ps,
                     reads=[prj2n[0]], writes=[f"{rbt.name}:rb"])
                p.op("vector", "tensor_mul",
                     ovt[hp:hp + 64,
                         (h16 // 2) * T + qc * 512:
                         (h16 // 2) * T + qc * 512 + 512],
                     cp[0:64, :], rbt[64:128, :],
                     reads=[cpn, f"{rbt.name}:rb"],
                     writes=[f"ctxq:{h16}_{qc}", f"t1_fm:{h16 // 2}"])

            NU = NT * NQC
            pend = []
            pendF = []
            gu = [0]

            def drain_one():
                h_, kc_, qc_, pb_ = pend.pop(0)
                B_ctx(h_, kc_, qc_, pb_)
                if kc_ == NT - 1:
                    B_final_a(h_, qc_)
                    pendF.append((gu[0], h_, qc_))

            for h16 in range(NH):
                for u in range(NU):
                    qc, kc = u // NT, u % NT
                    if pendF and gu[0] - pendF[0][0] >= 8:
                        B_final_b(*pendF.pop(0)[1:])
                    B_score_exp(h16, kc, qc, gu[0] % 4)
                    pend.append((h16, kc, qc, gu[0] % 4))
                    gu[0] += 1
                    if len(pend) > 3:
                        drain_one()
            while pend:
                drain_one()
            while pendF:
                B_final_b(*pendF.pop(0)[1:])

            for qc in range(NQC):
                ctx_r = [f"ctxq:{h}_{qc}" for h in range(NH)]
                for t4 in range(4):
                    tt = qc * 4 + t4
                    for half in range(2):
                        for h16 in range(NH):
                            pj = sc_ps[half] if h16 % 2 == 0 else prj2[half]
                            pjn = scn[half] if h16 % 2 == 0 else prj2n[half]
                            p.op("tensor", "matmul", pj[:],
                                 ovt[(h16 % 2) * 64:(h16 % 2) * 64 + 64,
                                     (h16 // 2) * T + qc * 512 + t4 * 128:
                                     (h16 // 2) * T + qc * 512 + t4 * 128
                                     + 128],
                                 wv2_bf[(h16 % 2) * 64:(h16 % 2) * 64 + 64,
                                        (h16 // 2) * D + half * 512:
                                        (h16 // 2) * D + half * 512 + 512],
                                 start=(h16 <= 1), stop=(h16 >= NH - 2),
                                 reads=ctx_r + wv2_r +
                                 [f"t1_fm:{h16 // 2}"],
                                 writes=[pjn])
                    p.dma("sync", fin_tl[:], fin_s[s, tt])
                    for half in range(2):
                        p.op("scalar", "copy", vtmp, prj2[half][:],
                             reads=[prj2n[half], VT], writes=[VT])
                        p.op("vector", "tensor_add", vtmp, vtmp,
                             bo2b[:, half * 512: half * 512 + 512],
                             reads=[VT, bo2b], writes=[VT])
                        p.op("vector", "tensor_add",
                             ao2[:, half * 512: half * 512 + 512],
                             sc_ps[half][:], vtmp,
                             reads=[scn[half], VT], writes=[f"ao2:{half}"])
                    ht = h_t[tt % 2]
                    p.op("vector", "tensor_add", ht[:], ao2[:], fin_tl[:],
                         reads=["ao2:0", "ao2:1", fin_tl], writes=[ht])
                    p.dma("scalar", h_s[s, tt], ht[:])
                    ln_plain(p, cfg, ht[:], xh2[:], [lnt, lnt2x][tt % 2])
                    p.op("vector", "tensor_mul", ao2[:], xh2[:], l2w[:],
                         reads=[xh2, l2w], writes=["ao2:0", "ao2:1"])
                    p.op("vector", "tensor_add", t2_tm[:], ao2[:], l2b[:],
                         reads=["ao2:0", "ao2:1", l2b], writes=[t2_tm])
                    for kc in range(KC):
                        i = kc % 2
                        p.op("tensor", "transpose", pT[i],
                             t2_tm[:, kc * 128:(kc + 1) * 128], eye_sb[:],
                             reads=[t2_tm, eye_sb], writes=[pTn[i]])
                        p.op("vector", "tensor_copy",
                             t2f_tt[:, kc * 128:(kc + 1) * 128],
                             pT[i], reads=[pTn[i]],
                             writes=[f"t2f_tt:{kc}"])
                    p.dma("scalar",
                          t2f_s[s, :, :, tt * 128:(tt + 1) * 128]
                          .rearrange("k p t -> p k t"),
                          ap3(t2f_tt, 0, [[128, KC], [1, 128]]),
                          reads=[f"t2f_tt:{kc}" for kc in range(KC)],
                          writes=[f"t2f_s:{s}_{tt}"])


def phase3(p, cfg, nc, g):
    D, E, HID, NHC = cfg.D, cfg.E, cfg.HID, cfg.NHC
    T, NSEQ, KC, NT, NQC = cfg.T, cfg.NSEQ, cfg.KC, cfg.NT, cfg.NQC
    w_gate, w1, b1_cols, w2 = (g["w_gate"], g["w1"], g["b1_cols"], g["w2"])
    b2pad, row0 = g["b2pad"], g["row0"]
    x1_s, h_s, t2f_s, y = g["x1_s"], g["h_s"], g["t2f_s"], g["y"]

    GH = min(4, NHC)
    NG = NHC // GH

    with contextlib.ExitStack() as S:
        def sb(n, s, d=F32):
            return S.enter_context(nc.sbuf_tensor(n, s, d))

        t2_fm = sb("t2_fm", [128, KC * T], BF16)
        acc = sb("acc", [128, NT * D])
        h1_sb = sb("h1_sb", [128, GH * T], BF16)
        w1blk = [sb(f"w1blk{i}", [128, KC * 128], BF16) for i in range(2)]
        w1st = [sb(f"w1st{i}", [128, KC * 128]) for i in range(2)]
        w2g = [sb(f"w2g{i}", [128, GH * D], BF16) for i in range(2)]
        w2st = [sb(f"w2st{i}", [128, D]) for i in range(2)]
        wg_st = sb("wg_st", [128, E * KC]); wg_bf = sb("wg_bf", [128, E * KC], BF16)
        b1c = sb("b1c", [128, E * NHC])
        b2bf = [sb(f"b2bf{i}", [128, D], BF16) for i in range(2)]
        b2stg = sb("b2stg", [128, D])
        row0_sb = sb("row0_sb", [128, 128], BF16)
        we_sb = sb("we_sb", [128, NT * 8])
        gl_all = sb("gl_all", [128, NT * 8])
        gwide = sb("gwide", [128, NT * 8]); gwide2 = sb("gwide2", [128, NT * 8])
        gmaskw = sb("gmaskw", [128, NT * 8])
        g16a = sb("g16a", [128, NT]); g16b = sb("g16b", [128, NT])
        g16c = sb("g16c", [128, NT])
        h_tl = [sb(f"h_tl{i}", [128, D]) for i in range(2)]
        x1_tl = [sb(f"x1_tl{i}", [128, D]) for i in range(2)]
        of_t = [sb(f"of_t{i}", [128, D]) for i in range(2)]
        lnof = [sb(f"lnof{i}", [128, D]) for i in range(2)]
        lnt = make_ln_tmp(nc, S, "p3ln")
        lnt3b = make_ln_tmp(nc, S, "p3lnB")

        hps = [S.enter_context(nc.psum_tensor(f"hps{i}", [128, 512], F32))
               for i in range(8)]
        hpn = [f"hps{i}" for i in range(8)]
        gl_ps = hps[0][:, 0:8]

        p.dma("sync", row0_sb[:], row0)
        p.dma("sync", ap3(wg_st, 0, [[E, KC], [1, E]]),
              w_gate.rearrange("(k p) e -> p k e", p=128), writes=[wg_st])
        p.op("vector", "tensor_copy", wg_bf[:], wg_st[:],
             reads=[wg_st], writes=[wg_bf])
        p.dma("sync", b1c[:], b1_cols)

        t2_r = [f"t2_fm:{kc}" for kc in range(KC)]
        for s in range(NSEQ):
            for kc in range(KC):
                p.dma("sync", t2_fm[:, kc * T:(kc + 1) * T], t2f_s[s, kc],
                      writes=[f"t2_fm:{kc}"])

            W8 = NT * 8
            seg = lambda t_: ap3(t_, 0, [[8, NT], [1, 8]])
            bc8 = lambda t_: ap3(t_, 0, [[16 if False else NT, 128]]
                                 ) if False else bass.AP(
                                     t_, 0, [[NT, 128], [1, NT], [0, 8]])
            for tt in range(NT):
                for kc in range(KC):
                    p.op("tensor", "matmul", gl_ps,
                         t2_fm[:, kc * T + tt * 128: kc * T + tt * 128 + 128],
                         wg_bf[:, kc * E: kc * E + E],
                         start=(kc == 0), stop=(kc == KC - 1),
                         reads=t2_r + [wg_bf], writes=[hpn[0]])
                p.op("scalar", "copy", gl_all[:, tt * 8: tt * 8 + 8], gl_ps,
                     reads=[hpn[0]], writes=[f"gl_all:{tt}"])
            gar = [f"gl_all:{tt}" for tt in range(NT)]
            p.op("vector", "reduce_max", g16a[:], seg(gl_all), axis=AX.X,
                 reads=gar, writes=[g16a])
            p.op("vector", "tensor_sub", gwide[:], gl_all[:], bc8(g16a),
                 reads=gar + [g16a], writes=[gwide])
            p.op("scalar", "activation", gwide[:], gwide[:], AF.Exp,
                 bias=0.0, scale=1.0, reads=[gwide], writes=[gwide])
            p.op("vector", "reduce_sum", g16b[:], seg(gwide), axis=AX.X,
                 reads=[gwide], writes=[g16b])
            p.op("vector", "reciprocal", g16c[:], g16b[:],
                 reads=[g16b], writes=[g16c])
            p.op("vector", "tensor_mul", gwide[:], gwide[:], bc8(g16c),
                 reads=[gwide, g16c], writes=[gwide])
            p.op("vector", "reduce_max", g16a[:], seg(gwide), axis=AX.X,
                 reads=[gwide], writes=[g16a])
            p.op("vector", "tensor_scalar_add", gwide2[:], gwide[:], 2.0,
                 reads=[gwide], writes=[gwide2])
            p.op("vector", "tensor_tensor", gmaskw[:], gwide[:], bc8(g16a),
                 ALU.is_lt, reads=[gwide, g16a], writes=[gmaskw])
            p.op("vector", "tensor_mul", gmaskw[:], gwide2[:], gmaskw[:],
                 reads=[gwide2, gmaskw], writes=[gmaskw])
            p.op("vector", "reduce_max", g16b[:], seg(gmaskw), axis=AX.X,
                 reads=[gmaskw], writes=[g16b])
            p.op("vector", "tensor_tensor", gmaskw[:], gwide2[:], bc8(g16b),
                 ALU.is_ge, reads=[gwide2, g16b], writes=[gmaskw])
            p.op("vector", "tensor_mul", gwide[:], gwide[:], gmaskw[:],
                 reads=[gwide, gmaskw], writes=[gwide])
            p.op("vector", "reduce_sum", g16b[:], seg(gwide), axis=AX.X,
                 reads=[gwide], writes=[g16b])
            p.op("vector", "reciprocal", g16c[:], g16b[:],
                 reads=[g16b], writes=[g16c])
            p.op("vector", "tensor_mul", we_sb[:], gwide[:], bc8(g16c),
                 reads=[gwide, g16c],
                 writes=[f"we_sb:{tt}" for tt in range(NT)])

            for tt in range(NT):
                p.op("gpsimd", "memset", acc[:, tt * D:(tt + 1) * D], 0.0,
                     writes=[f"acc:{tt}"])

            fin_done = [0]

            def finalize(tt):
                fin_done[0] += 1
                ht = h_tl[tt % 2]; x1t = x1_tl[tt % 2]
                p.dma("sync", ht[:], h_s[s, tt])
                p.dma("sync", x1t[:], x1_s[s, tt])
                t = [lnt, lnt3b][tt % 2]
                p.op("vector", "tensor_add", of_t[tt % 2][:],
                     acc[:, tt * D:(tt + 1) * D], ht[:],
                     reads=[f"acc:{tt}", ht], writes=[of_t[tt % 2]])
                ln_plain(p, cfg, of_t[tt % 2][:], lnof[tt % 2][:], t)
                p.op("vector", "tensor_add", lnof[tt % 2][:],
                     lnof[tt % 2][:], x1t[:],
                     reads=[lnof[tt % 2], x1t], writes=[lnof[tt % 2]])
                p.dma("scalar", y[s, tt * 128:(tt + 1) * 128, :],
                      lnof[tt % 2][:], writes=[f"y:{s}_{tt}"])

            for e in range(E):
                b2b = b2bf[e % 2]
                p.dma("sync", b2stg[:], b2pad[e])
                p.op("vector", "tensor_copy", b2b[:], b2stg[:],
                     reads=[b2stg], writes=[b2b])
                for gi in range(NG):
                    h1_r = []
                    for hg in range(GH):
                        hc = gi * GH + hg
                        wb = w1blk[hc % 2]; wst = w1st[hc % 2]
                        p.dma("sync",
                              ap3(wst, 0, [[128, KC], [1, 128]]),
                              w1[e, hc].rearrange("(k p) c -> p k c", p=128),
                              writes=[wst])
                        p.op("vector", "tensor_copy", wb[:], wst[:],
                             reads=[wst], writes=[wb])
                        for kc in range(KC):
                            for sub in range(NQC):
                                j = (hg % 2) * 4 + (sub % 4)
                                p.op("tensor", "matmul", hps[j][:],
                                     wb[:, kc * 128: kc * 128 + 128],
                                     t2_fm[:, kc * T + sub * 512:
                                           kc * T + sub * 512 + 512],
                                     start=(kc == 0), stop=(kc == KC - 1),
                                     reads=t2_r + [wb],
                                     writes=[hpn[j]])
                        for sub in range(NQC):
                            j = (hg % 2) * 4 + (sub % 4)
                            p.op("scalar", "activation",
                                 h1_sb[:, hg * T + sub * 512:
                                       hg * T + sub * 512 + 512],
                                 hps[j][:], AF.Gelu_apprx_tanh,
                                 bias=b1c[:, e * NHC + hc: e * NHC + hc + 1],
                                 scale=1.0,
                                 reads=[hpn[j], b1c],
                                 writes=[f"h1_sb:{hg}_{sub}"])
                            h1_r.append(f"h1_sb:{hg}_{sub}")
                    wg2 = w2g[gi % 2]
                    w2_r = []
                    for hg in range(GH):
                        hc = gi * GH + hg
                        wst = w2st[hg % 2]
                        p.dma("sync", wst[:], w2[e, hc * 128:(hc + 1) * 128, :])
                        p.op("vector", "tensor_copy",
                             wg2[:, hg * D: hg * D + D], wst[:],
                             reads=[wst], writes=[f"{wg2.name}:{hg}"])
                        w2_r.append(f"{wg2.name}:{hg}")
                    for tt in range(NT):
                        for half in range(2):
                            j = 4 + (tt % 2) * 2 + half
                            ps = hps[j]
                            if gi == 0:
                                p.op("tensor", "matmul", ps[:], row0_sb[:],
                                     b2b[:, half * 512: half * 512 + 512],
                                     start=True, stop=False,
                                     reads=[row0_sb, b2b],
                                     writes=[hpn[j]])
                            for hg in range(GH):
                                p.op("tensor", "matmul", ps[:],
                                     h1_sb[:, hg * T + tt * 128:
                                           hg * T + tt * 128 + 128],
                                     wg2[:, hg * D + half * 512:
                                         hg * D + half * 512 + 512],
                                     start=(hg == 0 and gi != 0),
                                     stop=(hg == GH - 1),
                                     reads=h1_r + w2_r, writes=[hpn[j]])
                            p.op("vector", "scalar_tensor_tensor",
                                 acc[:, tt * D + half * 512:
                                     tt * D + half * 512 + 512],
                                 ps[:],
                                 we_sb[:, tt * 8 + e: tt * 8 + e + 1],
                                 acc[:, tt * D + half * 512:
                                     tt * D + half * 512 + 512],
                                 ALU.mult, ALU.add,
                                 reads=[hpn[j], f"we_sb:{tt}", f"acc:{tt}"],
                                 writes=[f"acc:{tt}"])
                        if e == E - 1 and gi == NG - 1:
                            finalize(tt)

            assert fin_done[0] == NT
            fin_done[0] = 0


def _b2pad(b2, E, D):
    z = np.zeros((E, 128, D), np.float32)
    z[:, 0, :] = np.asarray(b2, np.float32)
    return z


def _row0():
    import ml_dtypes
    z = np.zeros((128, 128), np.float32)
    z[0] = 1.0
    return z.astype(ml_dtypes.bfloat16)


def make_in_map(cfg, x0_core, W):
    import ml_dtypes
    D, E, NHC = cfg.D, cfg.E, cfg.NHC

    def f32(v):
        return np.ascontiguousarray(np.asarray(v, np.float32))

    def bcast(v):
        v = np.asarray(v, np.float32)
        return np.ascontiguousarray(np.broadcast_to(v[None, :], (128, v.shape[0])))

    def cols(v, n):
        return np.ascontiguousarray(np.asarray(v, np.float32).reshape(n, 128).T)

    m = {
        "x0": f32(x0_core),
        "Wq": f32(W["L0_attn_Wq"]), "Wk": f32(W["L0_attn_Wk"]),
        "Wv": f32(W["L0_attn_Wv"]), "Wo": f32(W["L0_attn_Wo"]),
        "Wf": f32(W["L0_attn_Wf"]),
        "bf_col": f32(np.asarray(W["L0_attn_bf"]).reshape(-1, 1)),
        "bf_bc": bcast(np.asarray(W["L0_attn_bf"])),
        "bo_bc": bcast(np.asarray(W["L0_attn_bo"])),
        "w_qkv": f32(W["L0_ffn_w_qkv"]),
        "bqkv_cols": np.concatenate(
            [cols(np.asarray(W["L0_ffn_b_qkv"])[0:D], 8),
             cols(np.asarray(W["L0_ffn_b_qkv"])[D:2 * D], 8)], axis=1),
        "bqkv_v_bc": bcast(np.asarray(W["L0_ffn_b_qkv"])[2 * D:3 * D]),
        "w_o2": f32(W["L0_ffn_w_o"]),
        "bo2_bc": bcast(np.asarray(W["L0_ffn_b_o"])),
        "ln1w_bc": bcast(np.asarray(W["L0_ffn_ln1_w"])),
        "ln1b_bc": bcast(np.asarray(W["L0_ffn_ln1_b"])),
        "ln2w_bc": bcast(np.asarray(W["L0_ffn_ln2_w"])),
        "ln2b_bc": bcast(np.asarray(W["L0_ffn_ln2_b"])),
        "w_gate": f32(W["L0_ffn_w_gate"]),
        "w1": np.ascontiguousarray(
            np.asarray(W["L0_ffn_w1"], np.float32)
            .reshape(E, D, NHC, 128).transpose(0, 2, 1, 3)),
        "b1_cols": np.ascontiguousarray(
            np.asarray(W["L0_ffn_b1"], np.float32).reshape(E, NHC, 128)
            .transpose(2, 0, 1).reshape(128, E * NHC)),
        "w2": f32(W["L0_ffn_w2"]),
        "b2pad": _b2pad(W["L0_ffn_b2"], E, D),
        "row0": _row0(),
        "eye": np.eye(128).astype(ml_dtypes.bfloat16),
    }
    return {k: np.ascontiguousarray(v) for k, v in m.items()}


def kernel(**inputs):
    cfg = Cfg()
    x = np.asarray(inputs["L0_attn_x"], np.float32)
    nc = build_program(cfg)
    in_maps = [make_in_map(cfg, x[2 * c: 2 * c + 2], inputs) for c in range(8)]
    res = run_bass_kernel_spmd(nc, in_maps, core_ids=list(range(8)))
    out = np.concatenate([r["y"] for r in res.results], axis=0)
    return np.ascontiguousarray(out.astype(np.float32))


# version 13
# speedup vs baseline: 1.1934x; 1.1167x over previous
import contextlib
import numpy as np
import concourse.bass as bass
import concourse.mybir as mybir
from concourse.bass_utils import run_bass_kernel_spmd

F32 = mybir.dt.float32
BF16 = mybir.dt.bfloat16
AF = mybir.ActivationFunctionType
ALU = mybir.AluOpType
AX = mybir.AxisListType

ENGINES = ("tensor", "vector", "scalar", "gpsimd", "sync")


def apname(a):
    if isinstance(a, str):
        return a
    t = getattr(a, "tensor", None)
    if t is not None:
        return t.name
    return a.name


class Prog:

    SEM_CAP = 24000

    def __init__(self, nc, n_dma_sems=24):
        self.nc = nc
        self.eng = {e: getattr(nc, e) for e in ENGINES}
        self.stack = contextlib.ExitStack()
        self.psem = {e: [self.stack.enter_context(nc.semaphore(f"prog_{e}0"))]
                     for e in ENGINES}
        self.gen = {e: 0 for e in ENGINES}
        self.base = {e: 0 for e in ENGINES}
        self.pcnt = {e: 0 for e in ENGINES}
        self.dsems = [self.stack.enter_context(nc.semaphore(f"dma{i}"))
                      for i in range(n_dma_sems)]
        self.dcnt = [0] * n_dma_sems
        self.drr = 0
        self.res = {}
        self.waited = {e: {} for e in ENGINES}

    def _sem(self, kind, key):
        if kind == "eng":
            e, gen = key
            return self.psem[e][gen]
        return self.dsems[key]

    def _etok(self, e):
        if self.pcnt[e] - self.base[e] > self.SEM_CAP:
            self.gen[e] += 1
            self.base[e] = self.pcnt[e] - 1
            self.psem[e].append(self.stack.enter_context(
                self.nc.semaphore(f"prog_{e}{self.gen[e]}")))
        return ("eng", (e, self.gen[e]), self.pcnt[e] - self.base[e])

    def _edges(self, e, reads, writes, extra=()):
        need = {}

        def add(tok):
            if tok is None:
                return
            kind, key, val = tok
            if kind == "eng" and key[0] == e == "tensor":
                return
            k = (kind, key)
            if val > need.get(k, 0):
                need[k] = val

        for r in reads:
            rec = self.res.get(apname(r))
            if rec:
                add(rec["w"])
        for w in writes:
            rec = self.res.get(apname(w))
            if rec:
                add(rec["w"])
                for t in rec["r"]:
                    add(t)
        for t in extra:
            add(t)
        out = []
        for (kind, key), val in need.items():
            if val > self.waited[e].get((kind, key), 0):
                out.append((kind, key, val))
        return out

    def _post(self, e, tok, reads, writes, edges):
        w = self.waited[e]
        for (kind, key, val) in edges:
            if val > w.get((kind, key), 0):
                w[(kind, key)] = val
        for x in writes:
            self.res[apname(x)] = {"w": tok, "r": []}
        for r in reads:
            rec = self.res.setdefault(apname(r), {"w": None, "r": []})
            rec["r"].append(tok)

    def op(self, e, method, *args, reads=(), writes=(), **kw):
        edges = self._edges(e, reads, writes)
        for (kind, key, val) in edges[:-1]:
            self.eng[e].wait_ge(self._sem(kind, key), val)
        inst = getattr(self.eng[e], method)(*args, **kw)
        if edges:
            kind, key, val = edges[-1]
            inst._wait_ge(self._sem(kind, key), val)
        self.pcnt[e] += 1
        tok = self._etok(e)
        inst.then_inc(self._sem(tok[0], tok[1]), 1)
        self._post(e, tok, reads, writes, edges)
        return inst

    def dma(self, e, out, in_, reads=(), writes=(), **kw):
        i = self.drr
        self.drr = (self.drr + 1) % len(self.dsems)
        extra = []
        if self.dcnt[i] > 0:
            extra.append(("dma", i, self.dcnt[i]))
        reads = list(reads)
        writes = list(writes)
        if not isinstance(in_, str) and getattr(in_, "tensor", None) is not None:
            reads.append(in_)
        if not isinstance(out, str) and getattr(out, "tensor", None) is not None:
            writes.append(out)
        edges = self._edges(e, reads, writes, extra)
        for (kind, key, val) in edges[:-1]:
            self.eng[e].wait_ge(self._sem(kind, key), val)
        inst = self.eng[e].dma_start(out=out, in_=in_, **kw)
        if edges:
            kind, key, val = edges[-1]
            inst._wait_ge(self._sem(kind, key), val)
        self.dcnt[i] += 16
        inst.then_inc(self.dsems[i], 16)
        w = self.waited[e]
        for (kind, key, val) in edges:
            if val > w.get((kind, key), 0):
                w[(kind, key)] = val
        tok = ("dma", i, self.dcnt[i])
        for x in writes:
            self.res[apname(x)] = {"w": tok, "r": []}
        for r in reads:
            rec = self.res.setdefault(apname(r), {"w": None, "r": []})
            rec["r"].append(tok)
        return inst

    def barrier(self):
        for e in ENGINES:
            w = self.waited[e]
            for f in ENGINES:
                if f == e:
                    continue
                key = (f, self.gen[f])
                val = self.pcnt[f] - self.base[f]
                if val > w.get(("eng", key), 0):
                    self.eng[e].wait_ge(self.psem[f][self.gen[f]], val)
                    w[("eng", key)] = val
            for i, c in enumerate(self.dcnt):
                if c > w.get(("dma", i), 0):
                    self.eng[e].wait_ge(self.dsems[i], c)
                    w[("dma", i)] = c

    def finish(self):
        self.barrier()
        self.stack.close()


def ap3(t, off, dims, nparts=128, part0=0):
    fs = int(t.shape[-1])
    return bass.AP(t, part0 * fs + off, [[fs, nparts]] + dims)


class Cfg:
    def __init__(self, T=2048, NSEQ=2, HID=4096, E=8):
        self.D = 1024
        self.NH = 16
        self.DH = 64
        self.F = 64
        self.T = T
        self.NSEQ = NSEQ
        self.HID = HID
        self.E = E
        self.KC = self.D // 128
        self.NT = T // 128
        self.NQC = T // 512
        self.NHC = HID // 128
        self.EPS = 1e-5
        assert T % 512 == 0


def make_ln_tmp(nc, st, name):
    t = {}
    for n, sh in (("st6", 12), ("mv", 2), ("lnv", 1), ("rstd", 1),
                  ("eps", 1), ("sc1", 1), ("sc2", 1), ("sc3", 1), ("sc4", 1)):
        t[n] = st.enter_context(nc.sbuf_tensor(f"{name}_{n}", [128, sh], F32))
    t["init"] = False
    return t


def ln_plain(p, cfg, x_ap, out_ap, t, x_res=None, out_res=None):
    v = "vector"
    D = cfg.D
    if x_res is None:
        xr = [x_ap]
    elif isinstance(x_res, (list, tuple)):
        xr = list(x_res)
    else:
        xr = [x_res]
    if not t["init"]:
        p.op(v, "memset", t["eps"][:], cfg.EPS, writes=[t["eps"]])
        t["init"] = True
    half = 512
    xa = x_ap
    p.op(v, "bn_stats", t["st6"][:, 0:6],
         bass.AP(xa.tensor, xa.offset, [xa.ap[0], [1, half]]),
         reads=xr, writes=[t["st6"]])
    p.op(v, "bn_stats", t["st6"][:, 6:12],
         bass.AP(xa.tensor, xa.offset + half, [xa.ap[0], [1, half]]),
         reads=xr, writes=[t["st6"]])
    p.op(v, "bn_aggr", t["mv"][:], t["st6"][:],
         reads=[t["st6"]], writes=[t["mv"]])
    p.op("scalar", "activation", t["lnv"][:], t["mv"][:, 1:2], AF.Ln,
         bias=t["eps"][:, 0:1], scale=1.0,
         reads=[t["mv"], t["eps"]], writes=[t["lnv"]])
    p.op("scalar", "activation", t["rstd"][:], t["lnv"][:], AF.Exp,
         bias=0.0, scale=-0.5, reads=[t["lnv"]], writes=[t["rstd"]])
    p.op(v, "tensor_scalar", out_ap, x_ap, t["mv"][:, 0:1], t["rstd"][:, 0:1],
         ALU.subtract, ALU.mult,
         reads=xr + [t["mv"], t["rstd"]],
         writes=[out_res if out_res else out_ap])


def load_w_dd_units(p, cfg, wbuf, stag, hbm_ap, cols):
    src = hbm_ap.rearrange("(k p) c -> k p c", p=128)
    wn = wbuf.name

    def u(kc):
        s = stag[kc % 2]
        p.dma("sync", s[:, 0:cols], src[kc], writes=[s])
        p.op("vector", "tensor_copy", wbuf[:, kc * cols:(kc + 1) * cols],
             s[:, 0:cols], reads=[s], writes=[f"{wn}:{kc}"])
    return [lambda kc=kc: u(kc) for kc in range(cfg.KC)]


def load_w_dd(p, cfg, wbuf, stag, hbm_ap, cols):
    for u in load_w_dd_units(p, cfg, wbuf, stag, hbm_ap, cols):
        u()
    return [f"{wbuf.name}:{kc}" for kc in range(cfg.KC)]


def build_program(cfg: Cfg):
    nc = bass.Bass("TRN2", target_bir_lowering=False)
    D, NH, DH, Fd = cfg.D, cfg.NH, cfg.DH, cfg.F
    T, NSEQ, KC, NT, NQC = cfg.T, cfg.NSEQ, cfg.KC, cfg.NT, cfg.NQC
    NHC, E, HID = cfg.NHC, cfg.E, cfg.HID

    def dt(n, s, d=F32, k="ExternalInput"):
        return nc.dram_tensor(n, s, d, kind=k).ap()

    x0 = dt("x0", [NSEQ, T, D])
    Wq = dt("Wq", [D, D]); Wk = dt("Wk", [D, D]); Wv = dt("Wv", [D, D])
    Wo = dt("Wo", [D, D])
    Wf = dt("Wf", [DH, Fd])
    bf_col = dt("bf_col", [Fd, 1])
    bf_bc = dt("bf_bc", [128, Fd])
    bo_bc = dt("bo_bc", [128, D])
    w_qkv = dt("w_qkv", [D, 3 * D])
    bqkv_cols = dt("bqkv_cols", [128, 16])
    bqkv_v_bc = dt("bqkv_v_bc", [128, D])
    w_o2 = dt("w_o2", [D, D])
    bo2_bc = dt("bo2_bc", [128, D])
    ln1w_bc = dt("ln1w_bc", [128, D]); ln1b_bc = dt("ln1b_bc", [128, D])
    ln2w_bc = dt("ln2w_bc", [128, D]); ln2b_bc = dt("ln2b_bc", [128, D])
    w_gate = dt("w_gate", [D, E])
    w1 = dt("w1", [E, NHC, D, 128])
    b1_cols = dt("b1_cols", [128, E * NHC])
    w2 = dt("w2", [E, HID, D])
    b2pad = dt("b2pad", [E, 128, D])
    row0 = dt("row0", [128, 128], BF16)
    eye = dt("eye", [128, 128], BF16)
    y = dt("y", [NSEQ, T, D], F32, "ExternalOutput")

    def sc(n, s, d):
        return nc.dram_tensor(n, s, d, kind="Internal").ap()

    x1_s = sc("x1_s", [NSEQ, NT, 128, D], F32)
    fin_s = sc("fin_s", [NSEQ, NT, 128, D], F32)
    h_s = sc("h_s", [NSEQ, NT, 128, D], F32)
    t1f_s = sc("t1f_s", [NSEQ, KC, 128, T], BF16)
    t2f_s = sc("t2f_s", [NSEQ, KC, 128, T], BF16)

    p = Prog(nc)
    phase1(p, cfg, nc, locals())
    p.barrier()
    phase2(p, cfg, nc, locals())
    p.barrier()
    phase3(p, cfg, nc, locals())
    p.finish()
    return nc


def phase1(p, cfg, nc, g):
    D, NH, DH, Fd = cfg.D, cfg.NH, cfg.DH, cfg.F
    T, NSEQ, KC, NT, NQC = cfg.T, cfg.NSEQ, cfg.KC, cfg.NT, cfg.NQC
    x0, Wq, Wk, Wv, Wo = g["x0"], g["Wq"], g["Wk"], g["Wv"], g["Wo"]
    Wf, bf_col, bf_bc, bo_bc = g["Wf"], g["bf_col"], g["bf_bc"], g["bo_bc"]
    ln1w_bc, ln1b_bc = g["ln1w_bc"], g["ln1b_bc"]
    eye, x1_s, fin_s, t1f_s = g["eye"], g["x1_s"], g["fin_s"], g["t1f_s"]

    with contextlib.ExitStack() as S:
        def sb(n, s, d=F32):
            return S.enter_context(nc.sbuf_tensor(n, s, d))

        eye_sb = sb("eye_sb", [128, 128], BF16)
        p.dma("sync", eye_sb[:], eye)
        bfb = sb("bfb", [128, Fd]); p.dma("sync", bfb[:], bf_bc)
        bfc = sb("bfc", [Fd, 1]); p.dma("sync", bfc[:], bf_col)
        bob = sb("bob", [128, D]); p.dma("sync", bob[:], bo_bc)
        l1w = sb("l1w", [128, D]); p.dma("sync", l1w[:], ln1w_bc)
        l1b = sb("l1b", [128, D]); p.dma("sync", l1b[:], ln1b_bc)
        wf_st = sb("wf_st", [DH, Fd]); p.dma("sync", wf_st[:], Wf)
        wf_sb = sb("wf_sb", [128, Fd], BF16)
        p.op("vector", "tensor_copy", wf_sb[0:64, :], wf_st[:],
             reads=[wf_st], writes=["wf_sb:0"])
        p.op("vector", "tensor_copy", wf_sb[64:128, :], wf_st[:],
             reads=[wf_st], writes=["wf_sb:1"])
        ones64 = sb("ones64", [1, Fd], F32)
        p.op("vector", "memset", ones64[:], 1.0, writes=[ones64])

        a_fm = sb("a_fm", [128, KC * T], BF16)
        qf_fm = sb("qf_fm", [128, (NH // 2) * 512], BF16)
        v_tm = sb("v_tm", [128, NT * NH * 65], BF16)
        kv_sb = sb("kv_sb", [128, NH * 65], BF16)
        outn = sb("outn", [128, (NH // 2) * 512], BF16)
        wbuf = sb("wbufA", [128, KC * D], BF16)
        wbufB = sb("wbufB", [128, KC * D], BF16)
        wstag = [sb("wstag0", [128, D])]
        wstag.append(wstag[0])
        x0t = [sb(f"x0t{i}", [128, D]) for i in range(2)]
        a_tm = [sb(f"a_tm{i}", [128, D], BF16) for i in range(2)]
        lnt = make_ln_tmp(nc, S, "p1ln")
        lnt2 = make_ln_tmp(nc, S, "p1lnB")
        nrA = [sb(f"nrA{i}", [128, 1024]) for i in range(2)]
        ksb = [sb(f"ksb{i}", [128, T], BF16) for i in range(2)]
        kf_f = [sb(f"kf_f{i}", [128, Fd]) for i in range(2)]
        kf_t = [sb(f"kf_t{i}", [128, Fd], BF16) for i in range(4)]

        ao_t = [sb(f"ao_t{i}", [128, D]) for i in range(2)]
        lnao = sb("lnao", [128, D])
        aotmp = sb("aotmp", [128, 512])
        x1_t = [sb(f"x1_t{i}", [128, D]) for i in range(2)]
        fin_t = [sb(f"fin_t{i}", [128, D]) for i in range(2)]
        xh_t = sb("xh_t", [128, D])
        t1_tm = [sb(f"t1_tm{i}", [128, D], BF16) for i in range(4)]
        t1f_buf = [sb(f"t1f_buf{i}", [128, KC * 128], BF16) for i in range(2)]
        pend_t1 = []

        prj = [S.enter_context(nc.psum_tensor(f"prj{i}", [128, 512], F32))
               for i in range(4)]
        kv_ps = [S.enter_context(nc.psum_tensor(f"kv_ps{i}", [Fd, 65], F32))
                 for i in range(2)]
        kv_pn = ["kv_ps0", "kv_ps1"]
        rbp = S.enter_context(nc.psum_tensor("rbp", [Fd, 512], F32))
        pTT = S.enter_context(nc.psum_tensor("pTT", [128, 256], BF16))
        pT = [pTT[:, 0:128], pTT[:, 128:256]]
        pTn = ["pTT", "pTT"]
        qf_ps = [prj[2][0:64, :], prj[3][0:64, :]]
        qf_pn = ["prj2", "prj3"]
        kf_ps = [prj[2][:, 0:64], prj[3][:, 0:64]]
        kf_pn = ["prj2", "prj3"]
        out_ps = [prj[2][0:65, :], prj[3][0:65, :]]
        out_pn = ["prj2", "prj3"]
        prjn = ["prj0", "prj1", "prj2", "prj3"]

        p.op("vector", "memset",
             ap3(v_tm, 64, [[NH * 65, NT], [65, NH], [1, 1]]),
             1.0, writes=["v_tm:ones"])

        afm_r = [f"a_fm:{kc}_{tt}" for kc in range(KC) for tt in range(NT)]
        wbuf_r = [f"wbufA:{kc}" for kc in range(KC)]
        wbufB_r = [f"wbufB:{kc}" for kc in range(KC)]

        for s in range(NSEQ):
            wv_units = load_w_dd_units(p, cfg, wbuf, wstag, Wv, D)
            wk_units = load_w_dd_units(p, cfg, wbufB, wstag, Wk, D)

            def stage_a(tt):
                xt = x0t[tt % 2]
                p.dma(["sync", "gpsimd"][tt % 2], xt[:],
                      x0[s, tt * 128:(tt + 1) * 128, :])
                at = a_tm[tt % 2]
                ln_plain(p, cfg, xt[:], at[:], [lnt, lnt2][tt % 2])
                for kc in range(KC):
                    i = kc % 2
                    p.op("tensor", "transpose", pT[i],
                         at[:, kc * 128:(kc + 1) * 128], eye_sb[:],
                         reads=[at, eye_sb], writes=[pTn[i]])
                    p.op("vector", "tensor_copy",
                         a_fm[:, kc * T + tt * 128: kc * T + tt * 128 + 128],
                         pT[i], reads=[pTn[i]], writes=[f"a_fm:{kc}_{tt}"])

            def stage_b(lt):
                alt_r = [f"a_fm:{kc}_{lt}" for kc in range(KC)]
                for half in range(2):
                    ps = prj[half]
                    for kc in range(KC):
                        p.op("tensor", "matmul", ps[:],
                             a_fm[:, kc * T + lt * 128: kc * T + lt * 128 + 128],
                             wbuf[:, kc * D + half * 512: kc * D + half * 512 + 512],
                             start=(kc == 0), stop=(kc == KC - 1),
                             reads=alt_r + wbuf_r, writes=[prjn[half]])
                    p.op("vector", "tensor_copy",
                         ap3(v_tm, lt * NH * 65 + half * 8 * 65,
                             [[65, 8], [1, 64]]),
                         ap3(ps, 0, [[64, 8], [1, 64]]),
                         reads=[prjn[half]], writes=[f"v_tm:{lt}_{half}"])

            wunits = wv_units + wk_units
            for tt in range(NT + 4):
                if pend_t1:
                    pend_t1.pop(0)()
                if tt < 4:
                    wunits.pop(0)()
                    wunits.pop(0)()
                elif wunits:
                    wunits.pop(0)()
                if tt < NT:
                    stage_a(tt)
                if tt >= 4:
                    stage_b(tt - 4)
            v_r = [f"v_tm:{lt}_{h}" for lt in range(NT) for h in range(2)]

            load_w_dd(p, cfg, wbuf, wstag, Wq, D)

            def c_proj_units(dc):
                us = []
                ks = ksb[dc % 2]
                for qc in range(NQC):
                    aqc_r = [f"a_fm:{kc}_{qc * 4 + j}" for kc in range(KC)
                             for j in range(4)]
                    for kc in range(KC):
                        def um(dc=dc, qc=qc, kc=kc, aqc_r=aqc_r):
                            p.op("tensor", "matmul", prj[qc % 2][:],
                                 wbufB[:, kc * D + dc * 128:
                                       kc * D + dc * 128 + 128],
                                 a_fm[:, kc * T + qc * 512:
                                      kc * T + qc * 512 + 512],
                                 start=(kc == 0), stop=(kc == KC - 1),
                                 reads=aqc_r + wbufB_r, writes=[prjn[qc % 2]])
                        us.append(um)
                    def ue(dc=dc, qc=qc, ks=ks):
                        p.op("scalar", "copy",
                             ks[:, qc * 512:(qc + 1) * 512], prj[qc % 2][:],
                             reads=[prjn[qc % 2]],
                             writes=[f"{ks.name}:{qc}"])
                    us.append(ue)
                return us

            def c_kfkv_units(dc):
                us = []
                ks = ksb[dc % 2]
                ks_r = [f"{ks.name}:{qc}" for qc in range(NQC)]
                seqs = [(hh, lt) for hh in range(2) for lt in range(NT)]

                def kf_part(i, hh, lt):
                    j = i % 2
                    p.op("tensor", "matmul", kf_ps[j],
                         ks[hh * 64: hh * 64 + 64,
                            lt * 128: lt * 128 + 128],
                         wf_sb[hh * 64: hh * 64 + 64, :],
                         start=True, stop=True,
                         reads=ks_r + ["wf_sb:0", "wf_sb:1"],
                         writes=[kf_pn[j]])
                    p.op("vector", "tensor_add", kf_f[j][:], kf_ps[j],
                         bfb[:], reads=[kf_pn[j], bfb],
                         writes=[kf_f[j]])
                    p.op("vector", "tensor_relu", kf_t[i % 4][:],
                         kf_f[j][:], reads=[kf_f[j]],
                         writes=[kf_t[i % 4]])

                def kv_part(i, hh, lt):
                    h16 = dc * 2 + hh
                    p.op("tensor", "matmul",
                         kv_ps[h16 % 2][:], kf_t[i % 4][:],
                         ap3(v_tm, lt * NH * 65 + h16 * 65, [[1, 65]]),
                         start=(lt == 0), stop=(lt == NT - 1),
                         reads=[kf_t[i % 4], "v_tm:ones"] + v_r,
                         writes=[kv_pn[h16 % 2]])
                    if lt == NT - 1:
                        for hlf, wres in ((0, "kv_sb:0"), (64, "kv_sb:1")):
                            p.op("scalar", "copy",
                                 kv_sb[hlf:hlf + 64, h16 * 65: h16 * 65 + 65],
                                 kv_ps[h16 % 2][:],
                                 reads=[kv_pn[h16 % 2]],
                                 writes=[f"{wres}_{h16}"])

                for i, (hh, lt) in enumerate(seqs):
                    def u(i=i, hh=hh, lt=lt):
                        kf_part(i, hh, lt)
                        if i >= 2:
                            kv_part(i - 2, *seqs[i - 2])
                    us.append(u)
                def utail():
                    kv_part(len(seqs) - 2, *seqs[-2])
                    kv_part(len(seqs) - 1, *seqs[-1])
                us.append(utail)
                return us

            for u in c_proj_units(0):
                u()
            for dc in range(KC):
                kfu = c_kfkv_units(dc)
                nxt = c_proj_units(dc + 1) if dc + 1 < KC else []
                a = b = 0
                while a < len(kfu) or b < len(nxt):
                    if a < len(kfu):
                        kfu[a]()
                        a += 1
                    if b < len(nxt):
                        nxt[b]()
                        b += 1

            load_w_dd(p, cfg, wbufB, wstag, Wo, D)
            qf_r = [f"qf_fm:{h}" for h in range(NH)]
            for qc in range(NQC):
                for dc in range(KC):
                    if pend_t1:
                        pend_t1.pop(0)()
                    ks = ksb[dc % 2]
                    ps = prj[dc % 2]
                    for kc in range(KC):
                        p.op("tensor", "matmul", ps[:],
                             wbuf[:, kc * D + dc * 128: kc * D + dc * 128 + 128],
                             a_fm[:, kc * T + qc * 512: kc * T + qc * 512 + 512],
                             start=(kc == 0), stop=(kc == KC - 1),
                             reads=afm_r + wbuf_r, writes=[prjn[dc % 2]])
                    p.op("scalar", "copy", ks[:, 0:512], ps[:],
                         reads=[prjn[dc % 2]], writes=[f"{ks.name}:0"])
                    for hh in range(2):
                        h16 = dc * 2 + hh
                        i = dc % 2
                        p.op("tensor", "matmul", qf_ps[i],
                             wf_sb[hh * 64: hh * 64 + 64, :],
                             ks[hh * 64: hh * 64 + 64, 0:512],
                             start=True, stop=True,
                             reads=[f"{ks.name}:0", "wf_sb:0", "wf_sb:1"],
                             writes=[qf_pn[i]])
                        p.op("scalar", "activation",
                             qf_fm[(h16 % 2) * 64:(h16 % 2) * 64 + 64,
                                   (h16 // 2) * 512:(h16 // 2) * 512 + 512],
                             qf_ps[i], AF.Relu, bias=bfc[:, 0:1], scale=1.0,
                             reads=[qf_pn[i], bfc], writes=[f"qf_fm:{h16}"])
                on_r = [f"outn:{h}" for h in range(NH)]
                for h16 in range(NH):
                    i = h16 % 2
                    p.op("tensor", "matmul", out_ps[i],
                         kv_sb[(h16 % 2) * 64:(h16 % 2) * 64 + 64,
                               h16 * 65: h16 * 65 + 65],
                         qf_fm[(h16 % 2) * 64:(h16 % 2) * 64 + 64,
                               (h16 // 2) * 512:(h16 // 2) * 512 + 512],
                         start=True, stop=True,
                         reads=[f"kv_sb:0_{h16}", f"kv_sb:1_{h16}"] + qf_r,
                         writes=[out_pn[i]])
                    nr = nrA[i]
                    p.op("vector", "tensor_scalar_max", nr[0:1, 0:512],
                         out_ps[i][64:65, :], 1e-6,
                         reads=[out_pn[i]],
                         writes=[f"{nr.name}:a", f"{nr.name}:c"])
                    p.op("scalar", "activation", nr[0:1, 512:1024],
                         nr[0:1, 0:512], AF.Ln, bias=0.0, scale=1.0,
                         reads=[f"{nr.name}:a"], writes=[f"{nr.name}:b"])
                    p.op("scalar", "activation", nr[0:1, 0:512],
                         nr[0:1, 512:1024], AF.Exp, bias=0.0, scale=-1.0,
                         reads=[f"{nr.name}:b"],
                         writes=[f"{nr.name}:c", f"{nr.name}:a"])
                    p.op("tensor", "matmul", rbp[:], ones64[:],
                         nr[0:1, 0:512],
                         start=True, stop=True, reads=[ones64, f"{nr.name}:c"],
                         writes=["rbp"])
                    p.op("scalar", "copy", nr[64:128, 0:512], rbp[:],
                         reads=["rbp"], writes=[f"{nr.name}:rb"])
                    p.op("vector", "tensor_mul",
                         outn[(h16 % 2) * 64:(h16 % 2) * 64 + 64,
                              (h16 // 2) * 512:(h16 // 2) * 512 + 512],
                         out_ps[i][0:64, :], nr[64:128, 0:512],
                         reads=[out_pn[i], f"{nr.name}:rb"],
                         writes=[f"outn:{h16}"])
                wbuf_r2 = wbufB_r

                def chain(t4):
                    while len(pend_t1) >= 3:
                        pend_t1.pop(0)()
                    tt = qc * 4 + t4
                    ao = ao_t[t4 % 2]
                    t = [lnt, lnt2][t4 % 2]
                    lna = [lnao, xh_t][t4 % 2]
                    t1t = t1_tm[t4]
                    ln_plain(p, cfg, ao[:], lna[:], t,
                             x_res=[f"{ao.name}:0", f"{ao.name}:1"])
                    xt = x0t[tt % 2]
                    p.dma("sync", xt[:], x0[s, tt * 128:(tt + 1) * 128, :])
                    x1t = x1_t[tt % 2]; fint = fin_t[tt % 2]
                    p.op("vector", "tensor_add", x1t[:], lna[:], xt[:],
                         reads=[lna, xt], writes=[x1t])
                    p.dma("scalar", x1_s[s, tt], x1t[:])
                    v = "vector"
                    p.op(v, "bn_stats", t["st6"][:, 0:6], x1t[:, 0:512],
                         reads=[x1t], writes=[t["st6"]])
                    p.op(v, "bn_stats", t["st6"][:, 6:12], x1t[:, 512:1024],
                         reads=[x1t], writes=[t["st6"]])
                    p.op(v, "bn_aggr", t["mv"][:], t["st6"][:],
                         reads=[t["st6"]], writes=[t["mv"]])
                    p.op("scalar", "activation", t["lnv"][:], t["mv"][:, 1:2],
                         AF.Ln, bias=t["eps"][:, 0:1], scale=1.0,
                         reads=[t["mv"], t["eps"]], writes=[t["lnv"]])
                    p.op("scalar", "activation", t["rstd"][:], t["lnv"][:],
                         AF.Exp, bias=0.0, scale=-0.5,
                         reads=[t["lnv"]], writes=[t["rstd"]])
                    p.op(v, "tensor_scalar", fint[:], x1t[:], t["mv"][:, 0:1],
                         t["rstd"][:, 0:1], ALU.subtract, ALU.mult,
                         reads=[x1t, t["mv"], t["rstd"]], writes=[fint])
                    p.dma("scalar", fin_s[s, tt], fint[:])
                    p.op("scalar", "activation", t["sc1"][:], t["lnv"][:],
                         AF.Exp, bias=0.0, scale=-1.0,
                         reads=[t["lnv"]], writes=[t["sc1"]])
                    p.op(v, "tensor_mul", t["sc2"][:], t["mv"][:, 1:2],
                         t["sc1"][:], reads=[t["mv"], t["sc1"]],
                         writes=[t["sc2"]])
                    p.op("scalar", "activation", t["sc3"][:], t["sc2"][:],
                         AF.Ln, bias=t["eps"][:, 0:1], scale=1.0,
                         reads=[t["sc2"], t["eps"]], writes=[t["sc3"]])
                    p.op("scalar", "activation", t["sc4"][:], t["sc3"][:],
                         AF.Exp, bias=0.0, scale=-0.5,
                         reads=[t["sc3"]], writes=[t["sc4"]])
                    p.op(v, "tensor_mul", t["sc1"][:], t["rstd"][:],
                         t["sc4"][:], reads=[t["rstd"], t["sc4"]],
                         writes=[t["sc1"]])
                    p.op(v, "tensor_scalar", lna[:], x1t[:], t["mv"][:, 0:1],
                         t["sc1"][:, 0:1], ALU.subtract, ALU.mult,
                         reads=[x1t, t["mv"], t["sc1"]], writes=[lna])
                    p.op(v, "tensor_mul", lna[:], lna[:], l1w[:],
                         reads=[lna, l1w], writes=[lna])
                    p.op(v, "tensor_add", t1t[:], lna[:], l1b[:],
                         reads=[lna, l1b], writes=[t1t])

                    def spill_t1(s=s, tt=tt, t4=t4, t1t=t1t):
                        tb = t1f_buf[t4 % 2]
                        for kc in range(KC):
                            i = kc % 2
                            p.op("tensor", "transpose", pT[i],
                                 t1t[:, kc * 128:(kc + 1) * 128], eye_sb[:],
                                 reads=[t1t, eye_sb], writes=[pTn[i]])
                            p.op("vector", "tensor_copy",
                                 tb[:, kc * 128:(kc + 1) * 128],
                                 pT[i], reads=[pTn[i]],
                                 writes=[f"{tb.name}:{kc}"])
                        p.dma("gpsimd",
                              t1f_s[s, :, :, tt * 128:(tt + 1) * 128]
                              .rearrange("k p t -> p k t"),
                              ap3(tb, 0, [[128, KC], [1, 128]]),
                              reads=[f"{tb.name}:{kc}" for kc in range(KC)],
                              writes=[f"t1f_s:{s}_{tt}"])
                    pend_t1.append(spill_t1)
                def attn_out(t4):
                    ao = ao_t[t4 % 2]
                    for half in range(2):
                        for h16 in range(NH):
                            pi = half + 2 * (h16 % 2)
                            p.op("tensor", "matmul", prj[pi][:],
                                 outn[(h16 % 2) * 64:(h16 % 2) * 64 + 64,
                                      (h16 // 2) * 512 + t4 * 128:
                                      (h16 // 2) * 512 + t4 * 128 + 128],
                                 wbufB[(h16 % 2) * 64:(h16 % 2) * 64 + 64,
                                       (h16 // 2) * D + half * 512:
                                       (h16 // 2) * D + half * 512 + 512],
                                 start=(h16 <= 1), stop=(h16 >= NH - 2),
                                 reads=[f"outn:{h16}"] + wbuf_r2,
                                 writes=[prjn[pi]])
                        p.op("vector", "tensor_add", aotmp[:],
                             prj[2 + half][:],
                             bob[:, half * 512: half * 512 + 512],
                             reads=[prjn[2 + half], bob], writes=[aotmp])
                        p.op("vector", "tensor_add",
                             ao[:, half * 512: half * 512 + 512],
                             prj[half][:], aotmp[:],
                             reads=[prjn[half], aotmp],
                             writes=[f"{ao.name}:{half}"])

                attn_out(0)
                for t4 in range(4):
                    if t4 + 1 < 4:
                        attn_out(t4 + 1)
                    chain(t4)
        while pend_t1:
            pend_t1.pop(0)()


def phase2(p, cfg, nc, g):
    D, NH, DH = cfg.D, cfg.NH, cfg.DH
    T, NSEQ, KC, NT, NQC = cfg.T, cfg.NSEQ, cfg.KC, cfg.NT, cfg.NQC
    w_qkv, bqkv_cols, bqkv_v_bc = g["w_qkv"], g["bqkv_cols"], g["bqkv_v_bc"]
    w_o2, bo2_bc = g["w_o2"], g["bo2_bc"]
    ln2w_bc, ln2b_bc = g["ln2w_bc"], g["ln2b_bc"]
    eye, fin_s, h_s, t1f_s, t2f_s = (g["eye"], g["fin_s"], g["h_s"],
                                     g["t1f_s"], g["t2f_s"])

    with contextlib.ExitStack() as S:
        def sb(n, s, d=F32):
            return S.enter_context(nc.sbuf_tensor(n, s, d))

        eye_sb = sb("eye_sb2", [128, 128], BF16)
        p.dma("sync", eye_sb[:], eye)
        bqc = sb("bqc", [128, 16]); p.dma("sync", bqc[:], bqkv_cols)
        bvb = sb("bvb", [128, D]); p.dma("sync", bvb[:], bqkv_v_bc)
        bo2b = sb("bo2b", [128, D]); p.dma("sync", bo2b[:], bo2_bc)
        l2w = sb("l2w", [128, D]); p.dma("sync", l2w[:], ln2w_bc)
        l2b = sb("l2b", [128, D]); p.dma("sync", l2b[:], ln2b_bc)
        ones1 = sb("ones1", [1, 128], F32)
        p.op("vector", "memset", ones1[:], 1.0, writes=[ones1])

        t1_fm = sb("t1_fm", [128, KC * T], BF16)
        q_fm = sb("q_fm", [128, KC * T], BF16)
        k_fm = sb("k_fm", [128, KC * T], BF16)
        v_tm = sb("v_tm2", [128, NT * NH * 65], BF16)
        ovt = t1_fm
        wv2_bf = sb("wv2_bf", [128, KC * D], BF16)
        wqk = [sb(f"wqk{i}", [128, KC * 128], BF16) for i in range(2)]
        wqst = [sb("wqst0", [128, KC * 128])]
        wqst.append(wqst[0])
        t2f_tt = sb("t2f_tt", [128, KC * 128], BF16)
        wstag = [sb("w2stag0", [128, D])]
        wstag.append(wstag[0])
        p_bf = [wqk[0][:, 0:512], wqk[0][:, 512:1024],
                wqk[1][:, 0:512], wqk[1][:, 512:1024]]
        p_bn = ["wqk0a", "wqk0b", "wqk1a", "wqk1b"]
        lnt = make_ln_tmp(nc, S, "p2ln")
        lnt2x = make_ln_tmp(nc, S, "p2lnB")
        p2sc = sb("p2sc", [128, 512])
        vtmp = p2sc[:, 0:512]
        VT = "p2sc"
        bias_all = sb("bias_all", [128, KC])
        ones128b = sb("ones128b", [128, 1], BF16)
        p.op("vector", "memset", ones128b[:], 1.0, writes=[ones128b])
        nmaxq = sb("nmaxq", [1, 32]); nmaxk = sb("nmaxk", [1, 32])
        nsq = sb("nsq", [1, 16]); nsk = sb("nsk", [1, 16])
        brow = sb("brow", [1, 16])
        rrb = [sb(f"rrb{i}", [128, 512]) for i in range(2)]
        ao2 = sb("ao2", [128, D]); h_t = [sb(f"h_t{i}", [128, D]) for i in range(2)]
        fin_tl = sb("fin_tl", [128, D])
        xh2 = sb("xh2", [128, D])
        t2_tm = sb("t2_tm", [128, D], BF16)

        sc_ps = [S.enter_context(nc.psum_tensor(f"sc_ps{i}", [128, 512], F32))
                 for i in range(3)]
        scn = [f"sc_ps{i}" for i in range(3)]
        ctx_ps = [S.enter_context(nc.psum_tensor(f"ctx_ps{i}", [65, 512], F32))
                  for i in range(2)]
        ctx_pn = ["ctx_ps0", "ctx_ps1"]
        prj2 = [S.enter_context(nc.psum_tensor(f"prj2_{i}", [128, 512], F32))
                for i in range(2)]
        prj2n = ["prj2_0", "prj2_1"]
        smallT = S.enter_context(nc.psum_tensor("smallT", [128, 256], BF16))
        pT = [smallT[:, 0:128], smallT[:, 128:256]]
        pTn = ["smallT", "smallT"]
        rb2_ps = prj2[0][0:64, :]

        p.op("vector", "memset",
             ap3(v_tm, 64, [[NH * 65, NT], [65, NH], [1, 1]]),
             1.0, writes=["v_tm2:ones"])

        t1_r = [f"t1_fm:{kc}" for kc in range(KC)]
        wv2_r = [f"wv2_bf:{kc}" for kc in range(KC)]
        q_r = [f"q_fm:{dc}_{qc}" for dc in range(KC) for qc in range(NQC)]
        k_r = [f"k_fm:{dc}_{qc}" for dc in range(KC) for qc in range(NQC)]
        v_r = [f"v_tm2:{lt}_{h}" for lt in range(NT) for h in range(2)]

        for s in range(NSEQ):
            for kc in range(KC):
                p.dma(["sync", "gpsimd"][kc % 2],
                      t1_fm[:, kc * T:(kc + 1) * T], t1f_s[s, kc],
                      writes=[f"t1_fm:{kc}"])
            for kc in range(KC):
                st_ = wstag[kc % 2]
                p.dma(["sync", "gpsimd"][kc % 2], st_[:],
                      w_qkv.rearrange("(k p) c -> k p c", p=128)[kc, :,
                                                                2 * D:3 * D])
                p.op("vector", "tensor_copy",
                     wv2_bf[:, kc * D: kc * D + D],
                     st_[:], reads=[st_], writes=[f"wv2_bf:{kc}"])

            for qk in range(2):
                dst, dname = (q_fm, "q_fm") if qk == 0 else (k_fm, "k_fm")
                for dc in range(KC):
                    wq = wqk[dc % 2]; wqs = wqst[dc % 2]
                    p.dma("sync",
                          ap3(wqs, 0, [[128, KC], [1, 128]]),
                          w_qkv[:, qk * D + dc * 128: qk * D + dc * 128 + 128]
                          .rearrange("(k p) c -> p k c", p=128),
                          writes=[wqs])
                    p.op("vector", "tensor_copy", wq[:], wqs[:],
                         reads=[wqs], writes=[wq])
                    for qc in range(NQC):
                        ps = prj2[qc % 2]
                        for kc in range(KC):
                            p.op("tensor", "matmul", ps[:],
                                 wq[:, kc * 128: kc * 128 + 128],
                                 t1_fm[:, kc * T + qc * 512:
                                       kc * T + qc * 512 + 512],
                                 start=(kc == 0), stop=(kc == KC - 1),
                                 reads=t1_r + [wq], writes=[prj2n[qc % 2]])
                        p.op("scalar", "activation",
                             dst[:, dc * T + qc * 512: dc * T + qc * 512 + 512],
                             ps[:], AF.Identity,
                             bias=bqc[:, qk * 8 + dc: qk * 8 + dc + 1],
                             scale=1.0,
                             reads=[prj2n[qc % 2], bqc],
                             writes=[f"{dname}:{dc}_{qc}"])

            for lt in range(NT):
                for half in range(2):
                    ps = prj2[half]
                    for kc in range(KC):
                        p.op("tensor", "matmul", ps[:],
                             t1_fm[:, kc * T + lt * 128: kc * T + lt * 128 + 128],
                             wv2_bf[:, kc * D + half * 512:
                                    kc * D + half * 512 + 512],
                             start=(kc == 0), stop=(kc == KC - 1),
                             reads=t1_r + wv2_r, writes=[prj2n[half]])
                    p.op("vector", "tensor_add", vtmp, ps[:],
                         bvb[:, half * 512: half * 512 + 512],
                         reads=[prj2n[half], bvb, VT], writes=[VT])
                    p.op("vector", "tensor_copy",
                         ap3(v_tm, lt * NH * 65 + half * 8 * 65,
                             [[65, 8], [1, 64]]),
                         ap3(p2sc, 0, [[64, 8], [1, 64]]),
                         reads=[VT], writes=[f"v_tm2:{lt}_{half}"])

            load_w_dd(p, cfg, wv2_bf, wstag, w_o2, D)

            sct = [0]
            for src, s_r, nmx, nsq_ in ((q_fm, q_r, nmaxq, nsq),
                                        (k_fm, k_r, nmaxk, nsk)):
                for cb in range(KC):
                    p.op("vector", "tensor_mul", ovt[:, cb * T:(cb + 1) * T],
                         src[:, cb * T:(cb + 1) * T],
                         src[:, cb * T:(cb + 1) * T],
                         reads=s_r, writes=[f"t1_fm:{cb}"])
                    for ch in range(NQC):
                        i = sct[0] % 3
                        sct[0] += 1
                        p.op("tensor", "matmul", sc_ps[i][0:1, :],
                             ones128b[:],
                             ovt[:, cb * T + ch * 512: cb * T + ch * 512 + 512],
                             start=True, stop=True,
                             reads=[ones128b, f"t1_fm:{cb}"], writes=[scn[i]])
                        p.op("vector", "reduce_max",
                             nmx[0:1, cb * 4 + ch: cb * 4 + ch + 1],
                             sc_ps[i][0:1, :], axis=AX.X,
                             reads=[scn[i]], writes=[nmx])
                p.op("vector", "reduce_max", nsq_[0:1, 0:8],
                     ap3(nmx, 0, [[4, 8], [1, 4]], nparts=1), axis=AX.X,
                     reads=[nmx], writes=[f"{nsq_.name}:m"])
                p.op("scalar", "activation", nsq_[0:1, 8:16], nsq_[0:1, 0:8],
                     AF.Sqrt, bias=0.0, scale=1.0,
                     reads=[f"{nsq_.name}:m"], writes=[nsq_])
            p.op("vector", "tensor_mul", brow[0:1, 0:8], nsq[0:1, 8:16],
                 nsk[0:1, 8:16], reads=[nsq, nsk], writes=[f"{brow.name}:p"])
            p.op("vector", "tensor_scalar", brow[0:1, 8:16], brow[0:1, 0:8],
                 -0.125, None, ALU.mult,
                 reads=[f"{brow.name}:p"], writes=[brow])
            p.op("tensor", "matmul", prj2[1][:, 0:8], ones1[:],
                 brow[0:1, 8:16], start=True, stop=True,
                 reads=[ones1, brow], writes=[prj2n[1]])
            p.op("scalar", "copy", bias_all[:], prj2[1][:, 0:8],
                 reads=[prj2n[1]],
                 writes=[f"bias_all:{cb}" for cb in range(KC)])

            pb_init = [False] * 4

            def B_score_exp(h16, kc, qc, pb):
                hp = (h16 % 2) * 64
                hb = (h16 // 2) * T
                i = sct[0] % 3
                sct[0] += 1
                p.op("tensor", "matmul", sc_ps[i][:],
                     k_fm[hp:hp + 64, hb + kc * 128: hb + kc * 128 + 128],
                     q_fm[hp:hp + 64, hb + qc * 512: hb + qc * 512 + 512],
                     start=True, stop=True,
                     reads=q_r + k_r, writes=[scn[i]])
                wl = [p_bn[pb]]
                if not pb_init[pb]:
                    pb_init[pb] = True
                    wl.append(f"wqk{pb // 2}")
                p.op("scalar", "activation", p_bf[pb], sc_ps[i][:],
                     AF.Exp, bias=bias_all[:, h16 // 2: h16 // 2 + 1],
                     scale=0.125,
                     reads=[scn[i], f"bias_all:{h16 // 2}"],
                     writes=wl)

            def B_ctx(h16, kc, qc, pb):
                p.op("tensor", "matmul", ctx_ps[qc % 2][:],
                     ap3(v_tm, kc * NH * 65 + h16 * 65, [[1, 65]]),
                     p_bf[pb],
                     start=(kc == 0), stop=(kc == NT - 1),
                     reads=[p_bn[pb], "v_tm2:ones"] + v_r,
                     writes=[ctx_pn[qc % 2]])

            def B_final_a(h16, qc):
                cp, cpn = ctx_ps[qc % 2], ctx_pn[qc % 2]
                rbt = rrb[qc % 2]
                p.op("vector", "reciprocal", rbt[0:1, :], cp[64:65, :],
                     reads=[cpn], writes=[f"{rbt.name}:rr"])

            def B_final_b(h16, qc):
                hp = (h16 % 2) * 64
                cp, cpn = ctx_ps[qc % 2], ctx_pn[qc % 2]
                rbt = rrb[qc % 2]
                p.op("tensor", "matmul", rb2_ps, ones1[:, 0:64],
                     rbt[0:1, :],
                     start=True, stop=True,
                     reads=[f"{rbt.name}:rr", ones1],
                     writes=[prj2n[0]])
                p.op("scalar", "copy", rbt[64:128, :], rb2_ps,
                     reads=[prj2n[0]], writes=[f"{rbt.name}:rb"])
                p.op("vector", "tensor_mul",
                     ovt[hp:hp + 64,
                         (h16 // 2) * T + qc * 512:
                         (h16 // 2) * T + qc * 512 + 512],
                     cp[0:64, :], rbt[64:128, :],
                     reads=[cpn, f"{rbt.name}:rb"],
                     writes=[f"ctxq:{h16}_{qc}", f"t1_fm:{h16 // 2}"])

            NU = NT * NQC
            pend = []
            pendF = []
            gu = [0]

            def drain_one():
                h_, kc_, qc_, pb_ = pend.pop(0)
                B_ctx(h_, kc_, qc_, pb_)
                if kc_ == NT - 1:
                    B_final_a(h_, qc_)
                    pendF.append((gu[0], h_, qc_))

            for h16 in range(NH):
                for u in range(NU):
                    qc, kc = u // NT, u % NT
                    if pendF and gu[0] - pendF[0][0] >= 8:
                        B_final_b(*pendF.pop(0)[1:])
                    B_score_exp(h16, kc, qc, gu[0] % 4)
                    pend.append((h16, kc, qc, gu[0] % 4))
                    gu[0] += 1
                    if len(pend) > 3:
                        drain_one()
            while pend:
                drain_one()
            while pendF:
                B_final_b(*pendF.pop(0)[1:])

            for qc in range(NQC):
                ctx_r = [f"ctxq:{h}_{qc}" for h in range(NH)]
                for t4 in range(4):
                    tt = qc * 4 + t4
                    for half in range(2):
                        for h16 in range(NH):
                            pj = sc_ps[half] if h16 % 2 == 0 else prj2[half]
                            pjn = scn[half] if h16 % 2 == 0 else prj2n[half]
                            p.op("tensor", "matmul", pj[:],
                                 ovt[(h16 % 2) * 64:(h16 % 2) * 64 + 64,
                                     (h16 // 2) * T + qc * 512 + t4 * 128:
                                     (h16 // 2) * T + qc * 512 + t4 * 128
                                     + 128],
                                 wv2_bf[(h16 % 2) * 64:(h16 % 2) * 64 + 64,
                                        (h16 // 2) * D + half * 512:
                                        (h16 // 2) * D + half * 512 + 512],
                                 start=(h16 <= 1), stop=(h16 >= NH - 2),
                                 reads=ctx_r + wv2_r +
                                 [f"t1_fm:{h16 // 2}"],
                                 writes=[pjn])
                    p.dma("sync", fin_tl[:], fin_s[s, tt])
                    for half in range(2):
                        p.op("vector", "tensor_add", vtmp, prj2[half][:],
                             bo2b[:, half * 512: half * 512 + 512],
                             reads=[prj2n[half], VT, bo2b], writes=[VT])
                        p.op("vector", "tensor_add",
                             ao2[:, half * 512: half * 512 + 512],
                             sc_ps[half][:], vtmp,
                             reads=[scn[half], VT], writes=[f"ao2:{half}"])
                    ht = h_t[tt % 2]
                    p.op("vector", "tensor_add", ht[:], ao2[:], fin_tl[:],
                         reads=["ao2:0", "ao2:1", fin_tl], writes=[ht])
                    p.dma("scalar", h_s[s, tt], ht[:])
                    ln_plain(p, cfg, ht[:], xh2[:], [lnt, lnt2x][tt % 2])
                    p.op("vector", "tensor_mul", ao2[:], xh2[:], l2w[:],
                         reads=[xh2, l2w], writes=["ao2:0", "ao2:1"])
                    p.op("vector", "tensor_add", t2_tm[:], ao2[:], l2b[:],
                         reads=["ao2:0", "ao2:1", l2b], writes=[t2_tm])
                    for kc in range(KC):
                        i = kc % 2
                        p.op("tensor", "transpose", pT[i],
                             t2_tm[:, kc * 128:(kc + 1) * 128], eye_sb[:],
                             reads=[t2_tm, eye_sb], writes=[pTn[i]])
                        p.op("vector", "tensor_copy",
                             t2f_tt[:, kc * 128:(kc + 1) * 128],
                             pT[i], reads=[pTn[i]],
                             writes=[f"t2f_tt:{kc}"])
                    p.dma("scalar",
                          t2f_s[s, :, :, tt * 128:(tt + 1) * 128]
                          .rearrange("k p t -> p k t"),
                          ap3(t2f_tt, 0, [[128, KC], [1, 128]]),
                          reads=[f"t2f_tt:{kc}" for kc in range(KC)],
                          writes=[f"t2f_s:{s}_{tt}"])


def phase3(p, cfg, nc, g):
    D, E, HID, NHC = cfg.D, cfg.E, cfg.HID, cfg.NHC
    T, NSEQ, KC, NT, NQC = cfg.T, cfg.NSEQ, cfg.KC, cfg.NT, cfg.NQC
    w_gate, w1, b1_cols, w2 = (g["w_gate"], g["w1"], g["b1_cols"], g["w2"])
    b2pad, row0 = g["b2pad"], g["row0"]
    x1_s, h_s, t2f_s, y = g["x1_s"], g["h_s"], g["t2f_s"], g["y"]

    GH = min(4, NHC)
    NG = NHC // GH

    with contextlib.ExitStack() as S:
        def sb(n, s, d=F32):
            return S.enter_context(nc.sbuf_tensor(n, s, d))

        t2_fm = sb("t2_fm", [128, KC * T], BF16)
        acc = sb("acc", [128, NT * D])
        h1_sb = sb("h1_sb", [128, GH * T], BF16)
        w1blk = [sb(f"w1blk{i}", [128, KC * 128], BF16) for i in range(4)]
        w1st = [sb(f"w1st{i}", [128, KC * 128]) for i in range(2)]
        w2g = [sb(f"w2g{i}", [128, GH * D], BF16) for i in range(2)]
        w2st = [sb(f"w2st{i}", [128, D]) for i in range(2)]
        wg_st = sb("wg_st", [128, E * KC]); wg_bf = sb("wg_bf", [128, E * KC], BF16)
        b1c = sb("b1c", [128, E * NHC])
        b2bf = [sb(f"b2bf{i}", [128, D], BF16) for i in range(2)]
        b2stg = sb("b2stg", [128, D])
        row0_sb = sb("row0_sb", [128, 128], BF16)
        we_sb = sb("we_sb", [128, NT * 8])
        gl_all = sb("gl_all", [128, NT * 8])
        gwide = sb("gwide", [128, NT * 8]); gwide2 = sb("gwide2", [128, NT * 8])
        gmaskw = sb("gmaskw", [128, NT * 8])
        g16a = sb("g16a", [128, NT]); g16b = sb("g16b", [128, NT])
        g16c = sb("g16c", [128, NT])
        h_tl = [sb(f"h_tl{i}", [128, D]) for i in range(2)]
        x1_tl = [sb(f"x1_tl{i}", [128, D]) for i in range(2)]
        of_t = [sb(f"of_t{i}", [128, D]) for i in range(2)]
        lnof = [sb(f"lnof{i}", [128, D]) for i in range(2)]
        lnt = make_ln_tmp(nc, S, "p3ln")
        lnt3b = make_ln_tmp(nc, S, "p3lnB")

        hps = [S.enter_context(nc.psum_tensor(f"hps{i}", [128, 512], F32))
               for i in range(8)]
        hpn = [f"hps{i}" for i in range(8)]
        gl_ps = hps[0][:, 0:8]

        p.dma("sync", row0_sb[:], row0)
        p.dma("sync", ap3(wg_st, 0, [[E, KC], [1, E]]),
              w_gate.rearrange("(k p) e -> p k e", p=128), writes=[wg_st])
        p.op("vector", "tensor_copy", wg_bf[:], wg_st[:],
             reads=[wg_st], writes=[wg_bf])
        p.dma("sync", b1c[:], b1_cols)

        t2_r = [f"t2_fm:{kc}" for kc in range(KC)]
        for s in range(NSEQ):
            for kc in range(KC):
                p.dma(["sync", "gpsimd"][kc % 2],
                      t2_fm[:, kc * T:(kc + 1) * T], t2f_s[s, kc],
                      writes=[f"t2_fm:{kc}"])

            W8 = NT * 8
            seg = lambda t_: ap3(t_, 0, [[8, NT], [1, 8]])
            bc8 = lambda t_: ap3(t_, 0, [[16 if False else NT, 128]]
                                 ) if False else bass.AP(
                                     t_, 0, [[NT, 128], [1, NT], [0, 8]])
            for tt in range(NT):
                for kc in range(KC):
                    p.op("tensor", "matmul", gl_ps,
                         t2_fm[:, kc * T + tt * 128: kc * T + tt * 128 + 128],
                         wg_bf[:, kc * E: kc * E + E],
                         start=(kc == 0), stop=(kc == KC - 1),
                         reads=t2_r + [wg_bf], writes=[hpn[0]])
                p.op("scalar", "copy", gl_all[:, tt * 8: tt * 8 + 8], gl_ps,
                     reads=[hpn[0]], writes=[f"gl_all:{tt}"])
            gar = [f"gl_all:{tt}" for tt in range(NT)]
            p.op("vector", "reduce_max", g16a[:], seg(gl_all), axis=AX.X,
                 reads=gar, writes=[g16a])
            p.op("vector", "tensor_sub", gwide[:], gl_all[:], bc8(g16a),
                 reads=gar + [g16a], writes=[gwide])
            p.op("scalar", "activation", gwide[:], gwide[:], AF.Exp,
                 bias=0.0, scale=1.0, reads=[gwide], writes=[gwide])
            p.op("vector", "reduce_sum", g16b[:], seg(gwide), axis=AX.X,
                 reads=[gwide], writes=[g16b])
            p.op("vector", "reciprocal", g16c[:], g16b[:],
                 reads=[g16b], writes=[g16c])
            p.op("vector", "tensor_mul", gwide[:], gwide[:], bc8(g16c),
                 reads=[gwide, g16c], writes=[gwide])
            p.op("vector", "reduce_max", g16a[:], seg(gwide), axis=AX.X,
                 reads=[gwide], writes=[g16a])
            p.op("vector", "tensor_scalar_add", gwide2[:], gwide[:], 2.0,
                 reads=[gwide], writes=[gwide2])
            p.op("vector", "tensor_tensor", gmaskw[:], gwide[:], bc8(g16a),
                 ALU.is_lt, reads=[gwide, g16a], writes=[gmaskw])
            p.op("vector", "tensor_mul", gmaskw[:], gwide2[:], gmaskw[:],
                 reads=[gwide2, gmaskw], writes=[gmaskw])
            p.op("vector", "reduce_max", g16b[:], seg(gmaskw), axis=AX.X,
                 reads=[gmaskw], writes=[g16b])
            p.op("vector", "tensor_tensor", gmaskw[:], gwide2[:], bc8(g16b),
                 ALU.is_ge, reads=[gwide2, g16b], writes=[gmaskw])
            p.op("vector", "tensor_mul", gwide[:], gwide[:], gmaskw[:],
                 reads=[gwide, gmaskw], writes=[gwide])
            p.op("vector", "reduce_sum", g16b[:], seg(gwide), axis=AX.X,
                 reads=[gwide], writes=[g16b])
            p.op("vector", "reciprocal", g16c[:], g16b[:],
                 reads=[g16b], writes=[g16c])
            p.op("vector", "tensor_mul", we_sb[:], gwide[:], bc8(g16c),
                 reads=[gwide, g16c],
                 writes=[f"we_sb:{tt}" for tt in range(NT)])

            for tt in range(NT):
                p.op("gpsimd", "memset", acc[:, tt * D:(tt + 1) * D], 0.0,
                     writes=[f"acc:{tt}"])

            fin_done = [0]

            def finalize(tt):
                fin_done[0] += 1
                ht = h_tl[tt % 2]; x1t = x1_tl[tt % 2]
                p.dma("sync", ht[:], h_s[s, tt])
                p.dma("gpsimd", x1t[:], x1_s[s, tt])
                t = [lnt, lnt3b][tt % 2]
                p.op("vector", "tensor_add", of_t[tt % 2][:],
                     acc[:, tt * D:(tt + 1) * D], ht[:],
                     reads=[f"acc:{tt}", ht], writes=[of_t[tt % 2]])
                ln_plain(p, cfg, of_t[tt % 2][:], lnof[tt % 2][:], t)
                p.op("vector", "tensor_add", lnof[tt % 2][:],
                     lnof[tt % 2][:], x1t[:],
                     reads=[lnof[tt % 2], x1t], writes=[lnof[tt % 2]])
                p.dma("scalar", y[s, tt * 128:(tt + 1) * 128, :],
                      lnof[tt % 2][:], writes=[f"y:{s}_{tt}"])

            def w1_load(e, gi, hg):
                idx = (e * NG + gi) * GH + hg
                wb = w1blk[idx % 4]
                wst = w1st[idx % 2]
                p.dma(["sync", "gpsimd"][idx % 2],
                      ap3(wst, 0, [[128, KC], [1, 128]]),
                      w1[e, gi * GH + hg].rearrange("(k p) c -> p k c",
                                                    p=128),
                      writes=[wst])
                p.op("vector", "tensor_copy", wb[:], wst[:],
                     reads=[wst], writes=[wb])
                return wb

            def w2_load(e, gi, hg):
                wg2 = w2g[gi % 2]
                wst = w2st[hg % 2]
                hc = gi * GH + hg
                p.dma(["sync", "gpsimd"][hg % 2], wst[:],
                      w2[e, hc * 128:(hc + 1) * 128, :], writes=[wst])
                p.op("vector", "tensor_copy",
                     wg2[:, hg * D: hg * D + D], wst[:],
                     reads=[wst], writes=[f"{wg2.name}:{hg}"])

            def b2_load(e):
                p.dma("sync", b2stg[:], b2pad[e])
                p.op("vector", "tensor_copy", b2bf[e % 2][:], b2stg[:],
                     reads=[b2stg], writes=[b2bf[e % 2]])

            groups = [(e, gi) for e in range(E) for gi in range(NG)]
            for g, (e, gi) in enumerate(groups):
                wg2 = w2g[gi % 2]
                b2b = b2bf[e % 2]
                if g == 0:
                    b2_load(0)
                    w1_load(e, gi, 0)
                    w1_load(e, gi, 1)
                h1_r = []
                w2_r = [f"{wg2.name}:{hg}" for hg in range(GH)]
                for hg in range(GH):
                    idx = (e * NG + gi) * GH + hg
                    wb = w1blk[idx % 4]
                    if hg >= 2:
                        w1_load(e, gi, hg)
                    w2_load(e, gi, hg)
                    if gi == NG - 1 and hg == 0 and e + 1 < E:
                        b2_load(e + 1)
                    hc = gi * GH + hg
                    for kc in range(KC):
                        for sub in range(NQC):
                            j = (hg % 2) * 4 + (sub % 4)
                            p.op("tensor", "matmul", hps[j][:],
                                 wb[:, kc * 128: kc * 128 + 128],
                                 t2_fm[:, kc * T + sub * 512:
                                       kc * T + sub * 512 + 512],
                                 start=(kc == 0), stop=(kc == KC - 1),
                                 reads=t2_r + [wb],
                                 writes=[hpn[j]])
                    for sub in range(NQC):
                        j = (hg % 2) * 4 + (sub % 4)
                        p.op("scalar", "activation",
                             h1_sb[:, hg * T + sub * 512:
                                   hg * T + sub * 512 + 512],
                             hps[j][:], AF.Gelu_apprx_tanh,
                             bias=b1c[:, e * NHC + hc: e * NHC + hc + 1],
                             scale=1.0,
                             reads=[hpn[j], b1c],
                             writes=[f"h1_sb:{hg}_{sub}"])
                        h1_r.append(f"h1_sb:{hg}_{sub}")
                if g + 1 < len(groups):
                    en, gin = groups[g + 1]
                    w1_load(en, gin, 0)
                    w1_load(en, gin, 1)
                for tt in range(NT):
                    for half in range(2):
                        j = 4 + (tt % 2) * 2 + half
                        ps = hps[j]
                        if gi == 0:
                            p.op("tensor", "matmul", ps[:], row0_sb[:],
                                 b2b[:, half * 512: half * 512 + 512],
                                 start=True, stop=False,
                                 reads=[row0_sb, b2b],
                                 writes=[hpn[j]])
                        for hg in range(GH):
                            p.op("tensor", "matmul", ps[:],
                                 h1_sb[:, hg * T + tt * 128:
                                       hg * T + tt * 128 + 128],
                                 wg2[:, hg * D + half * 512:
                                     hg * D + half * 512 + 512],
                                 start=(hg == 0 and gi != 0),
                                 stop=(hg == GH - 1),
                                 reads=h1_r + w2_r, writes=[hpn[j]])
                        p.op("vector", "scalar_tensor_tensor",
                             acc[:, tt * D + half * 512:
                                 tt * D + half * 512 + 512],
                             ps[:],
                             we_sb[:, tt * 8 + e: tt * 8 + e + 1],
                             acc[:, tt * D + half * 512:
                                 tt * D + half * 512 + 512],
                             ALU.mult, ALU.add,
                             reads=[hpn[j], f"we_sb:{tt}", f"acc:{tt}"],
                             writes=[f"acc:{tt}"])
                    if e == E - 1 and gi == NG - 1:
                        finalize(tt)

            assert fin_done[0] == NT
            fin_done[0] = 0


def _b2pad(b2, E, D):
    z = np.zeros((E, 128, D), np.float32)
    z[:, 0, :] = np.asarray(b2, np.float32)
    return z


def _row0():
    import ml_dtypes
    z = np.zeros((128, 128), np.float32)
    z[0] = 1.0
    return z.astype(ml_dtypes.bfloat16)


def make_in_map(cfg, x0_core, W):
    import ml_dtypes
    D, E, NHC = cfg.D, cfg.E, cfg.NHC

    def f32(v):
        return np.ascontiguousarray(np.asarray(v, np.float32))

    def bcast(v):
        v = np.asarray(v, np.float32)
        return np.ascontiguousarray(np.broadcast_to(v[None, :], (128, v.shape[0])))

    def cols(v, n):
        return np.ascontiguousarray(np.asarray(v, np.float32).reshape(n, 128).T)

    m = {
        "x0": f32(x0_core),
        "Wq": f32(W["L0_attn_Wq"]), "Wk": f32(W["L0_attn_Wk"]),
        "Wv": f32(W["L0_attn_Wv"]), "Wo": f32(W["L0_attn_Wo"]),
        "Wf": f32(W["L0_attn_Wf"]),
        "bf_col": f32(np.asarray(W["L0_attn_bf"]).reshape(-1, 1)),
        "bf_bc": bcast(np.asarray(W["L0_attn_bf"])),
        "bo_bc": bcast(np.asarray(W["L0_attn_bo"])),
        "w_qkv": f32(W["L0_ffn_w_qkv"]),
        "bqkv_cols": np.concatenate(
            [cols(np.asarray(W["L0_ffn_b_qkv"])[0:D], 8),
             cols(np.asarray(W["L0_ffn_b_qkv"])[D:2 * D], 8)], axis=1),
        "bqkv_v_bc": bcast(np.asarray(W["L0_ffn_b_qkv"])[2 * D:3 * D]),
        "w_o2": f32(W["L0_ffn_w_o"]),
        "bo2_bc": bcast(np.asarray(W["L0_ffn_b_o"])),
        "ln1w_bc": bcast(np.asarray(W["L0_ffn_ln1_w"])),
        "ln1b_bc": bcast(np.asarray(W["L0_ffn_ln1_b"])),
        "ln2w_bc": bcast(np.asarray(W["L0_ffn_ln2_w"])),
        "ln2b_bc": bcast(np.asarray(W["L0_ffn_ln2_b"])),
        "w_gate": f32(W["L0_ffn_w_gate"]),
        "w1": np.ascontiguousarray(
            np.asarray(W["L0_ffn_w1"], np.float32)
            .reshape(E, D, NHC, 128).transpose(0, 2, 1, 3)),
        "b1_cols": np.ascontiguousarray(
            np.asarray(W["L0_ffn_b1"], np.float32).reshape(E, NHC, 128)
            .transpose(2, 0, 1).reshape(128, E * NHC)),
        "w2": f32(W["L0_ffn_w2"]),
        "b2pad": _b2pad(W["L0_ffn_b2"], E, D),
        "row0": _row0(),
        "eye": np.eye(128).astype(ml_dtypes.bfloat16),
    }
    return {k: np.ascontiguousarray(v) for k, v in m.items()}


def kernel(**inputs):
    cfg = Cfg()
    x = np.asarray(inputs["L0_attn_x"], np.float32)
    nc = build_program(cfg)
    in_maps = [make_in_map(cfg, x[2 * c: 2 * c + 2], inputs) for c in range(8)]
    res = run_bass_kernel_spmd(nc, in_maps, core_ids=list(range(8)))
    out = np.concatenate([r["y"] for r in res.results], axis=0)
    return np.ascontiguousarray(out.astype(np.float32))
